# version 06
# baseline (speedup 1.0000x reference)
import os
import numpy as np
import ml_dtypes
from contextlib import ExitStack

import concourse.bass as bass
import concourse.mybir as mybir
from concourse.bass_utils import run_bass_kernel_spmd

F32 = mybir.dt.float32
BF16 = mybir.dt.bfloat16
AF = mybir.ActivationFunctionType
ALU = mybir.AluOpType
AX = mybir.AxisListType

NCORES = 8
S = 2048
D = 1024
NT = S // 128
NC = D // 128
BF = ml_dtypes.bfloat16

ENGS = ("pe", "act", "dve", "pool", "sp")


class R:
    __slots__ = ("tt", "ap")
    def __init__(self, tt, ap):
        self.tt = tt
        self.ap = ap


class TT:
    __slots__ = ("kb", "h", "lw", "readers", "is_psum")
    def __init__(self, kb, h, is_psum=False):
        self.kb = kb
        self.h = h
        self.lw = None
        self.readers = []
        self.is_psum = is_psum
    def __getitem__(self, idx):
        return R(self, self.h[idx])
    @property
    def full(self):
        return R(self, self.h[:])


class KB:

    def __init__(self, nc):
        self.nc = nc
        self.eng = {"pe": nc.tensor, "act": nc.scalar, "dve": nc.vector,
                    "pool": nc.gpsimd, "sp": nc.sync}
        self.stack = ExitStack()
        self.sems = {}
        self.cnt = {}
        self.wm = {}
        self.fences = {}
        self._uid = 0
        self._es = None
        self._cast_rr = 0
        self._consts = {}
        self._cache = [{}]
        self.cur = {}
        self.last_dma = {}
        for e in ENGS:
            self.cur["s_" + e] = "s_" + e
            self._mksem("s_" + e)
        for q in ("sp", "pool", "act", "dve"):
            self.cur["d_" + q] = "d_" + q
            self._mksem("d_" + q)

    def _mksem(self, name):
        self.sems[name] = self.stack.enter_context(self.nc.semaphore(name))
        self.cnt[name] = 0

    def _sem(self, role):
        name = self.cur[role]
        if self.cnt[name] >= 60000:
            name = f"{role}#{len(self.cnt)}"
            self.cur[role] = name
            self._mksem(name)
        return self.cur[role]

    def uid(self, base):
        self._uid += 1
        return f"{base}_{self._uid}"

    def sb(self, name, shape, dtype, persistent=False):
        nm = self.uid(name)
        if persistent or self._es is None:
            h = self.nc.alloc_sbuf_tensor(nm, shape, dtype, side="right")
        else:
            h = self._es.enter_context(self.nc.sbuf_tensor(nm, shape, dtype))
        return TT(self, h)

    def sbc(self, name, shape, dtype):
        key = (name, tuple(shape), dtype)
        c = self._cache[-1]
        if key not in c:
            c[key] = self.sb(name, shape, dtype)
        return c[key]

    def ps(self, name, shape, dtype=F32):
        nm = self.uid(name)
        assert self._es is not None
        h = self._es.enter_context(self.nc.psum_tensor(nm, shape, dtype))
        return TT(self, h, is_psum=True)

    def phase(self):
        kb = self
        class _P:
            def __enter__(self):
                self.es = ExitStack()
                self.prev = kb._es
                kb._es = self.es.__enter__()
                kb._cache.append({})
                return kb._es
            def __exit__(self, *a):
                self.es.__exit__(*a)
                kb._es = self.prev
                kb._cache.pop()
                kb.fence()
                return False
        return _P()

    def fence(self):
        for s, v in self.cnt.items():
            if v > self.fences.get(s, 0):
                self.fences[s] = v

    def _wait(self, ename, sem, val):
        if val <= 0:
            return
        if self.wm.get((ename, sem), 0) >= val:
            return
        self.eng[ename].wait_ge(self.sems[sem], val)
        self.wm[(ename, sem)] = val

    def _gather(self, ename, reads, writes):
        need = dict(self.fences)
        def add(tok):
            if tok is not None:
                s, v = tok
                if need.get(s, 0) < v:
                    need[s] = v
        def tt_of(x):
            return x if isinstance(x, TT) else x.tt
        for r in reads:
            t = tt_of(r)
            add(t.lw)
            if t.is_psum:
                for tok in t.readers:
                    add(tok)
        for w in writes:
            add(tt_of(w).lw)
            for tok in tt_of(w).readers:
                add(tok)
        for s, v in sorted(need.items()):
            if ename == "pe" and s.startswith("s_pe"):
                continue
            self._wait(ename, s, v)

    def emit(self, ename, builder, reads=(), writes=()):
        self._gather(ename, reads, writes)
        inst = builder(self.eng[ename])
        sem = self._sem("s_" + ename)
        self.cnt[sem] += 1
        inst.then_inc(self.sems[sem], 1)
        tok = (sem, self.cnt[sem])
        for w in writes:
            (w if isinstance(w, TT) else w.tt).lw = tok
            (w if isinstance(w, TT) else w.tt).readers = []
        for r in reads:
            (r if isinstance(r, TT) else r.tt).readers.append(tok)
        return inst

    @staticmethod
    def _ap(x):
        if isinstance(x, TT):
            return x.h[:]
        return x.ap if isinstance(x, R) else x

    def dma(self, q, dst, src):
        reads = [src] if isinstance(src, R) else []
        writes = [dst] if isinstance(dst, R) else []
        self._gather(q, reads, writes)
        prev = self.last_dma.get(q)
        if prev is not None:
            self._wait(q, prev[0], prev[1])
        sem = self._sem("d_" + q)
        inst = self.eng[q].dma_start(out=self._ap(dst), in_=self._ap(src))
        self.cnt[sem] += 16
        self.last_dma[q] = (sem, self.cnt[sem])
        inst.then_inc(self.sems[sem], 16)
        tok = (sem, self.cnt[sem])
        for w in writes:
            (w if isinstance(w, TT) else w.tt).lw = tok
            (w if isinstance(w, TT) else w.tt).readers = []
        for r in reads:
            (r if isinstance(r, TT) else r.tt).readers.append(tok)
        return inst

    def mm(self, psum, pairs, start=True, stop=True):
        reads = []
        for l, r in pairs:
            reads += [l, r]
        self._gather("pe", reads, [psum])
        n = len(pairs)
        inst = None
        for i, (l, r) in enumerate(pairs):
            inst = self.eng["pe"].matmul(out=self._ap(psum), lhsT=self._ap(l), rhs=self._ap(r),
                                         start=(start and i == 0),
                                         stop=(stop and i == n - 1))
        sem = self._sem("s_pe")
        self.cnt[sem] += 1
        inst.then_inc(self.sems[sem], 1)
        tok = (sem, self.cnt[sem])
        (psum if isinstance(psum, TT) else psum.tt).lw = tok
        (psum if isinstance(psum, TT) else psum.tt).readers = []
        for r in reads:
            (r if isinstance(r, TT) else r.tt).readers.append(tok)

    def trans(self, psum, src, ident):
        self.emit("pe", lambda e: e.transpose(out=self._ap(psum), in_=self._ap(src), identity=self._ap(ident)),
                  reads=[src, ident], writes=[psum])

    def act(self, ename, dst, src, func, bias=0.0, scale=1.0, accum=None):
        reads = [src] + [x for x in (bias, scale) if isinstance(x, R)]
        writes = [dst] + ([accum] if accum is not None else [])
        kw = {}
        if accum is not None:
            kw["accum_out"] = self._ap(accum)
        self.emit(ename, lambda e: e.activation(out=self._ap(dst), in_=self._ap(src), func=func,
                                                bias=self._ap(bias), scale=self._ap(scale), **kw),
                  reads=reads, writes=writes)

    def copy(self, ename, dst, src):
        if ename == "act":
            self.emit(ename, lambda e: e.copy(out=self._ap(dst), in_=self._ap(src)), reads=[src], writes=[dst])
        else:
            self.emit(ename, lambda e: e.tensor_copy(out=self._ap(dst), in_=self._ap(src)), reads=[src], writes=[dst])

    def cast(self, dst, src, engines=("dve", "act", "pool")):
        e = engines[self._cast_rr % len(engines)]
        self._cast_rr += 1
        self.copy(e, dst, src)

    def tt_(self, ename, dst, a, b, op):
        self.emit(ename, lambda e: e.tensor_tensor(out=self._ap(dst), in0=self._ap(a), in1=self._ap(b), op=op),
                  reads=[a, b], writes=[dst])

    def ts_(self, ename, dst, a, s1, op0, s2=None, op1=None, accum=None):
        reads = [a] + [x for x in (s1, s2) if isinstance(x, R)]
        writes = [dst] + ([accum] if accum is not None else [])
        kw = {}
        if accum is not None:
            kw["accum_out"] = self._ap(accum)
        if op1 is None:
            self.emit(ename, lambda e: e.tensor_scalar(out=self._ap(dst), in0=self._ap(a), scalar1=self._ap(s1),
                                                       scalar2=None, op0=op0, **kw),
                      reads=reads, writes=writes)
        else:
            self.emit(ename, lambda e: e.tensor_scalar(out=self._ap(dst), in0=self._ap(a), scalar1=self._ap(s1),
                                                       scalar2=self._ap(s2), op0=op0, op1=op1, **kw),
                      reads=reads, writes=writes)

    def stt(self, ename, dst, a, s, b, op0, op1):
        reads = [a, b] + ([s] if isinstance(s, R) else [])
        self.emit(ename, lambda e: e.scalar_tensor_tensor(out=self._ap(dst), in0=self._ap(a), scalar=self._ap(s),
                                                          in1=self._ap(b), op0=op0, op1=op1),
                  reads=reads, writes=[dst])

    def red(self, dst, src, op, negate=False, absval=False):
        self.emit("dve", lambda e: e.tensor_reduce(out=self._ap(dst), in_=self._ap(src), axis=AX.X, op=op,
                                                   negate=negate, apply_absolute_value=absval),
                  reads=[src], writes=[dst])

    def memset(self, ename, dst, val):
        self.emit(ename, lambda e: e.memset(self._ap(dst), val), reads=[], writes=[dst])

    def recip(self, dst, src):
        def _b(e):
            with self.nc.allow_low_precision("bf16 reciprocal within validated tolerance"):
                return e.reciprocal(out=self._ap(dst), in_=self._ap(src))
        self.emit("dve", _b, reads=[src], writes=[dst])

    def track_dram(self, handle):
        return TT(self, handle)

    def constf(self, val, dtype=F32):
        key = (float(val), dtype)
        if key not in self._consts:
            t = self.sb("cst", [128, 1], dtype, persistent=True)
            self.memset("dve", t.full, float(val))
            self._consts[key] = t
        return self._consts[key].full

    def drain_all(self):
        for s, v in sorted(self.cnt.items()):
            if v > 0:
                self._wait("pool", s, v)



def ln_stats(kb, x, tiles, eps, junk, name="ln"):
    n = len(tiles)
    sums = kb.sbc(name + "_s", [128, n], F32)
    sq = kb.sbc(name + "_q", [128, n], F32)
    for j, t in enumerate(tiles):
        kb.red(sums[:, j:j + 1], x[:, t, :], ALU.add)
        kb.act("act", junk[:, 0:1024], x[:, t, :], AF.Square, accum=sq[:, j:j + 1])
    m = kb.sbc(name + "_m", [128, n], F32)
    var = kb.sbc(name + "_v", [128, n], F32)
    r = kb.sbc(name + "_r", [128, n], F32)
    b = kb.sbc(name + "_b", [128, n], F32)
    kb.ts_("dve", m.full, sums.full, 1.0 / D, ALU.mult)
    kb.tt_("dve", var.full, m.full, m.full, ALU.mult)
    kb.stt("dve", var.full, sq.full, 1.0 / D, var.full, ALU.mult, ALU.subtract)
    kb.act("act", var.full, var.full, AF.Sqrt, bias=kb.constf(eps))
    kb.recip(r.full, var.full)
    kb.stt("dve", b.full, m.full, -1.0, r.full, ALU.mult, ALU.mult)
    return r, b


def ln_transpose_fm(kb, dst, x, tiles, r, b, ident, tmp, ptile, scale_fm=None, dst_col0=0):
    for j, t in enumerate(tiles):
        kb.act("act", tmp.full, x[:, t, :], AF.Identity,
               bias=b[:, j:j + 1], scale=r[:, j:j + 1])
        for c in range(NC):
            kb.trans(ptile[:, 128 * c:128 * (c + 1)], tmp[:, 128 * c:128 * (c + 1)], ident)
        col = 128 * (dst_col0 + j)
        dsl = dst.h[:, :, col:col + 128]
        dR = R(dst, dsl)
        pview = ptile.h[:].rearrange("p (c n) -> p c n", c=NC)
        if scale_fm is not None:
            kb.emit("dve", lambda e: e.tensor_scalar(out=dsl, in0=pview, scalar1=kb._ap(scale_fm),
                                                     scalar2=None, op0=ALU.mult),
                    reads=[ptile, scale_fm], writes=[dR])
        else:
            kb.emit("dve", lambda e: e.tensor_copy(out=dsl, in_=pview),
                    reads=[ptile], writes=[dR])


def transpose_tm_fm(kb, dst, src, tiles, ident, psname="tp", dtype=BF16,
                    scale_fm=None, dst_f32_stage=None, chunks=range(NC)):
    grp = 4
    p = kb.ps(psname, [128, 128 * grp], dtype)
    for c in chunks:
        for g0 in range(0, len(tiles), grp):
            gt = tiles[g0:g0 + grp]
            for k, t in enumerate(gt):
                kb.trans(p[:, 128 * k:128 * (k + 1)], src[:, t, 128 * c:128 * (c + 1)], ident)
            dsl = dst[:, c, 128 * g0:128 * g0 + 128 * len(gt)]
            if scale_fm is not None:
                kb.ts_("dve", dsl, p[:, 0:128 * len(gt)], scale_fm[:, c:c + 1], ALU.mult)
            else:
                kb.copy("dve", dsl, p[:, 0:128 * len(gt)])


def load_weight_strip(kb, wdram, oc, kchunks, stage_f32, dst_bf16, wname):
    ap = wdram.rearrange("(c p) o -> p c o", p=128)[:, :, 128 * oc:128 * (oc + 1)]
    kb.dma("sp", stage_f32[:, 0:kchunks, :], ap)
    kb.cast(dst_bf16[:, 0:kchunks, :], stage_f32[:, 0:kchunks, :])




def epilogue_fm_group(kb, st, x, yfm, pt, tmp, junk, t0, ntile, ytm):
    for ts4 in range(ntile):
        for cg in range(2):
            for k in range(4):
                c = cg * 4 + k
                kb.trans(pt[:, 128 * k:128 * (k + 1)],
                         yfm[:, c, 128 * ts4:128 * (ts4 + 1)], st["identf"])
            kb.copy("dve", ytm[:, 0, 512 * cg:512 * (cg + 1)], pt[:, 0:512])
        r2, b2 = ln_stats(kb, ytm, [0], 1e-5, junk, name="lno")
        t = t0 + ts4
        kb.act("act", tmp.full, ytm[:, 0, :], AF.Identity,
               bias=b2[:, 0:1], scale=r2[:, 0:1])
        kb.tt_("dve", x[:, t, :], x[:, t, :], tmp.full, ALU.add)


def emit_ffn_ternary(kb, st, P):
    x = st["x"]
    identb = st["identb"]
    junk = st["junk"]
    TtT = 4
    Tcols = 128 * TtT
    HCH = 32
    nc = kb.nc

    caches = {}
    for wname, kch, noc in (("w1T", NC, HCH), ("w2T", NC, HCH), ("h1T", HCH, HCH),
                            ("h2T", HCH, HCH), ("w3T", HCH, NC)):
        h = nc.dram_tensor(kb.uid("wc_" + wname), [noc, 128, kch, 128], BF16,
                           kind="Internal")
        caches[wname] = (kb.track_dram(h), kch, noc)

    with kb.phase():
        s1 = kb.sb("s1", [128, HCH], F32); kb.dma("sp", s1.full, P["s1r"])
        s2 = kb.sb("s2", [128, HCH], F32); kb.dma("sp", s2.full, P["s2r"])
        sh1 = kb.sb("sh1", [128, HCH], F32); kb.dma("sp", sh1.full, P["sh1r"])
        sh2 = kb.sb("sh2", [128, HCH], F32); kb.dma("sp", sh2.full, P["sh2r"])
        s3 = kb.sb("s3", [128, NC], F32); kb.dma("sp", s3.full, P["s3r"])

        xn_fm = kb.sb("xn_fm", [128, NC, S], BF16)
        tmp = kb.sb("lntmp", [128, D], BF16)
        ptile = kb.ps("ptile", [128, D], BF16)
        r, b = ln_stats(kb, x, list(range(NT)), 1e-5, junk)
        ln_transpose_fm(kb, xn_fm, x, list(range(NT)), r, b, identb, tmp, ptile)

        pp = [kb.ps("pmm0", [128, Tcols]), kb.ps("pmm1", [128, Tcols])]
        pt = kb.ps("ptr", [128, 512], F32)
        wbufs = [kb.sb("wbuf", [128, HCH, 128], BF16) for _ in range(2)]
        stage0 = kb.sb("wstg", [128, NC, 128], F32)
        g_a = kb.sb("g_a", [128, HCH, Tcols], BF16)
        g_b = kb.sb("g_b", [128, HCH, Tcols], BF16)
        a_t = kb.sb("a_t", [128, Tcols], BF16)
        yfm = kb.sb("yfm", [128, NC, 512], F32)
        ytm = kb.sb("ytm", [128, 1, D], F32)
        sctr = [0]
        queues = ("sp", "act")

        def wstrip(wname, oc, first):
            wc, kch, noc = caches[wname]
            i = sctr[0] % 2
            q = queues[i]
            sctr[0] += 1
            wb = wbufs[i]
            stg = stage0
            if first:
                ap = P[wname].rearrange("(c p) o -> p c o", p=128)
                for q0 in range(0, kch, NC):
                    kb.dma(q, R(stg, stg.h[:, 0:NC, :]),
                           ap[:, q0:q0 + NC, 128 * oc:128 * (oc + 1)])
                    kb.cast(R(wb, wb.h[:, q0:q0 + NC, :]), R(stg, stg.h[:, 0:NC, :]),
                            engines=("dve", "act"))
                kb.dma("pool", R(wc, wc.h[oc, :, :, :]), R(wb, wb.h[:, 0:kch, :]))
            else:
                kb.dma(q, R(wb, wb.h[:, 0:kch, :]), R(wc, wc.h[oc, :, :, :]))
            return wb

        for tt in range(S // Tcols):
            first = tt == 0
            tok = slice(Tcols * tt, Tcols * (tt + 1))
            for oc in range(HCH):
                wb = wstrip("w1T", oc, first)
                p = pp[0]
                kb.mm(p.full, [(R(wb, wb.h[:, c, :]), xn_fm[:, c, tok]) for c in range(NC)])
                kb.act("act", a_t.full, p.full, AF.Silu, scale=s1[:, oc:oc + 1])
                wb = wstrip("w2T", oc, first)
                p2 = pp[1]
                kb.mm(p2.full, [(R(wb, wb.h[:, c, :]), xn_fm[:, c, tok]) for c in range(NC)])
                kb.stt("dve", g_a[:, oc, :], p2.full, s2[:, oc:oc + 1], a_t.full,
                       ALU.mult, ALU.mult)
            for oc in range(HCH):
                wb = wstrip("h1T", oc, first)
                p = pp[oc % 2]
                kb.mm(p.full, [(R(wb, wb.h[:, c, :]), g_a[:, c, :]) for c in range(HCH)])
                kb.act("act", g_b[:, oc, :], p.full, AF.Gelu, scale=sh1[:, oc:oc + 1])
            for oc in range(HCH):
                wb = wstrip("h2T", oc, first)
                p = pp[oc % 2]
                kb.mm(p.full, [(R(wb, wb.h[:, c, :]), g_b[:, c, :]) for c in range(HCH)])
                kb.act("act", g_a[:, oc, :], p.full, AF.Gelu, scale=sh2[:, oc:oc + 1])
            for oc in range(NC):
                wb = wstrip("w3T", oc, first)
                p = pp[oc % 2]
                kb.mm(p.full, [(R(wb, wb.h[:, c, :]), g_a[:, c, :]) for c in range(HCH)])
                kb.ts_("dve", yfm[:, oc, :], p.full, s3[:, oc:oc + 1], ALU.mult)
            epilogue_fm_group(kb, st, x, yfm, pt, tmp, junk, tt * TtT, TtT, ytm)



def emit_attn_causal(kb, st, P, rope=False, bias=False):
    x = st["x"]
    identb = st["identb"]
    junk = st["junk"]
    H, DH = 16, 64
    SCL = 0.125

    with kb.phase():
        if bias:
            bo = kb.sb("bo", [128, NC], F32)
            kb.dma("sp", bo.full, P["bo_r"])
        Q = kb.sb("Qfm", [128, NC, S], BF16)
        Kt = kb.sb("Kfm", [128, NC, S], BF16)
        V = kb.sb("Vaug", [128, NT, H, DH], BF16)

        with kb.phase():
            xn_fm = kb.sb("xn_fm", [128, NC, S], BF16)
            tmp = kb.sb("lntmp", [128, D], BF16)
            ptile = kb.ps("ptile", [128, D], BF16)
            r, b = ln_stats(kb, x, list(range(NT)), 1e-5, junk)
            ln_transpose_fm(kb, xn_fm, x, list(range(NT)), r, b, identb, tmp, ptile)

            stage = kb.sb("wstg", [128, 4, 128], F32)
            wbuf = kb.sb("wbuf", [128, NC, 128], BF16)
            pq = kb.ps("pq", [128, 512])
            wap = P["wqkvT"].rearrange("(c p) o -> p c o", p=128)

            wbufs2 = [wbuf, kb.sb("wbufB", [128, NC, 128], BF16)]
            pqs = [pq, kb.ps("pqB", [128, 512])]
            pctr = [0]

            def pstrip(ocol):
                i = pctr[0] % 2
                pctr[0] += 1
                q = ("sp", "act")[i]
                wb = wbufs2[i]
                for hh in range(2):
                    kb.dma(q, R(stage, stage.h[:, :, :]),
                           wap[:, 4 * hh:4 * (hh + 1), 128 * ocol:128 * (ocol + 1)])
                    kb.cast(R(wb, wb.h[:, 4 * hh:4 * (hh + 1), :]),
                            R(stage, stage.h[:, :, :]), engines=("dve", "pool"))
                return wb, pqs[i]

            def proj_fm(dst, ocol0, scl):
                for oc in range(NC):
                    wb, p = pstrip(ocol0 + oc)
                    for qc in range(4):
                        kb.mm(p.full, [(R(wb, wb.h[:, c, :]),
                                        xn_fm[:, c, 512 * qc:512 * (qc + 1)])
                                       for c in range(NC)])
                        kb.act("act", dst[:, oc, 512 * qc:512 * (qc + 1)], p.full,
                               AF.Identity, scale=scl)

            proj_fm(Q, 0, 0.125)
            proj_fm(Kt, 8, 1.0)
            for oc in range(NC):
                wb, p = pstrip(16 + oc)
                for kt in range(NT):
                    pv = R(p, p.h[:, 0:128])
                    kb.mm(pv, [(xn_fm[:, c, 128 * kt:128 * (kt + 1)],
                                R(wb, wb.h[:, c, :])) for c in range(NC)])
                    kb.copy("dve", V[:, kt, 2 * oc:2 * oc + 2, 0:DH],
                            R(p, p.h[:, 0:128].rearrange("p (h d) -> p h d", h=2)))

        if rope:
            with kb.phase():
                cosfm = kb.sb("cosfm", [128, S], BF16)
                sinfm = kb.sb("sinfm", [128, S], BF16)
                rt = kb.sb("rt", [128, 64], BF16)
                kb.dma("sp", cosfm.full, P["cos_fm"])
                kb.dma("sp", sinfm.full, P["sin_fm"])
                kb.dma("sp", rt.full, P["rt"])
                prs = [kb.ps("pr", [128, 512]) for _ in range(2)]
                rss = [kb.sb("ropes", [128, 512], BF16) for _ in range(2)]
                rcs = [kb.sb("ropec", [128, 512], BF16) for _ in range(2)]
                rctr = 0
                for dst in (Q, Kt):
                    for h in range(H):
                        ch, off = h // 2, 64 * (h % 2)
                        sl = slice(off, off + 64)
                        for qc in range(4):
                            pr, rs, rc = prs[rctr % 2], rss[rctr % 2], rcs[rctr % 2]
                            rctr += 1
                            cols = slice(512 * qc, 512 * (qc + 1))
                            src = R(dst, dst.h[sl, ch, cols])
                            kb.mm(R(pr, pr.h[sl, :]), [(R(rt, rt.h[sl, :]), src)])
                            kb.tt_("dve", R(rs, rs.h[sl, :]), R(pr, pr.h[sl, :]),
                                   R(sinfm, sinfm.h[sl, cols]), ALU.mult)
                            kb.tt_("pool", R(rc, rc.h[sl, :]), src,
                                   R(cosfm, cosfm.h[sl, cols]), ALU.mult)
                            kb.tt_("dve", src, R(rc, rc.h[sl, :]), R(rs, rs.h[sl, :]), ALU.add)

        mask01 = kb.sb("mask01", [128, 896], BF16)
        kb.dma("sp", mask01.full, P["mask01"])
        wo = kb.sb("wo", [128, NC, D], BF16)
        with kb.phase():
            wstg2 = kb.sb("wstg2", [128, 4, 128], F32)
            woap = P["woT"].rearrange("(c p) o -> p c o", p=128)
            for oc in range(NC):
                for hh in range(2):
                    kb.dma("sp", wstg2.full,
                           woap[:, 4 * hh:4 * (hh + 1), 128 * oc:128 * (oc + 1)])
                    kb.cast(wo[:, 4 * hh:4 * (hh + 1), 128 * oc:128 * (oc + 1)], wstg2.full)
        NB = 3
        ctxq = kb.sb("ctxq", [128, NC, 512], BF16)
        pbufs = [kb.sb("pbuf", [128, 512], BF16) for _ in range(2)]
        vscs = [kb.sb("vsc", [128, DH + 1], BF16) for _ in range(2)]
        mus = [kb.sb("mu", [128, 1], F32) for _ in range(2)]
        rvs = [kb.sb("rv", [128, 1], F32) for _ in range(2)]
        dens = kb.sb("dens", [1, 512], BF16)
        recb = kb.sb("recb", [64, 512], BF16)
        onesf = kb.sb("onesf", [1, 64], BF16)
        kb.memset("dve", onesf.full, 1.0)
        yfm = kb.sb("yfm_a", [128, NC, 256], F32)
        tmp2 = junk
        ytm = kb.sb("ytm_a", [128, 1, D], F32)

        psts = [kb.ps("pst", [128, 512]) for _ in range(NB)]
        pcxs = [kb.ps("pcx", [65, 512]) for _ in range(2)]
        pbc = kb.ps("pbc", [64, 512])
        pyo = kb.ps("pyo", [128, 512])
        pt = kb.ps("pte", [128, 512], F32)

        for qc in range(4):
            qcols = slice(512 * qc, 512 * (qc + 1))
            for h in range(H):
                ch, off = h // 2, 64 * (h % 2)
                pcx = pcxs[h % 2]
                nkt = 4 * qc + 4
                for kt in range(nkt):
                    pst = psts[kt % NB]
                    i = kt % 2
                    pbuf, vsc, mu, rv = pbufs[i], vscs[i], mus[i], rvs[i]
                    kb.mm(pst.full, [(R(Kt, Kt.h[off:off + 64, ch, 128 * kt:128 * (kt + 1)]),
                                      R(Q, Q.h[off:off + 64, ch, qcols]))])
                    kb.red(mu.full, pst.full, ALU.max, negate=True)
                    kb.act("act", pbuf.full, pst.full, AF.Exp, bias=mu.full)
                    kb.act("act", rv.full, mu.full, AF.Exp, scale=-1.0)
                    j = kt - 4 * qc
                    if j >= 0:
                        kb.tt_("dve", pbuf.full, pbuf.full,
                               R(mask01, mask01.h[:, 384 - 128 * j:896 - 128 * j]), ALU.mult)
                    kb.act("act", R(vsc, vsc.h[:, 0:DH]), V[:, kt, h, :],
                           AF.Identity, scale=rv.full)
                    kb.act("act", R(vsc, vsc.h[:, DH:DH + 1]), rv.full, AF.Identity)
                    kb.mm(pcx.full, [(vsc.full, pbuf.full)],
                          start=(kt == 0), stop=(kt == nkt - 1))
                kb.copy("dve", dens.full, R(pcx, pcx.h[64:65, :]))
                kb.act("act", dens.full, dens.full, AF.Ln)
                kb.act("act", dens.full, dens.full, AF.Exp, scale=-1.0)
                kb.mm(pbc.full, [(onesf.full, dens.full)])
                kb.copy("dve", recb.full, pbc.full)
                kb.tt_("dve", R(ctxq, ctxq.h[off:off + 64, ch, :]),
                       R(pcx, pcx.h[0:64, :]), recb.full, ALU.mult)
            for tg in range(2):
                for oc in range(NC):
                    kb.mm(R(pyo, pyo.h[:, 0:256]),
                          [(R(wo, wo.h[:, c, 128 * oc:128 * (oc + 1)]),
                            ctxq[:, c, 256 * tg:256 * (tg + 1)]) for c in range(NC)])
                    if bias:
                        kb.ts_("dve", yfm[:, oc, :], R(pyo, pyo.h[:, 0:256]),
                               bo[:, oc:oc + 1], ALU.add)
                    else:
                        kb.copy("dve", yfm[:, oc, :], R(pyo, pyo.h[:, 0:256]))
                epilogue_fm_group(kb, st, x, yfm, pt, tmp2, junk, 4 * qc + 2 * tg, 2, ytm)





def emit_moe(kb, st, P):
    nc = kb.nc
    x = st["x"]
    junk = st["junk"]
    identf = st["identf"]
    E = 8
    HCH = 16

    with kb.phase():
        scale_fm = kb.sb("mscale", [128, NC], F32)
        kb.dma("sp", scale_fm.full, P["scale_r"])
        ctr_tm = kb.sb("ctr_tm", [8, D], F32)
        kb.dma("sp", ctr_tm.full, P["ctr"])
        ctr_fm = kb.sb("ctr_fm", [128, NC, E], F32)
        kb.dma("sp", ctr_fm.full, P["ctr_fm"])
        ny = kb.sb("ny", [8, 1], F32)
        kb.act("act", junk[0:8, 0:D], ctr_tm.full, AF.Square, accum=ny.full)
        kb.ts_("dve", ny.full, ny.full, -0.5, ALU.mult)
        nyr = kb.sb("nyr", [1, 8], F32)
        onesc = kb.sb("onesc", [1, 128], F32)
        kb.memset("dve", onesc.full, 1.0)

        rL, bL = ln_stats(kb, x, list(range(NT)), 1e-5, junk, name="lnm")

        xn_fm = kb.sb("mxn_fm", [128, NC, S], BF16)
        masks = kb.sb("masks", [128, E, NT], F32)
        tmpa = kb.sb("tmpa", [128, D], F32)
        tmpb = kb.sb("tmpb", [128, D], F32)
        xnf = kb.sb("xnf", [128, NC, 128], F32)
        rmsq = kb.sb("rmsq", [128, 1], F32)
        rstd = kb.sb("rstd", [128, 1], F32)
        mx = kb.sb("mx", [128, 1], F32)
        ge = kb.sb("ge", [128, 1], F32)
        avail = kb.sb("avail", [128, 1], F32)

        with kb.phase():
            ptr = kb.ps("ptr_m", [128, D], F32)
            psc = kb.ps("psc", [128, 8], F32)
            pny = kb.ps("pny", [1, 8], F32)
            kb.trans(pny.full, ny.full, R(identf, identf.h[0:8, 0:8]))
            kb.copy("dve", nyr.full, pny.full)
            for t in range(NT):
                kb.act("act", tmpa.full, x[:, t, :], AF.Identity,
                       bias=bL[:, t:t + 1], scale=rL[:, t:t + 1])
                kb.act("act", junk[:, 0:D], tmpa.full, AF.Square, accum=rmsq.full)
                kb.ts_("dve", rmsq.full, rmsq.full, 1.0 / D, ALU.mult)
                kb.act("act", rmsq.full, rmsq.full, AF.Sqrt, bias=kb.constf(1e-6))
                kb.recip(rstd.full, rmsq.full)
                kb.act("act", tmpb.full, tmpa.full, AF.Identity, scale=rstd.full)
                for c in range(NC):
                    kb.trans(R(ptr, ptr.h[:, 128 * c:128 * (c + 1)]),
                             tmpb[:, 128 * c:128 * (c + 1)], identf)
                for c in range(NC):
                    kb.ts_("dve", xnf[:, c, :], R(ptr, ptr.h[:, 128 * c:128 * (c + 1)]),
                           scale_fm[:, c:c + 1], ALU.mult)
                kb.cast(R(xn_fm, xn_fm.h[:, :, 128 * t:128 * (t + 1)]), xnf.full,
                        engines=("dve",))
                kb.mm(psc.full, [(onesc.full, nyr.full)], stop=False)
                kb.mm(psc.full, [(xnf[:, c, :], ctr_fm[:, c, :]) for c in range(NC)],
                      start=False)
                kb.red(mx.full, psc.full, ALU.max)
                kb.memset("dve", avail.full, 1.0)
                for e in range(E):
                    kb.ts_("dve", ge.full, R(psc, psc.h[:, e:e + 1]), mx.full, ALU.is_ge)
                    kb.tt_("dve", R(masks, masks.h[:, e, t:t + 1]), ge.full, avail.full,
                           ALU.mult)
                    kb.tt_("dve", avail.full, avail.full,
                           R(masks, masks.h[:, e, t:t + 1]), ALU.subtract)

        aa = kb.sb("aa", [128, HCH, 1024], BF16)
        y_acc = kb.sb("y_acc", [128, NT // 2, D], F32)
        stage = kb.sb("mstg", [128, HCH, 128], F32)
        wbufs = [kb.sb("mwbuf", [128, HCH, 128], BF16) for _ in range(2)]
        sg = kb.sb("sg", [128, 512], BF16)
        upap = P["up_w"].rearrange("e (c p) o -> p e c o", p=128)
        dnap = P["down_w"].rearrange("e (c p) o -> p e c o", p=128)
        pmm = kb.ps("pmm_m", [128, 512])
        pmt = kb.ps("pmt_m", [128, 512], F32)
        upc = kb.track_dram(nc.dram_tensor(kb.uid("wc_up"), [E, 2 * HCH, 128, NC, 128],
                                           BF16, kind="Internal"))
        dnc = kb.track_dram(nc.dram_tensor(kb.uid("wc_dn"), [E, NC, 128, HCH, 128],
                                           BF16, kind="Internal"))
        mctr = [0]
        mqueues = ("sp", "act")

        def mstrip(which, e, oc, kch, first):
            i = mctr[0] % 2
            q = mqueues[i]
            mctr[0] += 1
            wb = wbufs[i]
            wc = upc if which == 0 else dnc
            src = upap if which == 0 else dnap
            if first:
                kb.dma(q, R(stage, stage.h[:, 0:kch, :]),
                       src[:, e, :, 128 * oc:128 * (oc + 1)])
                kb.cast(R(wb, wb.h[:, 0:kch, :]), R(stage, stage.h[:, 0:kch, :]),
                        engines=("dve", "act"))
                kb.dma("pool", R(wc, wc.h[e, oc, :, :, :]), R(wb, wb.h[:, 0:kch, :]))
            else:
                kb.dma(q, R(wb, wb.h[:, 0:kch, :]), R(wc, wc.h[e, oc, :, :, :]))
            return wb

        for half in range(2):
            first = half == 0
            tok0 = 1024 * half
            t0 = NT // 2 * half
            for tloc in range(NT // 2):
                kb.memset("dve", y_acc[:, tloc, :], 0.0)
            for e in range(E):
                for oc in range(2 * HCH):
                    wb = mstrip(0, e, oc, NC, first)
                    for tq in range(2):
                        cols = slice(tok0 + 512 * tq, tok0 + 512 * (tq + 1))
                        lcols = slice(512 * tq, 512 * (tq + 1))
                        kb.mm(pmm.full, [(R(wb, wb.h[:, c, :]), xn_fm[:, c, cols])
                                         for c in range(NC)])
                        if oc < HCH:
                            kb.act("act", R(aa, aa.h[:, oc, lcols]), pmm.full, AF.Identity)
                        else:
                            kb.act("act", sg.full, pmm.full, AF.Silu)
                            kb.tt_("dve", R(aa, aa.h[:, oc - HCH, lcols]),
                                   R(aa, aa.h[:, oc - HCH, lcols]), sg.full, ALU.mult)
                for od in range(NC):
                    wb = mstrip(1, e, od, HCH, first)
                    for tq in range(2):
                        lcols = slice(512 * tq, 512 * (tq + 1))
                        kb.mm(pmm.full, [(R(wb, wb.h[:, c, :]), R(aa, aa.h[:, c, lcols]))
                                         for c in range(HCH)])
                        kb.copy("dve", tmpa[:, 0:512], pmm.full)
                        for blk in range(4):
                            kb.trans(R(pmt, pmt.h[:, 128 * blk:128 * (blk + 1)]),
                                     tmpa[:, 128 * blk:128 * (blk + 1)], identf)
                        for blk in range(4):
                            tloc = 4 * tq + blk
                            tglob = t0 + tloc
                            kb.stt("dve", R(y_acc, y_acc.h[:, tloc, 128 * od:128 * (od + 1)]),
                                   R(pmt, pmt.h[:, 128 * blk:128 * (blk + 1)]),
                                   R(masks, masks.h[:, e, tglob:tglob + 1]),
                                   R(y_acc, y_acc.h[:, tloc, 128 * od:128 * (od + 1)]),
                                   ALU.mult, ALU.add)
            for tloc in range(NT // 2):
                t = t0 + tloc
                kb.act("act", tmpa.full, x[:, t, :], AF.Identity,
                       bias=bL[:, t:t + 1], scale=rL[:, t:t + 1])
                kb.tt_("dve", tmpb.full, R(y_acc, y_acc.h[:, tloc, :]), tmpa.full, ALU.add)
                tb = kb.sbc("moey", [128, 1, D], F32)
                kb.copy("dve", tb[:, 0, :], tmpb.full)
                r2, b2 = ln_stats(kb, tb, [0], 1e-5, junk, name="lnmo")
                kb.act("act", tmpa.full, tb[:, 0, :], AF.Identity,
                       bias=b2[:, 0:1], scale=r2[:, 0:1])
                kb.tt_("dve", x[:, t, :], x[:, t, :], tmpa.full, ALU.add)




def emit_attn_l1(kb, st, P):
    x = st["x"]
    identb = st["identb"]
    junk = st["junk"]
    H, DH = 16, 64
    SCL = 0.125

    with kb.phase():
        bo = kb.sb("bo1", [128, NC], F32)
        kb.dma("sp", bo.full, P["bo_r"])
        Q = kb.sb("Qfm1", [128, NC, S], BF16)
        Kt = kb.sb("Kfm1", [128, NC, S], BF16)
        V = kb.sb("Vaug1", [128, NT, H, DH], BF16)
        kg = kb.sb("kg", [128, NC, NT, 1], BF16)
        Vg = kb.sb("Vg", [16, H, DH + 1], BF16)
        kb.memset("dve", Vg[:, :, DH:DH + 1], 1.0)

        with kb.phase():
            xn_fm = kb.sb("xn_fm1", [128, NC, S], BF16)
            tmp = kb.sb("lntmp1", [128, D], BF16)
            ptile = kb.ps("ptile1", [128, D], BF16)
            r, b = ln_stats(kb, x, list(range(NT)), 1e-5, junk)
            ln_transpose_fm(kb, xn_fm, x, list(range(NT)), r, b, identb, tmp, ptile)
            xn_g = kb.sb("xn_g", [128, NC, NT, 1], BF16)
            kb.copy("dve", xn_g.full,
                    R(xn_fm, xn_fm.h[:].rearrange("p c (g s) -> p c g s", s=128)[:, :, :, 0:1]))

            stage = kb.sb("wstg1", [128, 4, 128], F32)
            wbuf = kb.sb("wbuf1", [128, NC, 128], BF16)
            pq = kb.ps("pq1", [128, 512])
            wap = P["wqkvT"].rearrange("(c p) o -> p c o", p=128)

            def ldstrip(ocol):
                for hh in range(2):
                    kb.dma("sp", stage.full,
                           wap[:, 4 * hh:4 * (hh + 1), 128 * ocol:128 * (ocol + 1)])
                    kb.cast(R(wbuf, wbuf.h[:, 4 * hh:4 * (hh + 1), :]), stage.full)

            def proj_fm(dst, ocol0, scl=1.0):
                for oc in range(NC):
                    ldstrip(ocol0 + oc)
                    for qc in range(4):
                        kb.mm(pq.full, [(wbuf[:, c, :], xn_fm[:, c, 512 * qc:512 * (qc + 1)])
                                        for c in range(NC)])
                        kb.act("act", dst[:, oc, 512 * qc:512 * (qc + 1)], pq.full,
                               AF.Identity, scale=scl)

            proj_fm(Q, 0, 0.125)
            proj_fm(Kt, 8)
            kb.copy("dve", kg.full,
                    R(Kt, Kt.h[:].rearrange("p c (g s) -> p c g s", s=128)[:, :, :, 0:1]))
            for oc in range(NC):
                ldstrip(16 + oc)
                for kt in range(NT):
                    pv = R(pq, pq.h[:, 0:128])
                    kb.mm(pv, [(xn_fm[:, c, 128 * kt:128 * (kt + 1)], wbuf[:, c, :])
                               for c in range(NC)])
                    kb.copy("dve", V[:, kt, 2 * oc:2 * oc + 2, 0:DH],
                            R(pq, pq.h[:, 0:128].rearrange("p (h d) -> p h d", h=2)))
                pvg = R(pq, pq.h[0:16, 0:128])
                kb.mm(pvg, [(R(xn_g, xn_g.h[:, c, :, 0]), wbuf[:, c, :]) for c in range(NC)])
                kb.copy("dve", R(Vg, Vg.h[0:16, 2 * oc:2 * oc + 2, 0:DH]),
                        R(pq, pq.h[0:16, 0:128].rearrange("p (h d) -> p h d", h=2)))

        maskloc = kb.sb("maskloc", [128, 2, 128], BF16)
        kb.dma("sp", maskloc.full, P["maskloc"])
        wo = kb.sb("wo1", [128, NC, D], BF16)
        with kb.phase():
            wstg2 = kb.sb("wstg21", [128, NC, 128], F32)
            woap = P["woT"].rearrange("(c p) o -> p c o", p=128)
            for oc in range(NC):
                kb.dma("sp", wstg2.full, woap[:, :, 128 * oc:128 * (oc + 1)])
                kb.cast(wo[:, :, 128 * oc:128 * (oc + 1)], wstg2.full)
        NB = 3
        ctxq = kb.sb("ctxq1", [128, NC, 512], BF16)
        pbufs = [kb.sb("pbuf1", [128, 512], BF16) for _ in range(NB)]
        vscs = [kb.sb("vsc1", [128, DH + 1], BF16) for _ in range(NB)]
        vscg = kb.sb("vscg", [16, DH + 1], BF16)
        mus = [kb.sb("mu1", [128, 1], F32) for _ in range(NB)]
        rvs = [kb.sb("rv1", [128, 1], F32) for _ in range(NB)]
        dens = kb.sb("dens1", [1, 512], BF16)
        recb = kb.sb("recb1", [64, 512], BF16)
        onesf = kb.sb("onesf1", [1, 64], BF16)
        kb.memset("dve", onesf.full, 1.0)
        tmpg = pbufs[1]
        yfm = kb.sb("yfm_1", [128, NC, 256], F32)
        tmp2 = junk
        ytm = kb.sb("ytm_1", [128, 1, D], F32)

        psts = [kb.ps("pst1", [128, 512]) for _ in range(NB)]
        pcx = kb.ps("pcx1", [65, 512])
        pcg = kb.ps("pcg1", [65, 512])
        pbc = kb.ps("pbc1", [64, 512])
        pyo = kb.ps("pyo1", [128, 512])
        pt = kb.ps("pte1", [128, 512], F32)

        def normalize(pnum, dst, add, off=0):
            kb.copy("dve", dens.full, R(pnum, pnum.h[64:65, :]))
            kb.act("act", dens.full, dens.full, AF.Ln)
            kb.act("act", dens.full, dens.full, AF.Exp, scale=-1.0)
            kb.mm(pbc.full, [(onesf.full, dens.full)])
            kb.copy("dve", recb.full, pbc.full)
            if add:
                kb.tt_("dve", R(tmpg, tmpg.h[off:off + 64, :]), R(pnum, pnum.h[0:64, :]),
                       recb.full, ALU.mult)
            else:
                kb.tt_("dve", dst, R(pnum, pnum.h[0:64, :]), recb.full, ALU.mult)

        bctr = [0]
        for qc in range(4):
            qcols = slice(512 * qc, 512 * (qc + 1))
            for h in range(H):
                ch, off = h // 2, 64 * (h % 2)
                for j in range(4):
                    qt = 4 * qc + j
                    kts = [kt for kt in (qt - 1, qt) if kt >= 0]
                    for i, kt in enumerate(kts):
                        bi = bctr[0] % NB
                        bctr[0] += 1
                        pst, pbuf, vsc, mu, rv = psts[bi], pbufs[bi], vscs[bi], mus[bi], rvs[bi]
                        pslice = R(pst, pst.h[:, 0:128])
                        kb.mm(pslice, [(R(Kt, Kt.h[off:off + 64, ch, 128 * kt:128 * (kt + 1)]),
                                        R(Q, Q.h[off:off + 64, ch, 128 * qt:128 * (qt + 1)]))])
                        kb.red(mu.full, pslice, ALU.max, negate=True)
                        kb.act("act", R(pbuf, pbuf.h[:, 0:128]), pslice, AF.Exp,
                               bias=mu.full)
                        mj = 0 if kt < qt else 1
                        kb.tt_("dve", R(pbuf, pbuf.h[:, 0:128]), R(pbuf, pbuf.h[:, 0:128]),
                               maskloc[:, mj, :], ALU.mult)
                        kb.act("act", rv.full, mu.full, AF.Exp, scale=-1.0)
                        kb.act("act", R(vsc, vsc.h[:, 0:DH]), V[:, kt, h, :],
                               AF.Identity, scale=rv.full)
                        kb.act("act", R(vsc, vsc.h[:, DH:DH + 1]), rv.full, AF.Identity)
                        kb.mm(R(pcx, pcx.h[:, 128 * j:128 * (j + 1)]),
                              [(vsc.full, R(pbuf, pbuf.h[:, 0:128]))],
                              start=(i == 0), stop=(i == len(kts) - 1))
                normalize(pcx, R(ctxq, ctxq.h[off:off + 64, ch, :]), add=False)
                pst, pbuf, mu, rv = psts[0], pbufs[0], mus[0], rvs[0]
                pg = R(pst, pst.h[0:16, 0:512])
                kb.mm(pg, [(R(kg, kg.h[off:off + 64, ch, :, 0]),
                            R(Q, Q.h[off:off + 64, ch, qcols]))])
                kb.red(R(mu, mu.h[0:16, :]), pg, ALU.max, negate=True)
                kb.act("act", R(pbuf, pbuf.h[0:16, :]), pg, AF.Exp,
                       bias=R(mu, mu.h[0:16, :]))
                kb.act("act", R(rv, rv.h[0:16, :]), R(mu, mu.h[0:16, :]), AF.Exp, scale=-1.0)
                kb.act("act", vscg.full, R(Vg, Vg.h[0:16, h, :]), AF.Identity,
                       scale=R(rv, rv.h[0:16, :]))
                kb.mm(pcg.full, [(vscg.full, R(pbuf, pbuf.h[0:16, :]))])
                normalize(pcg, None, add=True, off=off)
                kb.tt_("dve", R(ctxq, ctxq.h[off:off + 64, ch, :]),
                       R(ctxq, ctxq.h[off:off + 64, ch, :]),
                       R(tmpg, tmpg.h[off:off + 64, :]), ALU.add)
            for tg in range(2):
                for oc in range(NC):
                    kb.mm(R(pyo, pyo.h[:, 0:256]),
                          [(R(wo, wo.h[:, c, 128 * oc:128 * (oc + 1)]),
                            ctxq[:, c, 256 * tg:256 * (tg + 1)]) for c in range(NC)])
                    kb.ts_("dve", yfm[:, oc, :], R(pyo, pyo.h[:, 0:256]),
                           bo[:, oc:oc + 1], ALU.add)
                epilogue_fm_group(kb, st, x, yfm, pt, tmp2, junk, 4 * qc + 2 * tg, 2, ytm)




def emit_attn_yat(kb, st, P):
    x = st["x"]
    identb = st["identb"]
    identf = st["identf"]
    junk = st["junk"]
    H, DH = 16, 64
    LNC3 = float(np.log(np.sqrt(1024.0) / np.log(1025.0)))

    with kb.phase():
        q_fm = kb.sb("q3", [128, NC, S], BF16)
        k_fm = kb.sb("k3", [128, NC, S], BF16)
        V_tm = kb.sb("v3", [128, NT, D], BF16)
        ones1 = kb.sb("ones1", [1, 128], F32)
        kb.memset("dve", ones1.full, 1.0)
        ones64 = kb.sb("ones64", [128, 1], F32)
        kb.memset("dve", ones64.full, 1.0)
        onesb = kb.sb("onesb1", [1, 128], BF16)
        kb.memset("dve", onesb.full, 1.0)
        bias_map = {}
        for nm in ("bq", "bk", "bv", "bo"):
            t_ = kb.sb(nm + "3", [128, NC], F32)
            kb.dma("sp", t_.full, P[nm + "_r"])
            bias_map[nm] = t_
        scb = {}
        pbx = None
        for nm in ("aq", "ak", "av"):
            a1 = kb.sb(nm, [1, 1], F32)
            kb.dma("sp", a1.full, P[nm])
            kb.act("act", a1.full, a1.full, AF.Exp, scale=LNC3)
            sb_ = kb.sb(nm + "b", [128, 1], F32)
            scb[nm] = (a1, sb_)

        L3DBGP = int(os.environ.get("L3DBG", "0"))
        with kb.phase():
            xn_fm = kb.sb("xn3", [128, NC, S], BF16)
            nxb_all = kb.sb("nxb3", [128, NT, 128], BF16)
            nx = kb.sb("nx3", [128, NT], F32)
            with kb.phase():
                pb1 = kb.ps("pb13", [128, 128])
                for nm in ("aq", "ak", "av"):
                    a1, sb_ = scb[nm]
                    kb.mm(R(pb1, pb1.h[:, 0:1]), [(ones1.full, a1.full)])
                    kb.copy("dve", sb_.full, R(pb1, pb1.h[:, 0:1]))
                tmp = kb.sb("lntmp3", [128, D], BF16)
                ptile = kb.ps("ptile3", [128, D], BF16)
                r, b = ln_stats(kb, x, list(range(NT)), 1e-5, junk)
                for t in range(NT):
                    kb.act("act", tmp.full, x[:, t, :], AF.Identity,
                           bias=b[:, t:t + 1], scale=r[:, t:t + 1])
                    kb.act("act", junk[:, 0:D], tmp.full, AF.Square, accum=nx[:, t:t + 1])
                ln_transpose_fm(kb, xn_fm, x, list(range(NT)), r, b, identb, tmp, ptile)
                nxrow = kb.sb("nxrow", [1, 128], F32)
                pnx = kb.ps("pnx3", [1, 128])
                for t in range(NT if L3DBGP < 4 else 0):
                    kb.trans(pnx.full, nx[:, t:t + 1], identf)
                    kb.copy("dve", nxrow.full, pnx.full)
                    kb.mm(pb1.full, [(ones1.full, nxrow.full)])
                    kb.copy("dve", R(nxb_all, nxb_all.h[:, t, :]), pb1.full)
                if L3DBGP >= 4:
                    kb.memset("dve", nxb_all.full, 1.0)
            ptile = kb.ps("ptile3b", [128, D], BF16)

            stage = kb.sb("wstg3", [128, 2, 128], F32)
            wbuf = kb.sb("wbuf3", [128, NC, 128], BF16)
            pd = kb.ps("pd3", [128, 512])

            sq_sb = kb.sb("sq3", [128, 256], F32)
            junkf = kb.sb("junkf3", [128, 256], BF16)
            knw_row = kb.sb("knwrow", [1, 128], F32)
            den_sb = kb.sb("den3", [128, 256], F32)
            nxbf = kb.sb("nxbf3", [128, 256], F32)
            nww = kb.sb("nw3", [128, NC], F32)

            if L3DBGP in (3, 4) or L3DBGP >= 10:
                pass
            if L3DBGP >= 10:
                kb.memset("dve", q_fm.full, 0.01)
                kb.memset("dve", k_fm.full, 0.01)
                kb.memset("dve", V_tm.full, 0.01)
            onesbc = kb.sb("onesbc", [128, 1], BF16)
            kb.memset("dve", onesbc.full, 1.0)
            pnw = kb.ps("pnw3", [1, 128])
            for pj, wkey, bkey, akey in (() if L3DBGP in (3, 4) else (
                (0, "wq", "bq", "aq"), (1, "wk", "bk", "ak"),
                (2, "wv", "bv", "av"))):
                wap = P[wkey].rearrange("(c p) o -> p c o", p=128)
                bv_ = bias_map[bkey]
                scv = scb[akey][1]
                for oc in range(NC):
                    for hh in range(4):
                        kb.dma("pool", stage.full,
                               wap[:, 2 * hh:2 * (hh + 1), 128 * oc:128 * (oc + 1)])
                        kb.cast(wbuf[:, 2 * hh:2 * (hh + 1), :], stage.full,
                                engines=("dve", "act"))
                        if L3DBGP == 10:
                            continue
                        kb.tt_("dve", junkf.full,
                               R(stage, stage.h[:].rearrange("p a b -> p (a b)")),
                               R(stage, stage.h[:].rearrange("p a b -> p (a b)")),
                               ALU.mult)
                        jv = junkf.h[:].rearrange("p (a b) -> p a b", a=2)
                        kb.mm(pnw.full, [(onesbc.full, R(junkf, jv[:, a_, :]))
                                         for a_ in range(2)],
                              start=(hh == 0), stop=(hh == 3))
                    if L3DBGP == 10:
                        kb.memset("dve", nww[:, oc:oc + 1], 1.0)
                    else:
                        kb.copy("dve", R(knw_row, knw_row.h[:, 0:128]), pnw.full)
                        kb.trans(R(pd, pd.h[:, 0:1]), R(knw_row, knw_row.h[:, 0:128]),
                                 R(identf, identf.h[0:1, 0:1]))
                        kb.copy("dve", nww[:, oc:oc + 1], R(pd, pd.h[:, 0:1]))
                    pdv = R(pd, pd.h[:, 0:256])
                    for tq in range(8 if L3DBGP not in (11,) else 0):
                        cols = slice(256 * tq, 256 * (tq + 1))
                        kb.mm(pdv, [(wbuf[:, c, :], xn_fm[:, c, cols])
                                    for c in range(NC)])
                        kb.act("act", sq_sb.full, pdv, AF.Square)
                        if L3DBGP == 20:
                            continue
                        kb.copy("dve", nxbf.full,
                                R(nxb_all, nxb_all.h[:].rearrange(
                                    "p t n -> p (t n)")[:, cols]))
                        if L3DBGP == 23:
                            continue
                        kb.ts_("dve", den_sb.full, pdv, -2.0, ALU.mult)
                        kb.tt_("dve", den_sb.full, den_sb.full, nxbf.full, ALU.add)
                        if L3DBGP == 24:
                            continue
                        kb.ts_("dve", den_sb.full, den_sb.full, nww[:, oc:oc + 1],
                               ALU.add)
                        kb.ts_("dve", den_sb.full, den_sb.full, 1e-5, ALU.add)
                        if L3DBGP == 21:
                            continue
                        kb.act("act", den_sb.full, den_sb.full, AF.Ln)
                        kb.act("act", den_sb.full, den_sb.full, AF.Exp, scale=-1.0)
                        kb.tt_("dve", sq_sb.full, sq_sb.full, den_sb.full, ALU.mult)
                        if L3DBGP == 22:
                            continue
                        if pj == 0:
                            kb.ts_("dve", R(q_fm, q_fm.h[:, oc, cols]), sq_sb.full,
                                   bv_[:, oc:oc + 1], ALU.add, s2=scv.full, op1=ALU.mult)
                        elif pj == 1:
                            kb.ts_("dve", R(k_fm, k_fm.h[:, oc, cols]), sq_sb.full,
                                   bv_[:, oc:oc + 1], ALU.add, s2=scv.full, op1=ALU.mult)
                        else:
                            kb.ts_("dve", R(junkf, junkf.h[:, 0:256]), sq_sb.full,
                                   bv_[:, oc:oc + 1], ALU.add, s2=scv.full, op1=ALU.mult)
                            for blk in range(2):
                                kb.trans(R(ptile, ptile.h[:, 128 * blk:128 * (blk + 1)]),
                                         R(junkf, junkf.h[:, 128 * blk:128 * (blk + 1)]), identb)
                            for blk in range(2):
                                t_ = 2 * tq + blk
                                kb.copy("dve", R(V_tm, V_tm.h[:, t_, 128 * oc:128 * (oc + 1)]),
                                        R(ptile, ptile.h[:, 128 * blk:128 * (blk + 1)]))

        for half in range(2):
            with kb.phase():
                ctxh = kb.sb("ctxh", [128, NC, S // 2], BF16)
                with kb.phase():
                    kn_b = kb.sb("knb3", [128, S], BF16)
                    knrow = kb.sb("knrow", [1, 512], BF16)
                    qn_t = kb.sb("qnt3", [128, NT // 2], F32)
                    ksq = kb.sb("ksq3", [128, 512], F32)
                    den2s = [kb.sb("den23", [128, S], BF16) for _ in range(2)]
                    sq_bs = [kb.sb("sqb3", [128, S], BF16)]
                    pbufs = [kb.sb("pbuf3", [128, S], BF16)]
                    sumss = [kb.sb("sums3", [128, 1], F32) for _ in range(2)]
                    rs_s = [kb.sb("rs3", [128, 1], F32) for _ in range(2)]
                    nmxs = [kb.sb("nmx3", [128, 1], F32) for _ in range(2)]
                    ctmbs = [kb.sb("ctmb3", [128, 64], BF16) for _ in range(2)]

                    pS = kb.ps("pS3", [128, S])
                    pkn = kb.ps("pkn3", [1, 512])
                    pknb = kb.ps("pknb3", [128, 512])
                    pcx = kb.ps("pcx3", [128, 64])
                    ptr = kb.ps("ptr3", [128, 512], BF16)

                    for h in range(H):
                        ch, off = h // 2, 64 * (h % 2)
                        sl = slice(off, off + 64)
                        for qc in range(4):
                            cols = slice(512 * qc, 512 * (qc + 1))
                            kb.act("act", R(ksq, ksq.h[sl, :]),
                                   R(k_fm, k_fm.h[sl, ch, cols]), AF.Square)
                            kb.mm(pkn.full, [(R(ones64, ones64.h[sl, :]),
                                              R(ksq, ksq.h[sl, :]))])
                            kb.copy("dve", knrow.full, pkn.full)
                            kb.mm(pknb.full, [(onesb.full, knrow.full)])
                            kb.copy("dve", R(kn_b, kn_b.h[:, cols]), pknb.full)
                        for j in range(NT // 2):
                            t_ = (NT // 2) * half + j
                            kb.act("act", R(ksq, ksq.h[sl, 0:128]),
                                   R(q_fm, q_fm.h[sl, ch, 128 * t_:128 * (t_ + 1)]), AF.Square)
                            kb.mm(R(pkn, pkn.h[:, 0:128]),
                                  [(R(ones64, ones64.h[sl, :]), R(ksq, ksq.h[sl, 0:128]))])
                            kb.copy("dve", R(knrow, knrow.h[:, 0:128]), R(pkn, pkn.h[:, 0:128]))
                            kb.trans(R(ptr, ptr.h[:, 0:1]), R(knrow, knrow.h[:, 0:128]),
                                     R(identb, identb.h[0:1, 0:1]))
                            kb.copy("dve", qn_t[:, j:j + 1], R(ptr, ptr.h[:, 0:1]))
                        for j in range(NT // 2):
                            qt = (NT // 2) * half + j
                            par = j % 2
                            sq_b, den2, pbuf = sq_bs[0], den2s[par], pbufs[0]
                            sums, rs_, nmx, ctmb = (sumss[par], rs_s[par],
                                                    nmxs[par], ctmbs[par])
                            for qc in range(4):
                                kb.mm(R(pS, pS.h[:, 512 * qc:512 * (qc + 1)]),
                                      [(R(q_fm, q_fm.h[sl, ch, 128 * qt:128 * (qt + 1)]),
                                        R(k_fm, k_fm.h[sl, ch, 512 * qc:512 * (qc + 1)]))])
                            kb.act("act", sq_b.full, R(pS, pS.h[:, :]), AF.Square)
                            kb.stt("dve", den2.full, sq_b.full, kb.constf(-2.0), kn_b.full,
                                   ALU.mult, ALU.add)
                            kb.ts_("pool", den2.full, den2.full, qn_t[:, j:j + 1],
                                   ALU.add, s2=1e-5, op1=ALU.add)
                            kb.recip(den2.full, den2.full)
                            kb.tt_("pool", den2.full, sq_b.full, den2.full, ALU.mult)
                            kb.red(nmx.full, den2.full, ALU.max, negate=True)
                            kb.act("act", pbuf.full, den2.full, AF.Exp, bias=nmx.full,
                                   accum=sums.full)
                            kb.recip(rs_.full, sums.full)
                            for ktg in range(4):
                                for b4 in range(4):
                                    kt = 4 * ktg + b4
                                    kb.trans(R(ptr, ptr.h[:, 128 * b4:128 * (b4 + 1)]),
                                             pbuf[:, 128 * kt:128 * (kt + 1)], identb)
                                kb.copy("act", pbuf[:, 512 * ktg:512 * (ktg + 1)], ptr.full)
                            kb.mm(pcx.full,
                                  [(pbuf[:, 128 * kt:128 * (kt + 1)],
                                    R(V_tm, V_tm.h[:, kt, 64 * h:64 * h + 64]))
                                   for kt in range(NT)])
                            kb.ts_("dve", ctmb.full, pcx.full, rs_.full, ALU.mult)
                            kb.trans(R(ptr, ptr.h[0:64, 0:128]), ctmb.full,
                                     R(identb, identb.h[0:128, 0:128]))
                            kb.copy("dve", R(ctxh, ctxh.h[sl, ch, 128 * j:128 * (j + 1)]),
                                    R(ptr, ptr.h[0:64, 0:128]))

                with kb.phase():
                    wob = kb.sb("wob3", [128, NC, D], BF16)
                    stage2 = kb.sb("stg3b", [128, 4, 128], F32)
                    woap = P["wo"].rearrange("(c p) o -> p c o", p=128)
                    for oc in range(NC):
                        for hh in range(2):
                            kb.dma("sp", stage2.full,
                                   woap[:, 4 * hh:4 * (hh + 1), 128 * oc:128 * (oc + 1)])
                            kb.cast(R(wob, wob.h[:, 4 * hh:4 * (hh + 1),
                                                 128 * oc:128 * (oc + 1)]), stage2.full)
                    yfm = kb.sb("yfm3", [128, NC, 128], F32)
                    tmp2 = junk
                    ytm = kb.sb("ytm3", [128, 1, D], F32)
                    pyo = kb.ps("pyo3", [128, 512])
                    pt = kb.ps("pt3", [128, 512], F32)
                    for j in range(NT // 2):
                        for oc in range(NC):
                            kb.mm(R(pyo, pyo.h[:, 128 * (oc % 4):128 * (oc % 4) + 128]),
                                  [(R(wob, wob.h[:, c, 128 * oc:128 * (oc + 1)]),
                                    ctxh[:, c, 128 * j:128 * (j + 1)])
                                   for c in range(NC)])
                            kb.ts_("dve", yfm[:, oc, :],
                                   R(pyo, pyo.h[:, 128 * (oc % 4):128 * (oc % 4) + 128]),
                                   R(bias_map["bo"], bias_map["bo"].h[:, oc:oc + 1]), ALU.add)
                        epilogue_fm_group(kb, st, x, yfm, pt, tmp2, junk,
                                          (NT // 2) * half + j, 1, ytm)



def emit_ffn_quant(kb, st, P):
    x = st["x"]
    identb = st["identb"]
    identf = st["identf"]
    junk = st["junk"]
    QB = 128.0

    with kb.phase():
        gfm = kb.sb("qgam", [128, NC], F32); kb.dma("sp", gfm.full, P["g_r"])
        bfm = kb.sb("qbet", [128, NC], F32); kb.dma("sp", bfm.full, P["b_r"])
        ones1q = kb.sb("ones1q", [1, 128], F32)
        kb.memset("dve", ones1q.full, 1.0)
        onescq = kb.sb("onescq", [128, 1], F32)
        kb.memset("dve", onescq.full, 1.0)
        xn_fm = kb.sb("qxn", [128, NC, S], BF16)
        wq_b = kb.sb("qwq", [128, NC, D], BF16)
        stage = kb.sb("qstg", [128, D], F32)
        tmp = kb.sb("qtmp", [128, D], BF16)
        sc1 = kb.sb("qsc1", [128, 1], F32)
        rowq = kb.sb("qrow", [1, 128], F32)
        gcol = kb.sb("qgcol", [128, NC], F32)
        thr_b = kb.sb("qthr", [128, 1], F32)
        sxb = kb.sb("qsx", [128, 1], F32)
        soutb = kb.sb("qso", [128, 1], F32)
        beta_s = kb.sb("qbeta", [1, 1], F32)
        gam_s = kb.sb("qgam1", [1, 1], F32)
        tmpf = kb.sb("qtf", [128, 1024], F32)
        yfm = kb.sb("qyfm", [128, NC, 512], F32)
        ytm = kb.sb("qytm", [128, 1, D], F32)
        tmp2 = junk

        ptile = kb.ps("qptile", [128, D], BF16)
        prow = kb.ps("qprow", [1, 128])
        pcol = kb.ps("qpcol", [128, 128])
        pmm = kb.ps("qpmm", [128, 512])
        pt = kb.ps("qpt", [128, 512], F32)

        r, b = ln_stats(kb, x, list(range(NT)), 1e-3, junk, name="lnq")
        for t in range(NT):
            kb.act("act", tmp.full, x[:, t, :], AF.Identity,
                   bias=b[:, t:t + 1], scale=r[:, t:t + 1])
            for c in range(NC):
                kb.trans(R(ptile, ptile.h[:, 128 * c:128 * (c + 1)]),
                         tmp[:, 128 * c:128 * (c + 1)], identb)
            dsl = xn_fm.h[:, :, 128 * t:128 * (t + 1)]
            pview = ptile.h[:].rearrange("p (c n) -> p c n", c=NC)
            for c in range(NC):
                kb.emit("dve", lambda e, c=c: e.tensor_scalar(
                    out=dsl[:, c, :], in0=pview[:, c, :], scalar1=gfm.h[:, c:c + 1],
                    scalar2=bfm.h[:, c:c + 1], op0=ALU.mult, op1=ALU.add),
                    reads=[ptile, gfm, bfm], writes=[R(xn_fm, dsl)])

        for c in range(NC):
            kb.red(gcol[:, c:c + 1], xn_fm[:, c, :], ALU.max, absval=True)
        kb.red(sc1.full, gcol.full, ALU.max)
        kb.trans(prow.full, sc1.full, identf)
        kb.copy("dve", rowq.full, prow.full)
        kb.red(R(gam_s, gam_s.h[0:1, :]), rowq.full, ALU.max)

        wap = P["weight"].rearrange("(c p) o -> p c o", p=128)
        bs = kb.sb("qbs", [128, NC], F32)
        for c in range(NC):
            kb.dma("sp", stage.full, wap[:, c, :])
            kb.red(bs[:, c:c + 1], stage.full, ALU.add, absval=True)
        kb.red(sc1.full, bs.full, ALU.add)
        kb.mm(R(prow, prow.h[0:1, 0:1]), [(sc1.full, onescq.full)])
        kb.copy("dve", R(beta_s, beta_s.h[0:1, :]), R(prow, prow.h[0:1, 0:1]))
        kb.ts_("dve", R(beta_s, beta_s.h[0:1, :]), R(beta_s, beta_s.h[0:1, :]),
               1.0 / (D * D), ALU.mult)
        kb.ts_("dve", R(sc1, sc1.h[0:1, :]), R(beta_s, beta_s.h[0:1, :]),
               1e-5, ALU.add, s2=0.5, op1=ALU.mult)
        kb.mm(R(pcol, pcol.h[:, 0:1]), [(ones1q.full, R(sc1, sc1.h[0:1, 0:1]))])
        kb.copy("dve", thr_b.full, R(pcol, pcol.h[:, 0:1]))
        for c in range(NC):
            kb.dma("sp", stage.full, wap[:, c, :])
            kb.act("act", tmpf.full, stage.full, AF.Abs)
            kb.ts_("dve", tmpf.full, tmpf.full, thr_b.full, ALU.subtract)
            kb.act("act", tmpf.full, tmpf.full, AF.Sign)
            kb.ts_("dve", tmpf.full, tmpf.full, 0.0, ALU.max)
            kb.act("act", R(junk, junk.h[:, 0:D]), stage.full, AF.Sign)
            kb.tt_("dve", R(wq_b, wq_b.h[:, c, :]), tmpf.full,
                   R(junk, junk.h[:, 0:D]), ALU.mult)

        kb.act("act", R(sc1, sc1.h[0:1, :]), R(gam_s, gam_s.h[0:1, :]), AF.Ln)
        kb.act("act", R(sc1, sc1.h[0:1, :]), R(sc1, sc1.h[0:1, :]), AF.Exp, scale=-1.0)
        kb.ts_("dve", R(sc1, sc1.h[0:1, :]), R(sc1, sc1.h[0:1, :]), QB, ALU.mult)
        kb.mm(R(pcol, pcol.h[:, 0:1]), [(ones1q.full, R(sc1, sc1.h[0:1, 0:1]))])
        kb.copy("dve", sxb.full, R(pcol, pcol.h[:, 0:1]))
        for c in range(NC):
            kb.ts_("dve", R(xn_fm, xn_fm.h[:, c, :]), R(xn_fm, xn_fm.h[:, c, :]),
                   sxb.full, ALU.mult)
            kb.ts_("dve", R(xn_fm, xn_fm.h[:, c, :]), R(xn_fm, xn_fm.h[:, c, :]),
                   float(-QB + 1e-5), ALU.max)
            kb.ts_("dve", R(xn_fm, xn_fm.h[:, c, :]), R(xn_fm, xn_fm.h[:, c, :]),
                   float(QB - 1e-5), ALU.min)
        kb.tt_("dve", R(sc1, sc1.h[0:1, :]), R(gam_s, gam_s.h[0:1, :]),
               R(beta_s, beta_s.h[0:1, :]), ALU.mult)
        kb.ts_("dve", R(sc1, sc1.h[0:1, :]), R(sc1, sc1.h[0:1, :]), 1.0 / QB, ALU.mult)
        kb.mm(R(pcol, pcol.h[:, 0:1]), [(ones1q.full, R(sc1, sc1.h[0:1, 0:1]))])
        kb.copy("dve", soutb.full, R(pcol, pcol.h[:, 0:1]))

        for tg in range(4):
            cols = slice(512 * tg, 512 * (tg + 1))
            for oc in range(NC):
                kb.mm(pmm.full, [(R(wq_b, wq_b.h[:, c, 128 * oc:128 * (oc + 1)]),
                                  R(xn_fm, xn_fm.h[:, c, cols])) for c in range(NC)])
                kb.ts_("dve", yfm[:, oc, :], pmm.full, soutb.full, ALU.mult)
            epilogue_fm_group(kb, st, x, yfm, pt, tmp2, junk, 4 * tg, 4, ytm)


def declare_io(nc, names_shapes):
    out = {}
    for item in names_shapes:
        name, shape = item[0], item[1]
        dt = item[2] if len(item) > 2 else F32
        out[name] = nc.dram_tensor(name, list(shape), dt, kind="ExternalInput").ap()
    return out


def persistent_state(kb):
    nc = kb.nc
    st = {}
    identb_np = np.eye(128, dtype=BF)
    identf_np = np.eye(128, dtype=np.float32)
    hb = nc.inline_tensor(np.asarray(identb_np), "identb_c")
    hf = nc.inline_tensor(identf_np, "identf_c")
    st["identb"] = kb.sb("identb", [128, 128], BF16, persistent=True)
    st["identf"] = kb.sb("identf", [128, 128], F32, persistent=True)
    kb.dma("sp", st["identb"].full, hb[:])
    kb.dma("sp", st["identf"].full, hf[:])
    st["junk"] = kb.sb("junk", [128, 1024], BF16, persistent=True)
    st["x"] = kb.sb("stream", [128, NT, D], F32, persistent=True)
    return st


def load_stream(kb, st, x_ap):
    kb.dma("sp", st["x"].full, x_ap.rearrange("(i p) d -> p i d", p=128))


def store_stream(kb, st, y_ap):
    kb.fence()
    kb.dma("pool", y_ap.rearrange("(i p) d -> p i d", p=128), st["x"].full)
    kb.drain_all()


def prep_ffn_ternary(inp, li):
    g = lambda k: inp[f"L{li}_ffn_{k}"]
    return {
        "w1T": np.ascontiguousarray(g("w1_w").T),
        "w2T": np.ascontiguousarray(g("w2_w").T),
        "h1T": np.ascontiguousarray(g("h1_w").T),
        "h2T": np.ascontiguousarray(g("h2_w").T),
        "w3T": np.ascontiguousarray(g("w3_w").T),
        "s1r": np.ascontiguousarray(g("w1_s").reshape(32, 128).T),
        "s2r": np.ascontiguousarray(g("w2_s").reshape(32, 128).T),
        "sh1r": np.ascontiguousarray(g("h1_s").reshape(32, 128).T),
        "sh2r": np.ascontiguousarray(g("h2_s").reshape(32, 128).T),
        "s3r": np.ascontiguousarray(g("w3_s").reshape(8, 128).T),
    }


FFN_TERNARY_SHAPES = [("w1T", (D, 4096)), ("w2T", (D, 4096)), ("h1T", (4096, 4096)),
                      ("h2T", (4096, 4096)), ("w3T", (4096, D)),
                      ("s1r", (128, 32)), ("s2r", (128, 32)), ("sh1r", (128, 32)),
                      ("sh2r", (128, 32)), ("s3r", (128, 8))]

MOE_SHAPES = [("up_w", (8, D, 4096)), ("down_w", (8, 2048, D)),
              ("scale_r", (128, 8)), ("ctr", (8, D)), ("ctr_fm", (128, 8, 8))]


def prep_moe(inp):
    g = lambda k: inp[f"L0_ffn_{k}"]
    C = g("centroids")
    return {
        "up_w": np.ascontiguousarray(g("up_w")),
        "down_w": np.ascontiguousarray(g("down_w")),
        "scale_r": np.ascontiguousarray(g("scale").reshape(8, 128).T),
        "ctr": np.ascontiguousarray(C),
        "ctr_fm": np.ascontiguousarray(C.T.reshape(8, 128, 8).transpose(1, 0, 2)),
    }


ATTN_SHAPES_BASE = [("wqkvT", (D, 3 * D)), ("woT", (D, D)), ("mask01", (128, 896), BF16)]
ATTN_SHAPES_L0 = ATTN_SHAPES_BASE + [("bo_r", (128, 8))]
ATTN_SHAPES_L2 = ATTN_SHAPES_BASE + [("cos_fm", (128, S), BF16), ("sin_fm", (128, S), BF16),
                                     ("rt", (128, 64), BF16)]


def host_mask01():
    p = np.arange(128)[:, None]
    m = np.arange(896)[None, :]
    return (m >= 384 + p).astype(BF)


def host_rope_tables():
    inv = 1.0 / (10000.0 ** (np.arange(0, 64, 2, dtype=np.float32) / 64.0))
    t = np.arange(S, dtype=np.float32)
    fr = np.einsum("i,j->ij", t, inv)
    emb = np.concatenate([fr, fr], axis=-1)
    cos = np.cos(emb).T.astype(BF)
    sin = np.sin(emb).T.astype(BF)
    cos = np.concatenate([cos, cos], axis=0)
    sin = np.concatenate([sin, sin], axis=0)
    return np.ascontiguousarray(cos), np.ascontiguousarray(sin)


def host_rt():
    rt = np.zeros((64, 64), dtype=BF)
    for i in range(32):
        rt[i + 32, i] = -1.0
    for i in range(32, 64):
        rt[i - 32, i] = 1.0
    return np.ascontiguousarray(np.concatenate([rt, rt], axis=0))


def prep_attn_l0(inp):
    g = lambda k: inp[f"L0_attn_{k}"]
    return {
        "wqkvT": np.ascontiguousarray(np.concatenate(
            [g("w_q").T, g("w_k").T, g("w_v").T], axis=1)),
        "woT": np.ascontiguousarray(g("w_o").T),
        "bo_r": np.ascontiguousarray(g("b_o").reshape(8, 128).T),
        "mask01": host_mask01(),
    }


ATTN_SHAPES_L1 = [("wqkvT", (D, 3 * D)), ("woT", (D, D)), ("bo_r", (128, 8)),
                  ("maskloc", (128, 2, 128), BF16)]


ATTN_SHAPES_L3 = [("wq", (D, D)), ("wk", (D, D)), ("wv", (D, D)), ("wo", (D, D)),
                  ("bq_r", (128, 8)), ("bk_r", (128, 8)), ("bv_r", (128, 8)),
                  ("bo_r", (128, 8)),
                  ("aq", (1, 1)), ("ak", (1, 1)), ("av", (1, 1))]


def prep_attn_l3(inp):
    g = lambda k: inp[f"L3_attn_{k}"]
    return {
        "wq": np.ascontiguousarray(g("wq")), "wk": np.ascontiguousarray(g("wk")),
        "wv": np.ascontiguousarray(g("wv")), "wo": np.ascontiguousarray(g("wo")),
        "bq_r": np.ascontiguousarray(g("bq").reshape(8, 128).T),
        "bk_r": np.ascontiguousarray(g("bk").reshape(8, 128).T),
        "bv_r": np.ascontiguousarray(g("bv").reshape(8, 128).T),
        "bo_r": np.ascontiguousarray(g("bo").reshape(8, 128).T),
        "aq": g("aq").reshape(1, 1).astype(np.float32),
        "ak": g("ak").reshape(1, 1).astype(np.float32),
        "av": g("av").reshape(1, 1).astype(np.float32),
    }


FFN_QUANT_SHAPES = [("weight", (D, D)), ("g_r", (128, 8)), ("b_r", (128, 8))]


def prep_ffn_quant(inp):
    g = lambda k: inp[f"L3_ffn_{k}"]
    return {
        "weight": np.ascontiguousarray(g("weight")),
        "g_r": np.ascontiguousarray(g("ln_gamma").reshape(8, 128).T),
        "b_r": np.ascontiguousarray(g("ln_beta").reshape(8, 128).T),
    }


def prep_attn_l1(inp):
    g = lambda k: inp[f"L1_attn_{k}"]
    p = np.arange(128)[:, None]
    c = np.arange(128)[None, :]
    maskloc = np.stack([(c <= p), (c >= p)], axis=1).astype(BF)
    return {
        "wqkvT": np.ascontiguousarray(g("w_qkv").T),
        "woT": np.ascontiguousarray(g("w_out").T),
        "bo_r": np.ascontiguousarray(g("b_out").reshape(8, 128).T),
        "maskloc": maskloc,
    }


def prep_attn_l2(inp):
    g = lambda k: inp[f"L2_attn_{k}"]
    cos, sin = host_rope_tables()
    return {
        "wqkvT": np.ascontiguousarray(g("w_qkv").T),
        "woT": np.ascontiguousarray(g("w_out").T),
        "mask01": host_mask01(),
        "cos_fm": cos, "sin_fm": sin, "rt": host_rt(),
    }


def build_block_program(block):
    nc = bass.Bass("TRN2", target_bir_lowering=False)
    x_ap = nc.dram_tensor("x", [S, D], F32, kind="ExternalInput").ap()
    y_ap = nc.dram_tensor("y", [S, D], F32, kind="ExternalOutput").ap()
    kb = KB(nc)
    st = persistent_state(kb)
    load_stream(kb, st, x_ap)
    if block == "ffn_ternary":
        P = declare_io(nc, FFN_TERNARY_SHAPES)
        emit_ffn_ternary(kb, st, P)
    elif block == "attn_l0":
        P = declare_io(nc, ATTN_SHAPES_L0)
        emit_attn_causal(kb, st, P, rope=False, bias=True)
    elif block == "attn_l2":
        P = declare_io(nc, ATTN_SHAPES_L2)
        emit_attn_causal(kb, st, P, rope=True, bias=False)
    elif block == "moe":
        P = declare_io(nc, MOE_SHAPES)
        emit_moe(kb, st, P)
    elif block == "attn_l1":
        P = declare_io(nc, ATTN_SHAPES_L1)
        emit_attn_l1(kb, st, P)
    elif block == "attn_l3":
        P = declare_io(nc, ATTN_SHAPES_L3)
        emit_attn_yat(kb, st, P)
    elif block == "ffn_quant":
        P = declare_io(nc, FFN_QUANT_SHAPES)
        emit_ffn_quant(kb, st, P)
    else:
        raise ValueError(block)
    store_stream(kb, st, y_ap)
    return nc


FUSED_CHAIN = [
    ("b0", ATTN_SHAPES_L0, lambda kb, st, P: emit_attn_causal(kb, st, P, rope=False, bias=True),
     lambda inp: prep_attn_l0(inp)),
    ("b1", MOE_SHAPES, lambda kb, st, P: emit_moe(kb, st, P),
     lambda inp: prep_moe(inp)),
    ("b2", ATTN_SHAPES_L1, lambda kb, st, P: emit_attn_l1(kb, st, P),
     lambda inp: prep_attn_l1(inp)),
    ("b3", FFN_TERNARY_SHAPES, lambda kb, st, P: emit_ffn_ternary(kb, st, P),
     lambda inp: prep_ffn_ternary(inp, 1)),
    ("b4", ATTN_SHAPES_L2, lambda kb, st, P: emit_attn_causal(kb, st, P, rope=True, bias=False),
     lambda inp: prep_attn_l2(inp)),
    ("b5", FFN_TERNARY_SHAPES, lambda kb, st, P: emit_ffn_ternary(kb, st, P),
     lambda inp: prep_ffn_ternary(inp, 2)),
    ("b6", ATTN_SHAPES_L3, lambda kb, st, P: emit_attn_yat(kb, st, P),
     lambda inp: prep_attn_l3(inp)),
    ("b7", FFN_QUANT_SHAPES, lambda kb, st, P: emit_ffn_quant(kb, st, P),
     lambda inp: prep_ffn_quant(inp)),
]


def build_fused_program():
    nc = bass.Bass("TRN2", target_bir_lowering=False)
    x_ap = nc.dram_tensor("x", [S, D], F32, kind="ExternalInput").ap()
    y_ap = nc.dram_tensor("y", [S, D], F32, kind="ExternalOutput").ap()
    kb = KB(nc)
    st = persistent_state(kb)
    load_stream(kb, st, x_ap)
    for prefix, shapes, emitfn, _ in FUSED_CHAIN:
        pre = [(f"{prefix}_{it[0]}",) + tuple(it[1:]) for it in shapes]
        Praw = declare_io(nc, pre)
        P = {k[len(prefix) + 1:]: v for k, v in Praw.items()}
        emitfn(kb, st, P)
    store_stream(kb, st, y_ap)
    return nc


def prep_all(inputs):
    d = {}
    for prefix, _, _, prep in FUSED_CHAIN:
        for k, v in prep(inputs).items():
            d[f"{prefix}_{k}"] = v
    return d


_CHAIN = [
    ("attn_l0", lambda inp: prep_attn_l0(inp)),
    ("moe", lambda inp: prep_moe(inp)),
    ("attn_l1", lambda inp: prep_attn_l1(inp)),
    ("ffn_ternary", lambda inp: prep_ffn_ternary(inp, 1)),
    ("attn_l2", lambda inp: prep_attn_l2(inp)),
    ("ffn_ternary", lambda inp: prep_ffn_ternary(inp, 2)),
    ("attn_l3", lambda inp: prep_attn_l3(inp)),
    ("ffn_quant", lambda inp: prep_ffn_quant(inp)),
]

_prog_cache = {}


def _get_prog(block):
    if block not in _prog_cache:
        _prog_cache[block] = build_block_program(block)
    return _prog_cache[block]


def kernel(**inputs):
    x = inputs["L0_attn_x"]
    shards = [np.ascontiguousarray(x[c]) for c in range(NCORES)]
    if os.environ.get("MK_UNFUSED", "0") == "1":
        for block, prep in _CHAIN:
            nc = _get_prog(block)
            P = prep(inputs)
            in_maps = [{"x": shards[c], **P} for c in range(NCORES)]
            res = run_bass_kernel_spmd(nc, in_maps, core_ids=list(range(NCORES)))
            shards = [np.asarray(r["y"], dtype=np.float32) for r in res.results]
        return np.stack(shards, axis=0)
    if "fused" not in _prog_cache:
        _prog_cache["fused"] = build_fused_program()
    nc = _prog_cache["fused"]
    P = prep_all(inputs)
    in_maps = [{"x": shards[c], **P} for c in range(NCORES)]
    res = run_bass_kernel_spmd(nc, in_maps, core_ids=list(range(NCORES)))
    return np.stack([np.asarray(r["y"], dtype=np.float32) for r in res.results], axis=0)


# version 07
# speedup vs baseline: 1.1700x; 1.1700x over previous
import os
import numpy as np
import ml_dtypes
from contextlib import ExitStack

import concourse.bass as bass
import concourse.mybir as mybir
from concourse.bass_utils import run_bass_kernel_spmd

F32 = mybir.dt.float32
BF16 = mybir.dt.bfloat16
AF = mybir.ActivationFunctionType
ALU = mybir.AluOpType
AX = mybir.AxisListType

NCORES = 8
S = 2048
D = 1024
NT = S // 128
NC = D // 128
BF = ml_dtypes.bfloat16

ENGS = ("pe", "act", "dve", "pool", "sp")


class R:
    __slots__ = ("tt", "ap")
    def __init__(self, tt, ap):
        self.tt = tt
        self.ap = ap


class TT:
    __slots__ = ("kb", "h", "lw", "readers", "is_psum")
    def __init__(self, kb, h, is_psum=False):
        self.kb = kb
        self.h = h
        self.lw = None
        self.readers = []
        self.is_psum = is_psum
    def __getitem__(self, idx):
        return R(self, self.h[idx])
    @property
    def full(self):
        return R(self, self.h[:])


class KB:

    def __init__(self, nc):
        self.nc = nc
        self.eng = {"pe": nc.tensor, "act": nc.scalar, "dve": nc.vector,
                    "pool": nc.gpsimd, "sp": nc.sync}
        self.stack = ExitStack()
        self.sems = {}
        self.cnt = {}
        self.wm = {}
        self.fences = {}
        self._uid = 0
        self._es = None
        self._cast_rr = 0
        self._consts = {}
        self._cache = [{}]
        self.cur = {}
        self.last_dma = {}
        for e in ENGS:
            self.cur["s_" + e] = "s_" + e
            self._mksem("s_" + e)
        for q in ("sp", "pool", "act", "dve"):
            self.cur["d_" + q] = "d_" + q
            self._mksem("d_" + q)

    def _mksem(self, name):
        self.sems[name] = self.stack.enter_context(self.nc.semaphore(name))
        self.cnt[name] = 0

    def _sem(self, role):
        name = self.cur[role]
        if self.cnt[name] >= 60000:
            name = f"{role}#{len(self.cnt)}"
            self.cur[role] = name
            self._mksem(name)
        return self.cur[role]

    def uid(self, base):
        self._uid += 1
        return f"{base}_{self._uid}"

    def sb(self, name, shape, dtype, persistent=False):
        nm = self.uid(name)
        if persistent or self._es is None:
            h = self.nc.alloc_sbuf_tensor(nm, shape, dtype, side="right")
        else:
            h = self._es.enter_context(self.nc.sbuf_tensor(nm, shape, dtype))
        return TT(self, h)

    def sbc(self, name, shape, dtype):
        key = (name, tuple(shape), dtype)
        c = self._cache[-1]
        if key not in c:
            c[key] = self.sb(name, shape, dtype)
        return c[key]

    def ps(self, name, shape, dtype=F32):
        nm = self.uid(name)
        assert self._es is not None
        h = self._es.enter_context(self.nc.psum_tensor(nm, shape, dtype))
        return TT(self, h, is_psum=True)

    def phase(self):
        kb = self
        class _P:
            def __enter__(self):
                self.es = ExitStack()
                self.prev = kb._es
                kb._es = self.es.__enter__()
                kb._cache.append({})
                return kb._es
            def __exit__(self, *a):
                self.es.__exit__(*a)
                kb._es = self.prev
                kb._cache.pop()
                kb.fence()
                return False
        return _P()

    def fence(self):
        for s, v in self.cnt.items():
            if v > self.fences.get(s, 0):
                self.fences[s] = v

    def _wait(self, ename, sem, val):
        if val <= 0:
            return
        if self.wm.get((ename, sem), 0) >= val:
            return
        self.eng[ename].wait_ge(self.sems[sem], val)
        self.wm[(ename, sem)] = val

    def _gather(self, ename, reads, writes):
        need = dict(self.fences)
        def add(tok):
            if tok is not None:
                s, v = tok
                if need.get(s, 0) < v:
                    need[s] = v
        def tt_of(x):
            return x if isinstance(x, TT) else x.tt
        for r in reads:
            t = tt_of(r)
            add(t.lw)
            if t.is_psum:
                for tok in t.readers:
                    add(tok)
        for w in writes:
            add(tt_of(w).lw)
            for tok in tt_of(w).readers:
                add(tok)
        for s, v in sorted(need.items()):
            if ename == "pe" and s.startswith("s_pe"):
                continue
            self._wait(ename, s, v)

    def emit(self, ename, builder, reads=(), writes=()):
        self._gather(ename, reads, writes)
        inst = builder(self.eng[ename])
        sem = self._sem("s_" + ename)
        self.cnt[sem] += 1
        inst.then_inc(self.sems[sem], 1)
        tok = (sem, self.cnt[sem])
        for w in writes:
            (w if isinstance(w, TT) else w.tt).lw = tok
            (w if isinstance(w, TT) else w.tt).readers = []
        for r in reads:
            (r if isinstance(r, TT) else r.tt).readers.append(tok)
        return inst

    @staticmethod
    def _ap(x):
        if isinstance(x, TT):
            return x.h[:]
        return x.ap if isinstance(x, R) else x

    def dma(self, q, dst, src):
        reads = [src] if isinstance(src, R) else []
        writes = [dst] if isinstance(dst, R) else []
        self._gather(q, reads, writes)
        prev = self.last_dma.get(q)
        if prev is not None:
            self._wait(q, prev[0], prev[1])
        sem = self._sem("d_" + q)
        inst = self.eng[q].dma_start(out=self._ap(dst), in_=self._ap(src))
        self.cnt[sem] += 16
        self.last_dma[q] = (sem, self.cnt[sem])
        inst.then_inc(self.sems[sem], 16)
        tok = (sem, self.cnt[sem])
        for w in writes:
            (w if isinstance(w, TT) else w.tt).lw = tok
            (w if isinstance(w, TT) else w.tt).readers = []
        for r in reads:
            (r if isinstance(r, TT) else r.tt).readers.append(tok)
        return inst

    def mm(self, psum, pairs, start=True, stop=True):
        reads = []
        for l, r in pairs:
            reads += [l, r]
        self._gather("pe", reads, [psum])
        n = len(pairs)
        inst = None
        for i, (l, r) in enumerate(pairs):
            inst = self.eng["pe"].matmul(out=self._ap(psum), lhsT=self._ap(l), rhs=self._ap(r),
                                         start=(start and i == 0),
                                         stop=(stop and i == n - 1))
        sem = self._sem("s_pe")
        self.cnt[sem] += 1
        inst.then_inc(self.sems[sem], 1)
        tok = (sem, self.cnt[sem])
        (psum if isinstance(psum, TT) else psum.tt).lw = tok
        (psum if isinstance(psum, TT) else psum.tt).readers = []
        for r in reads:
            (r if isinstance(r, TT) else r.tt).readers.append(tok)

    def trans(self, psum, src, ident):
        self.emit("pe", lambda e: e.transpose(out=self._ap(psum), in_=self._ap(src), identity=self._ap(ident)),
                  reads=[src, ident], writes=[psum])

    def act(self, ename, dst, src, func, bias=0.0, scale=1.0, accum=None):
        reads = [src] + [x for x in (bias, scale) if isinstance(x, R)]
        writes = [dst] + ([accum] if accum is not None else [])
        kw = {}
        if accum is not None:
            kw["accum_out"] = self._ap(accum)
        self.emit(ename, lambda e: e.activation(out=self._ap(dst), in_=self._ap(src), func=func,
                                                bias=self._ap(bias), scale=self._ap(scale), **kw),
                  reads=reads, writes=writes)

    def copy(self, ename, dst, src):
        if ename == "act":
            self.emit(ename, lambda e: e.copy(out=self._ap(dst), in_=self._ap(src)), reads=[src], writes=[dst])
        else:
            self.emit(ename, lambda e: e.tensor_copy(out=self._ap(dst), in_=self._ap(src)), reads=[src], writes=[dst])

    def cast(self, dst, src, engines=("dve", "act", "pool")):
        e = engines[self._cast_rr % len(engines)]
        self._cast_rr += 1
        self.copy(e, dst, src)

    def tt_(self, ename, dst, a, b, op):
        self.emit(ename, lambda e: e.tensor_tensor(out=self._ap(dst), in0=self._ap(a), in1=self._ap(b), op=op),
                  reads=[a, b], writes=[dst])

    def ts_(self, ename, dst, a, s1, op0, s2=None, op1=None, accum=None):
        reads = [a] + [x for x in (s1, s2) if isinstance(x, R)]
        writes = [dst] + ([accum] if accum is not None else [])
        kw = {}
        if accum is not None:
            kw["accum_out"] = self._ap(accum)
        if op1 is None:
            self.emit(ename, lambda e: e.tensor_scalar(out=self._ap(dst), in0=self._ap(a), scalar1=self._ap(s1),
                                                       scalar2=None, op0=op0, **kw),
                      reads=reads, writes=writes)
        else:
            self.emit(ename, lambda e: e.tensor_scalar(out=self._ap(dst), in0=self._ap(a), scalar1=self._ap(s1),
                                                       scalar2=self._ap(s2), op0=op0, op1=op1, **kw),
                      reads=reads, writes=writes)

    def stt(self, ename, dst, a, s, b, op0, op1):
        reads = [a, b] + ([s] if isinstance(s, R) else [])
        self.emit(ename, lambda e: e.scalar_tensor_tensor(out=self._ap(dst), in0=self._ap(a), scalar=self._ap(s),
                                                          in1=self._ap(b), op0=op0, op1=op1),
                  reads=reads, writes=[dst])

    def red(self, dst, src, op, negate=False, absval=False):
        self.emit("dve", lambda e: e.tensor_reduce(out=self._ap(dst), in_=self._ap(src), axis=AX.X, op=op,
                                                   negate=negate, apply_absolute_value=absval),
                  reads=[src], writes=[dst])

    def memset(self, ename, dst, val):
        self.emit(ename, lambda e: e.memset(self._ap(dst), val), reads=[], writes=[dst])

    def recip(self, dst, src):
        def _b(e):
            with self.nc.allow_low_precision("bf16 reciprocal within validated tolerance"):
                return e.reciprocal(out=self._ap(dst), in_=self._ap(src))
        self.emit("dve", _b, reads=[src], writes=[dst])

    def track_dram(self, handle):
        return TT(self, handle)

    def constf(self, val, dtype=F32):
        key = (float(val), dtype)
        if key not in self._consts:
            t = self.sb("cst", [128, 1], dtype, persistent=True)
            self.memset("dve", t.full, float(val))
            self._consts[key] = t
        return self._consts[key].full

    def drain_all(self):
        for s, v in sorted(self.cnt.items()):
            if v > 0:
                self._wait("pool", s, v)



def ln_stats(kb, x, tiles, eps, junk, name="ln"):
    n = len(tiles)
    sums = kb.sbc(name + "_s", [128, n], F32)
    sq = kb.sbc(name + "_q", [128, n], F32)
    for j, t in enumerate(tiles):
        kb.red(sums[:, j:j + 1], x[:, t, :], ALU.add)
        kb.act("act", junk[:, 0:1024], x[:, t, :], AF.Square, accum=sq[:, j:j + 1])
    m = kb.sbc(name + "_m", [128, n], F32)
    var = kb.sbc(name + "_v", [128, n], F32)
    r = kb.sbc(name + "_r", [128, n], F32)
    b = kb.sbc(name + "_b", [128, n], F32)
    kb.ts_("dve", m.full, sums.full, 1.0 / D, ALU.mult)
    kb.tt_("dve", var.full, m.full, m.full, ALU.mult)
    kb.stt("dve", var.full, sq.full, 1.0 / D, var.full, ALU.mult, ALU.subtract)
    kb.act("act", var.full, var.full, AF.Sqrt, bias=kb.constf(eps))
    kb.recip(r.full, var.full)
    kb.stt("dve", b.full, m.full, -1.0, r.full, ALU.mult, ALU.mult)
    return r, b


def ln_transpose_fm(kb, dst, x, tiles, r, b, ident, tmp, ptile, scale_fm=None, dst_col0=0):
    for j, t in enumerate(tiles):
        kb.act("act", tmp.full, x[:, t, :], AF.Identity,
               bias=b[:, j:j + 1], scale=r[:, j:j + 1])
        for c in range(NC):
            kb.trans(ptile[:, 128 * c:128 * (c + 1)], tmp[:, 128 * c:128 * (c + 1)], ident)
        col = 128 * (dst_col0 + j)
        dsl = dst.h[:, :, col:col + 128]
        dR = R(dst, dsl)
        pview = ptile.h[:].rearrange("p (c n) -> p c n", c=NC)
        if scale_fm is not None:
            kb.emit("dve", lambda e: e.tensor_scalar(out=dsl, in0=pview, scalar1=kb._ap(scale_fm),
                                                     scalar2=None, op0=ALU.mult),
                    reads=[ptile, scale_fm], writes=[dR])
        else:
            kb.emit("dve", lambda e: e.tensor_copy(out=dsl, in_=pview),
                    reads=[ptile], writes=[dR])


def transpose_tm_fm(kb, dst, src, tiles, ident, psname="tp", dtype=BF16,
                    scale_fm=None, dst_f32_stage=None, chunks=range(NC)):
    grp = 4
    p = kb.ps(psname, [128, 128 * grp], dtype)
    for c in chunks:
        for g0 in range(0, len(tiles), grp):
            gt = tiles[g0:g0 + grp]
            for k, t in enumerate(gt):
                kb.trans(p[:, 128 * k:128 * (k + 1)], src[:, t, 128 * c:128 * (c + 1)], ident)
            dsl = dst[:, c, 128 * g0:128 * g0 + 128 * len(gt)]
            if scale_fm is not None:
                kb.ts_("dve", dsl, p[:, 0:128 * len(gt)], scale_fm[:, c:c + 1], ALU.mult)
            else:
                kb.copy("dve", dsl, p[:, 0:128 * len(gt)])


def load_weight_strip(kb, wdram, oc, kchunks, stage_f32, dst_bf16, wname):
    ap = wdram.rearrange("(c p) o -> p c o", p=128)[:, :, 128 * oc:128 * (oc + 1)]
    kb.dma("sp", stage_f32[:, 0:kchunks, :], ap)
    kb.cast(dst_bf16[:, 0:kchunks, :], stage_f32[:, 0:kchunks, :])




def epilogue_fm_group(kb, st, x, yfm, pt, tmp, junk, t0, ntile, ytm):
    for ts4 in range(ntile):
        for cg in range(2):
            for k in range(4):
                c = cg * 4 + k
                kb.trans(pt[:, 128 * k:128 * (k + 1)],
                         yfm[:, c, 128 * ts4:128 * (ts4 + 1)], st["identf"])
            kb.copy("dve", ytm[:, 0, 512 * cg:512 * (cg + 1)], pt[:, 0:512])
        r2, b2 = ln_stats(kb, ytm, [0], 1e-5, junk, name="lno")
        t = t0 + ts4
        kb.act("act", tmp.full, ytm[:, 0, :], AF.Identity,
               bias=b2[:, 0:1], scale=r2[:, 0:1])
        kb.tt_("dve", x[:, t, :], x[:, t, :], tmp.full, ALU.add)


def emit_ffn_ternary(kb, st, P):
    x = st["x"]
    identb = st["identb"]
    junk = st["junk"]
    TtT = 4
    Tcols = 128 * TtT
    HCH = 32
    nc = kb.nc

    caches = {}
    for wname, kch, noc in (("w1T", NC, HCH), ("w2T", NC, HCH), ("h1T", HCH, HCH),
                            ("h2T", HCH, HCH), ("w3T", HCH, NC)):
        h = nc.dram_tensor(kb.uid("wc_" + wname), [noc, 128, kch, 128], BF16,
                           kind="Internal")
        caches[wname] = (kb.track_dram(h), kch, noc)

    with kb.phase():
        s1 = kb.sb("s1", [128, HCH], F32); kb.dma("sp", s1.full, P["s1r"])
        s2 = kb.sb("s2", [128, HCH], F32); kb.dma("sp", s2.full, P["s2r"])
        sh1 = kb.sb("sh1", [128, HCH], F32); kb.dma("sp", sh1.full, P["sh1r"])
        sh2 = kb.sb("sh2", [128, HCH], F32); kb.dma("sp", sh2.full, P["sh2r"])
        s3 = kb.sb("s3", [128, NC], F32); kb.dma("sp", s3.full, P["s3r"])

        xn_fm = kb.sb("xn_fm", [128, NC, S], BF16)
        tmp = kb.sb("lntmp", [128, D], BF16)
        ptile = kb.ps("ptile", [128, D], BF16)
        r, b = ln_stats(kb, x, list(range(NT)), 1e-5, junk)
        ln_transpose_fm(kb, xn_fm, x, list(range(NT)), r, b, identb, tmp, ptile)

        pp = [kb.ps("pmm0", [128, Tcols]), kb.ps("pmm1", [128, Tcols])]
        pt = kb.ps("ptr", [128, 512], F32)
        wbufs = [kb.sb("wbuf", [128, HCH, 128], BF16) for _ in range(2)]
        stage0 = kb.sb("wstg", [128, NC, 128], F32)
        g_a = kb.sb("g_a", [128, HCH, Tcols], BF16)
        g_b = kb.sb("g_b", [128, HCH, Tcols], BF16)
        a_t = kb.sb("a_t", [128, Tcols], BF16)
        yfm = kb.sb("yfm", [128, NC, 512], F32)
        ytm = kb.sb("ytm", [128, 1, D], F32)
        sctr = [0]
        queues = ("sp", "act")

        def wstrip(wname, oc, first):
            wc, kch, noc = caches[wname]
            i = sctr[0] % 2
            q = queues[i]
            sctr[0] += 1
            wb = wbufs[i]
            stg = stage0
            if first:
                ap = P[wname].rearrange("(c p) o -> p c o", p=128)
                for q0 in range(0, kch, NC):
                    kb.dma(q, R(stg, stg.h[:, 0:NC, :]),
                           ap[:, q0:q0 + NC, 128 * oc:128 * (oc + 1)])
                    kb.cast(R(wb, wb.h[:, q0:q0 + NC, :]), R(stg, stg.h[:, 0:NC, :]),
                            engines=("dve", "act"))
                kb.dma("pool", R(wc, wc.h[oc, :, :, :]), R(wb, wb.h[:, 0:kch, :]))
            else:
                kb.dma(q, R(wb, wb.h[:, 0:kch, :]), R(wc, wc.h[oc, :, :, :]))
            return wb

        for tt in range(S // Tcols):
            first = tt == 0
            tok = slice(Tcols * tt, Tcols * (tt + 1))
            for oc in range(HCH):
                wb = wstrip("w1T", oc, first)
                p = pp[0]
                kb.mm(p.full, [(R(wb, wb.h[:, c, :]), xn_fm[:, c, tok]) for c in range(NC)])
                kb.act("act", a_t.full, p.full, AF.Silu, scale=s1[:, oc:oc + 1])
                wb = wstrip("w2T", oc, first)
                p2 = pp[1]
                kb.mm(p2.full, [(R(wb, wb.h[:, c, :]), xn_fm[:, c, tok]) for c in range(NC)])
                kb.stt("dve", g_a[:, oc, :], p2.full, s2[:, oc:oc + 1], a_t.full,
                       ALU.mult, ALU.mult)
            for oc in range(HCH):
                wb = wstrip("h1T", oc, first)
                p = pp[oc % 2]
                kb.mm(p.full, [(R(wb, wb.h[:, c, :]), g_a[:, c, :]) for c in range(HCH)])
                kb.act("act", g_b[:, oc, :], p.full, AF.Gelu, scale=sh1[:, oc:oc + 1])
            for oc in range(HCH):
                wb = wstrip("h2T", oc, first)
                p = pp[oc % 2]
                kb.mm(p.full, [(R(wb, wb.h[:, c, :]), g_b[:, c, :]) for c in range(HCH)])
                kb.act("act", g_a[:, oc, :], p.full, AF.Gelu, scale=sh2[:, oc:oc + 1])
            for oc in range(NC):
                wb = wstrip("w3T", oc, first)
                p = pp[oc % 2]
                kb.mm(p.full, [(R(wb, wb.h[:, c, :]), g_a[:, c, :]) for c in range(HCH)])
                kb.ts_("dve", yfm[:, oc, :], p.full, s3[:, oc:oc + 1], ALU.mult)
            epilogue_fm_group(kb, st, x, yfm, pt, tmp, junk, tt * TtT, TtT, ytm)



def emit_attn_causal(kb, st, P, rope=False, bias=False):
    x = st["x"]
    identb = st["identb"]
    junk = st["junk"]
    H, DH = 16, 64
    SCL = 0.125

    with kb.phase():
        if bias:
            bo = kb.sb("bo", [128, NC], F32)
            kb.dma("sp", bo.full, P["bo_r"])
        Q = kb.sb("Qfm", [128, NC, S], BF16)
        Kt = kb.sb("Kfm", [128, NC, S], BF16)
        V = kb.sb("Vaug", [128, NT, H, DH], BF16)

        with kb.phase():
            xn_fm = kb.sb("xn_fm", [128, NC, S], BF16)
            tmp = kb.sb("lntmp", [128, D], BF16)
            ptile = kb.ps("ptile", [128, D], BF16)
            r, b = ln_stats(kb, x, list(range(NT)), 1e-5, junk)
            ln_transpose_fm(kb, xn_fm, x, list(range(NT)), r, b, identb, tmp, ptile)

            stage = kb.sb("wstg", [128, 4, 128], F32)
            wbuf = kb.sb("wbuf", [128, NC, 128], BF16)
            pq = kb.ps("pq", [128, 512])
            wap = P["wqkvT"].rearrange("(c p) o -> p c o", p=128)

            wbufs2 = [wbuf, kb.sb("wbufB", [128, NC, 128], BF16)]
            pqs = [pq, kb.ps("pqB", [128, 512])]
            pctr = [0]

            def pstrip(ocol):
                i = pctr[0] % 2
                pctr[0] += 1
                q = ("sp", "act")[i]
                wb = wbufs2[i]
                for hh in range(2):
                    kb.dma(q, R(stage, stage.h[:, :, :]),
                           wap[:, 4 * hh:4 * (hh + 1), 128 * ocol:128 * (ocol + 1)])
                    kb.cast(R(wb, wb.h[:, 4 * hh:4 * (hh + 1), :]),
                            R(stage, stage.h[:, :, :]), engines=("dve", "pool"))
                return wb, pqs[i]

            def proj_fm(dst, ocol0, scl):
                for oc in range(NC):
                    wb, p = pstrip(ocol0 + oc)
                    for qc in range(4):
                        kb.mm(p.full, [(R(wb, wb.h[:, c, :]),
                                        xn_fm[:, c, 512 * qc:512 * (qc + 1)])
                                       for c in range(NC)])
                        kb.act("act", dst[:, oc, 512 * qc:512 * (qc + 1)], p.full,
                               AF.Identity, scale=scl)

            proj_fm(Q, 0, 0.125)
            proj_fm(Kt, 8, 1.0)
            for oc in range(NC):
                wb, p = pstrip(16 + oc)
                for kt in range(NT):
                    pv = R(p, p.h[:, 0:128])
                    kb.mm(pv, [(xn_fm[:, c, 128 * kt:128 * (kt + 1)],
                                R(wb, wb.h[:, c, :])) for c in range(NC)])
                    kb.copy("dve", V[:, kt, 2 * oc:2 * oc + 2, 0:DH],
                            R(p, p.h[:, 0:128].rearrange("p (h d) -> p h d", h=2)))

        if rope:
            with kb.phase():
                cosfm = kb.sb("cosfm", [128, S], BF16)
                sinfm = kb.sb("sinfm", [128, S], BF16)
                rt = kb.sb("rt", [128, 64], BF16)
                kb.dma("sp", cosfm.full, P["cos_fm"])
                kb.dma("sp", sinfm.full, P["sin_fm"])
                kb.dma("sp", rt.full, P["rt"])
                prs = [kb.ps("pr", [128, 512]) for _ in range(2)]
                rss = [kb.sb("ropes", [128, 512], BF16) for _ in range(2)]
                rcs = [kb.sb("ropec", [128, 512], BF16) for _ in range(2)]
                rctr = 0
                for dst in (Q, Kt):
                    for h in range(H):
                        ch, off = h // 2, 64 * (h % 2)
                        sl = slice(off, off + 64)
                        for qc in range(4):
                            pr, rs, rc = prs[rctr % 2], rss[rctr % 2], rcs[rctr % 2]
                            rctr += 1
                            cols = slice(512 * qc, 512 * (qc + 1))
                            src = R(dst, dst.h[sl, ch, cols])
                            kb.mm(R(pr, pr.h[sl, :]), [(R(rt, rt.h[sl, :]), src)])
                            kb.tt_("dve", R(rs, rs.h[sl, :]), R(pr, pr.h[sl, :]),
                                   R(sinfm, sinfm.h[sl, cols]), ALU.mult)
                            kb.tt_("pool", R(rc, rc.h[sl, :]), src,
                                   R(cosfm, cosfm.h[sl, cols]), ALU.mult)
                            kb.tt_("dve", src, R(rc, rc.h[sl, :]), R(rs, rs.h[sl, :]), ALU.add)

        mask01 = kb.sb("mask01", [128, 896], BF16)
        kb.dma("sp", mask01.full, P["mask01"])
        wo = kb.sb("wo", [128, NC, D], BF16)
        with kb.phase():
            wstg2 = kb.sb("wstg2", [128, 4, 128], F32)
            woap = P["woT"].rearrange("(c p) o -> p c o", p=128)
            for oc in range(NC):
                for hh in range(2):
                    kb.dma("sp", wstg2.full,
                           woap[:, 4 * hh:4 * (hh + 1), 128 * oc:128 * (oc + 1)])
                    kb.cast(wo[:, 4 * hh:4 * (hh + 1), 128 * oc:128 * (oc + 1)], wstg2.full)
        NB = 3
        ctxq = kb.sb("ctxq", [128, NC, 512], BF16)
        pbufs = [kb.sb("pbuf", [128, 512], BF16) for _ in range(2)]
        vscs = [kb.sb("vsc", [128, DH + 1], BF16) for _ in range(2)]
        mus = [kb.sb("mu", [128, 1], F32) for _ in range(2)]
        rvs = [kb.sb("rv", [128, 1], F32) for _ in range(2)]
        dens = kb.sb("dens", [1, 512], BF16)
        recb = kb.sb("recb", [64, 512], BF16)
        onesf = kb.sb("onesf", [1, 64], BF16)
        kb.memset("dve", onesf.full, 1.0)
        yfm = kb.sb("yfm_a", [128, NC, 256], F32)
        tmp2 = junk
        ytm = kb.sb("ytm_a", [128, 1, D], F32)

        psts = [kb.ps("pst", [128, 512]) for _ in range(NB)]
        pcxs = [kb.ps("pcx", [65, 512]) for _ in range(2)]
        pbc = kb.ps("pbc", [64, 512])
        pyo = kb.ps("pyo", [128, 512])
        pt = kb.ps("pte", [128, 512], F32)

        for qc in range(4):
            qcols = slice(512 * qc, 512 * (qc + 1))
            for h in range(H):
                ch, off = h // 2, 64 * (h % 2)
                pcx = pcxs[h % 2]
                nkt = 4 * qc + 4
                for kt in range(nkt):
                    pst = psts[kt % NB]
                    i = kt % 2
                    pbuf, vsc, mu, rv = pbufs[i], vscs[i], mus[i], rvs[i]
                    kb.mm(pst.full, [(R(Kt, Kt.h[off:off + 64, ch, 128 * kt:128 * (kt + 1)]),
                                      R(Q, Q.h[off:off + 64, ch, qcols]))])
                    kb.red(mu.full, pst.full, ALU.max, negate=True)
                    kb.act("act", pbuf.full, pst.full, AF.Exp, bias=mu.full)
                    kb.act("act", rv.full, mu.full, AF.Exp, scale=-1.0)
                    j = kt - 4 * qc
                    if j >= 0:
                        kb.tt_("dve", pbuf.full, pbuf.full,
                               R(mask01, mask01.h[:, 384 - 128 * j:896 - 128 * j]), ALU.mult)
                    kb.act("act", R(vsc, vsc.h[:, 0:DH]), V[:, kt, h, :],
                           AF.Identity, scale=rv.full)
                    kb.act("act", R(vsc, vsc.h[:, DH:DH + 1]), rv.full, AF.Identity)
                    kb.mm(pcx.full, [(vsc.full, pbuf.full)],
                          start=(kt == 0), stop=(kt == nkt - 1))
                kb.copy("dve", dens.full, R(pcx, pcx.h[64:65, :]))
                kb.act("act", dens.full, dens.full, AF.Ln)
                kb.act("act", dens.full, dens.full, AF.Exp, scale=-1.0)
                kb.mm(pbc.full, [(onesf.full, dens.full)])
                kb.copy("dve", recb.full, pbc.full)
                kb.tt_("dve", R(ctxq, ctxq.h[off:off + 64, ch, :]),
                       R(pcx, pcx.h[0:64, :]), recb.full, ALU.mult)
            for tg in range(2):
                for oc in range(NC):
                    kb.mm(R(pyo, pyo.h[:, 0:256]),
                          [(R(wo, wo.h[:, c, 128 * oc:128 * (oc + 1)]),
                            ctxq[:, c, 256 * tg:256 * (tg + 1)]) for c in range(NC)])
                    if bias:
                        kb.ts_("dve", yfm[:, oc, :], R(pyo, pyo.h[:, 0:256]),
                               bo[:, oc:oc + 1], ALU.add)
                    else:
                        kb.copy("dve", yfm[:, oc, :], R(pyo, pyo.h[:, 0:256]))
                epilogue_fm_group(kb, st, x, yfm, pt, tmp2, junk, 4 * qc + 2 * tg, 2, ytm)





def emit_moe(kb, st, P):
    nc = kb.nc
    x = st["x"]
    junk = st["junk"]
    identf = st["identf"]
    E = 8
    HCH = 16

    with kb.phase():
        scale_fm = kb.sb("mscale", [128, NC], F32)
        kb.dma("sp", scale_fm.full, P["scale_r"])
        ctr_tm = kb.sb("ctr_tm", [8, D], F32)
        kb.dma("sp", ctr_tm.full, P["ctr"])
        ctr_fm = kb.sb("ctr_fm", [128, NC, E], F32)
        kb.dma("sp", ctr_fm.full, P["ctr_fm"])
        ny = kb.sb("ny", [8, 1], F32)
        kb.act("act", junk[0:8, 0:D], ctr_tm.full, AF.Square, accum=ny.full)
        kb.ts_("dve", ny.full, ny.full, -0.5, ALU.mult)
        nyr = kb.sb("nyr", [1, 8], F32)
        onesc = kb.sb("onesc", [1, 128], F32)
        kb.memset("dve", onesc.full, 1.0)

        rL, bL = ln_stats(kb, x, list(range(NT)), 1e-5, junk, name="lnm")

        xn_fm = kb.sb("mxn_fm", [128, NC, S], BF16)
        masks = kb.sb("masks", [128, E, NT], F32)
        tmpa = kb.sb("tmpa", [128, D], F32)
        tmpb = kb.sb("tmpb", [128, D], F32)
        xnf = kb.sb("xnf", [128, NC, 128], F32)
        rmsq = kb.sb("rmsq", [128, 1], F32)
        rstd = kb.sb("rstd", [128, 1], F32)
        mx = kb.sb("mx", [128, 1], F32)
        ge = kb.sb("ge", [128, 1], F32)
        avail = kb.sb("avail", [128, 1], F32)

        with kb.phase():
            ptr = kb.ps("ptr_m", [128, D], F32)
            psc = kb.ps("psc", [128, 8], F32)
            pny = kb.ps("pny", [1, 8], F32)
            kb.trans(pny.full, ny.full, R(identf, identf.h[0:8, 0:8]))
            kb.copy("dve", nyr.full, pny.full)
            for t in range(NT):
                kb.act("act", tmpa.full, x[:, t, :], AF.Identity,
                       bias=bL[:, t:t + 1], scale=rL[:, t:t + 1])
                kb.act("act", junk[:, 0:D], tmpa.full, AF.Square, accum=rmsq.full)
                kb.ts_("dve", rmsq.full, rmsq.full, 1.0 / D, ALU.mult)
                kb.act("act", rmsq.full, rmsq.full, AF.Sqrt, bias=kb.constf(1e-6))
                kb.recip(rstd.full, rmsq.full)
                kb.act("act", tmpb.full, tmpa.full, AF.Identity, scale=rstd.full)
                for c in range(NC):
                    kb.trans(R(ptr, ptr.h[:, 128 * c:128 * (c + 1)]),
                             tmpb[:, 128 * c:128 * (c + 1)], identf)
                for c in range(NC):
                    kb.ts_("dve", xnf[:, c, :], R(ptr, ptr.h[:, 128 * c:128 * (c + 1)]),
                           scale_fm[:, c:c + 1], ALU.mult)
                kb.cast(R(xn_fm, xn_fm.h[:, :, 128 * t:128 * (t + 1)]), xnf.full,
                        engines=("dve",))
                kb.mm(psc.full, [(onesc.full, nyr.full)], stop=False)
                kb.mm(psc.full, [(xnf[:, c, :], ctr_fm[:, c, :]) for c in range(NC)],
                      start=False)
                kb.red(mx.full, psc.full, ALU.max)
                kb.memset("dve", avail.full, 1.0)
                for e in range(E):
                    kb.ts_("dve", ge.full, R(psc, psc.h[:, e:e + 1]), mx.full, ALU.is_ge)
                    kb.tt_("dve", R(masks, masks.h[:, e, t:t + 1]), ge.full, avail.full,
                           ALU.mult)
                    kb.tt_("dve", avail.full, avail.full,
                           R(masks, masks.h[:, e, t:t + 1]), ALU.subtract)

        aa = kb.sb("aa", [128, HCH, 1024], BF16)
        y_acc = kb.sb("y_acc", [128, NT // 2, D], F32)
        stage = kb.sb("mstg", [128, HCH, 128], F32)
        wbufs = [kb.sb("mwbuf", [128, HCH, 128], BF16) for _ in range(2)]
        sg = kb.sb("sg", [128, 512], BF16)
        upap = P["up_w"].rearrange("e (c p) o -> p e c o", p=128)
        dnap = P["down_w"].rearrange("e (c p) o -> p e c o", p=128)
        pmm = kb.ps("pmm_m", [128, 512])
        pmt = kb.ps("pmt_m", [128, 512], F32)
        upc = kb.track_dram(nc.dram_tensor(kb.uid("wc_up"), [E, 2 * HCH, 128, NC, 128],
                                           BF16, kind="Internal"))
        dnc = kb.track_dram(nc.dram_tensor(kb.uid("wc_dn"), [E, NC, 128, HCH, 128],
                                           BF16, kind="Internal"))
        mctr = [0]
        mqueues = ("sp", "act")

        def mstrip(which, e, oc, kch, first):
            i = mctr[0] % 2
            q = mqueues[i]
            mctr[0] += 1
            wb = wbufs[i]
            wc = upc if which == 0 else dnc
            src = upap if which == 0 else dnap
            if first:
                kb.dma(q, R(stage, stage.h[:, 0:kch, :]),
                       src[:, e, :, 128 * oc:128 * (oc + 1)])
                kb.cast(R(wb, wb.h[:, 0:kch, :]), R(stage, stage.h[:, 0:kch, :]),
                        engines=("dve", "act"))
                kb.dma("pool", R(wc, wc.h[e, oc, :, :, :]), R(wb, wb.h[:, 0:kch, :]))
            else:
                kb.dma(q, R(wb, wb.h[:, 0:kch, :]), R(wc, wc.h[e, oc, :, :, :]))
            return wb

        for half in range(2):
            first = half == 0
            tok0 = 1024 * half
            t0 = NT // 2 * half
            for tloc in range(NT // 2):
                kb.memset("dve", y_acc[:, tloc, :], 0.0)
            for e in range(E):
                for oc in range(2 * HCH):
                    wb = mstrip(0, e, oc, NC, first)
                    for tq in range(2):
                        cols = slice(tok0 + 512 * tq, tok0 + 512 * (tq + 1))
                        lcols = slice(512 * tq, 512 * (tq + 1))
                        kb.mm(pmm.full, [(R(wb, wb.h[:, c, :]), xn_fm[:, c, cols])
                                         for c in range(NC)])
                        if oc < HCH:
                            kb.act("act", R(aa, aa.h[:, oc, lcols]), pmm.full, AF.Identity)
                        else:
                            kb.act("act", sg.full, pmm.full, AF.Silu)
                            kb.tt_("dve", R(aa, aa.h[:, oc - HCH, lcols]),
                                   R(aa, aa.h[:, oc - HCH, lcols]), sg.full, ALU.mult)
                for od in range(NC):
                    wb = mstrip(1, e, od, HCH, first)
                    for tq in range(2):
                        lcols = slice(512 * tq, 512 * (tq + 1))
                        kb.mm(pmm.full, [(R(wb, wb.h[:, c, :]), R(aa, aa.h[:, c, lcols]))
                                         for c in range(HCH)])
                        kb.copy("dve", tmpa[:, 0:512], pmm.full)
                        for blk in range(4):
                            kb.trans(R(pmt, pmt.h[:, 128 * blk:128 * (blk + 1)]),
                                     tmpa[:, 128 * blk:128 * (blk + 1)], identf)
                        for blk in range(4):
                            tloc = 4 * tq + blk
                            tglob = t0 + tloc
                            kb.stt("dve", R(y_acc, y_acc.h[:, tloc, 128 * od:128 * (od + 1)]),
                                   R(pmt, pmt.h[:, 128 * blk:128 * (blk + 1)]),
                                   R(masks, masks.h[:, e, tglob:tglob + 1]),
                                   R(y_acc, y_acc.h[:, tloc, 128 * od:128 * (od + 1)]),
                                   ALU.mult, ALU.add)
            for tloc in range(NT // 2):
                t = t0 + tloc
                kb.act("act", tmpa.full, x[:, t, :], AF.Identity,
                       bias=bL[:, t:t + 1], scale=rL[:, t:t + 1])
                kb.tt_("dve", tmpb.full, R(y_acc, y_acc.h[:, tloc, :]), tmpa.full, ALU.add)
                tb = kb.sbc("moey", [128, 1, D], F32)
                kb.copy("dve", tb[:, 0, :], tmpb.full)
                r2, b2 = ln_stats(kb, tb, [0], 1e-5, junk, name="lnmo")
                kb.act("act", tmpa.full, tb[:, 0, :], AF.Identity,
                       bias=b2[:, 0:1], scale=r2[:, 0:1])
                kb.tt_("dve", x[:, t, :], x[:, t, :], tmpa.full, ALU.add)




def emit_attn_l1(kb, st, P):
    x = st["x"]
    identb = st["identb"]
    junk = st["junk"]
    H, DH = 16, 64
    SCL = 0.125

    with kb.phase():
        bo = kb.sb("bo1", [128, NC], F32)
        kb.dma("sp", bo.full, P["bo_r"])
        Q = kb.sb("Qfm1", [128, NC, S], BF16)
        Kt = kb.sb("Kfm1", [128, NC, S], BF16)
        V = kb.sb("Vaug1", [128, NT, H, DH], BF16)
        kg = kb.sb("kg", [128, NC, NT, 1], BF16)
        Vg = kb.sb("Vg", [16, H, DH + 1], BF16)
        kb.memset("dve", Vg[:, :, DH:DH + 1], 1.0)

        with kb.phase():
            xn_fm = kb.sb("xn_fm1", [128, NC, S], BF16)
            tmp = kb.sb("lntmp1", [128, D], BF16)
            ptile = kb.ps("ptile1", [128, D], BF16)
            r, b = ln_stats(kb, x, list(range(NT)), 1e-5, junk)
            ln_transpose_fm(kb, xn_fm, x, list(range(NT)), r, b, identb, tmp, ptile)
            xn_g = kb.sb("xn_g", [128, NC, NT, 1], BF16)
            kb.copy("dve", xn_g.full,
                    R(xn_fm, xn_fm.h[:].rearrange("p c (g s) -> p c g s", s=128)[:, :, :, 0:1]))

            stage = kb.sb("wstg1", [128, 4, 128], F32)
            wbuf = kb.sb("wbuf1", [128, NC, 128], BF16)
            pq = kb.ps("pq1", [128, 512])
            wap = P["wqkvT"].rearrange("(c p) o -> p c o", p=128)

            def ldstrip(ocol):
                for hh in range(2):
                    kb.dma("sp", stage.full,
                           wap[:, 4 * hh:4 * (hh + 1), 128 * ocol:128 * (ocol + 1)])
                    kb.cast(R(wbuf, wbuf.h[:, 4 * hh:4 * (hh + 1), :]), stage.full)

            def proj_fm(dst, ocol0, scl=1.0):
                for oc in range(NC):
                    ldstrip(ocol0 + oc)
                    for qc in range(4):
                        kb.mm(pq.full, [(wbuf[:, c, :], xn_fm[:, c, 512 * qc:512 * (qc + 1)])
                                        for c in range(NC)])
                        kb.act("act", dst[:, oc, 512 * qc:512 * (qc + 1)], pq.full,
                               AF.Identity, scale=scl)

            proj_fm(Q, 0, 0.125)
            proj_fm(Kt, 8)
            kb.copy("dve", kg.full,
                    R(Kt, Kt.h[:].rearrange("p c (g s) -> p c g s", s=128)[:, :, :, 0:1]))
            for oc in range(NC):
                ldstrip(16 + oc)
                for kt in range(NT):
                    pv = R(pq, pq.h[:, 0:128])
                    kb.mm(pv, [(xn_fm[:, c, 128 * kt:128 * (kt + 1)], wbuf[:, c, :])
                               for c in range(NC)])
                    kb.copy("dve", V[:, kt, 2 * oc:2 * oc + 2, 0:DH],
                            R(pq, pq.h[:, 0:128].rearrange("p (h d) -> p h d", h=2)))
                pvg = R(pq, pq.h[0:16, 0:128])
                kb.mm(pvg, [(R(xn_g, xn_g.h[:, c, :, 0]), wbuf[:, c, :]) for c in range(NC)])
                kb.copy("dve", R(Vg, Vg.h[0:16, 2 * oc:2 * oc + 2, 0:DH]),
                        R(pq, pq.h[0:16, 0:128].rearrange("p (h d) -> p h d", h=2)))

        maskloc = kb.sb("maskloc", [128, 2, 128], BF16)
        kb.dma("sp", maskloc.full, P["maskloc"])
        wo = kb.sb("wo1", [128, NC, D], BF16)
        with kb.phase():
            wstg2 = kb.sb("wstg21", [128, NC, 128], F32)
            woap = P["woT"].rearrange("(c p) o -> p c o", p=128)
            for oc in range(NC):
                kb.dma("sp", wstg2.full, woap[:, :, 128 * oc:128 * (oc + 1)])
                kb.cast(wo[:, :, 128 * oc:128 * (oc + 1)], wstg2.full)
        NB = 3
        ctxq = kb.sb("ctxq1", [128, NC, 512], BF16)
        pbufs = [kb.sb("pbuf1", [128, 512], BF16) for _ in range(NB)]
        vscs = [kb.sb("vsc1", [128, DH + 1], BF16) for _ in range(NB)]
        vscg = kb.sb("vscg", [16, DH + 1], BF16)
        mus = [kb.sb("mu1", [128, 1], F32) for _ in range(NB)]
        rvs = [kb.sb("rv1", [128, 1], F32) for _ in range(NB)]
        dens = kb.sb("dens1", [1, 512], BF16)
        recb = kb.sb("recb1", [64, 512], BF16)
        onesf = kb.sb("onesf1", [1, 64], BF16)
        kb.memset("dve", onesf.full, 1.0)
        tmpg = pbufs[1]
        yfm = kb.sb("yfm_1", [128, NC, 256], F32)
        tmp2 = junk
        ytm = kb.sb("ytm_1", [128, 1, D], F32)

        psts = [kb.ps("pst1", [128, 512]) for _ in range(NB)]
        pcx = kb.ps("pcx1", [65, 512])
        pcg = kb.ps("pcg1", [65, 512])
        pbc = kb.ps("pbc1", [64, 512])
        pyo = kb.ps("pyo1", [128, 512])
        pt = kb.ps("pte1", [128, 512], F32)

        def normalize(pnum, dst, add, off=0):
            kb.copy("dve", dens.full, R(pnum, pnum.h[64:65, :]))
            kb.act("act", dens.full, dens.full, AF.Ln)
            kb.act("act", dens.full, dens.full, AF.Exp, scale=-1.0)
            kb.mm(pbc.full, [(onesf.full, dens.full)])
            kb.copy("dve", recb.full, pbc.full)
            if add:
                kb.tt_("dve", R(tmpg, tmpg.h[off:off + 64, :]), R(pnum, pnum.h[0:64, :]),
                       recb.full, ALU.mult)
            else:
                kb.tt_("dve", dst, R(pnum, pnum.h[0:64, :]), recb.full, ALU.mult)

        bctr = [0]
        for qc in range(4):
            qcols = slice(512 * qc, 512 * (qc + 1))
            for h in range(H):
                ch, off = h // 2, 64 * (h % 2)
                for j in range(4):
                    qt = 4 * qc + j
                    kts = [kt for kt in (qt - 1, qt) if kt >= 0]
                    for i, kt in enumerate(kts):
                        bi = bctr[0] % NB
                        bctr[0] += 1
                        pst, pbuf, vsc, mu, rv = psts[bi], pbufs[bi], vscs[bi], mus[bi], rvs[bi]
                        pslice = R(pst, pst.h[:, 0:128])
                        kb.mm(pslice, [(R(Kt, Kt.h[off:off + 64, ch, 128 * kt:128 * (kt + 1)]),
                                        R(Q, Q.h[off:off + 64, ch, 128 * qt:128 * (qt + 1)]))])
                        kb.red(mu.full, pslice, ALU.max, negate=True)
                        kb.act("act", R(pbuf, pbuf.h[:, 0:128]), pslice, AF.Exp,
                               bias=mu.full)
                        mj = 0 if kt < qt else 1
                        kb.tt_("dve", R(pbuf, pbuf.h[:, 0:128]), R(pbuf, pbuf.h[:, 0:128]),
                               maskloc[:, mj, :], ALU.mult)
                        kb.act("act", rv.full, mu.full, AF.Exp, scale=-1.0)
                        kb.act("act", R(vsc, vsc.h[:, 0:DH]), V[:, kt, h, :],
                               AF.Identity, scale=rv.full)
                        kb.act("act", R(vsc, vsc.h[:, DH:DH + 1]), rv.full, AF.Identity)
                        kb.mm(R(pcx, pcx.h[:, 128 * j:128 * (j + 1)]),
                              [(vsc.full, R(pbuf, pbuf.h[:, 0:128]))],
                              start=(i == 0), stop=(i == len(kts) - 1))
                normalize(pcx, R(ctxq, ctxq.h[off:off + 64, ch, :]), add=False)
                pst, pbuf, mu, rv = psts[0], pbufs[0], mus[0], rvs[0]
                pg = R(pst, pst.h[0:16, 0:512])
                kb.mm(pg, [(R(kg, kg.h[off:off + 64, ch, :, 0]),
                            R(Q, Q.h[off:off + 64, ch, qcols]))])
                kb.red(R(mu, mu.h[0:16, :]), pg, ALU.max, negate=True)
                kb.act("act", R(pbuf, pbuf.h[0:16, :]), pg, AF.Exp,
                       bias=R(mu, mu.h[0:16, :]))
                kb.act("act", R(rv, rv.h[0:16, :]), R(mu, mu.h[0:16, :]), AF.Exp, scale=-1.0)
                kb.act("act", vscg.full, R(Vg, Vg.h[0:16, h, :]), AF.Identity,
                       scale=R(rv, rv.h[0:16, :]))
                kb.mm(pcg.full, [(vscg.full, R(pbuf, pbuf.h[0:16, :]))])
                normalize(pcg, None, add=True, off=off)
                kb.tt_("dve", R(ctxq, ctxq.h[off:off + 64, ch, :]),
                       R(ctxq, ctxq.h[off:off + 64, ch, :]),
                       R(tmpg, tmpg.h[off:off + 64, :]), ALU.add)
            for tg in range(2):
                for oc in range(NC):
                    kb.mm(R(pyo, pyo.h[:, 0:256]),
                          [(R(wo, wo.h[:, c, 128 * oc:128 * (oc + 1)]),
                            ctxq[:, c, 256 * tg:256 * (tg + 1)]) for c in range(NC)])
                    kb.ts_("dve", yfm[:, oc, :], R(pyo, pyo.h[:, 0:256]),
                           bo[:, oc:oc + 1], ALU.add)
                epilogue_fm_group(kb, st, x, yfm, pt, tmp2, junk, 4 * qc + 2 * tg, 2, ytm)




def emit_attn_yat(kb, st, P):
    x = st["x"]
    identb = st["identb"]
    identf = st["identf"]
    junk = st["junk"]
    H, DH = 16, 64
    LNC3 = float(np.log(np.sqrt(1024.0) / np.log(1025.0)))

    with kb.phase():
        q_fm = kb.sb("q3", [128, NC, S], BF16)
        k_fm = kb.sb("k3", [128, NC, S], BF16)
        V_tm = kb.sb("v3", [128, NT, D], BF16)
        ones1 = kb.sb("ones1", [1, 128], F32)
        kb.memset("dve", ones1.full, 1.0)
        ones64 = kb.sb("ones64", [128, 1], F32)
        kb.memset("dve", ones64.full, 1.0)
        onesb = kb.sb("onesb1", [1, 128], BF16)
        kb.memset("dve", onesb.full, 1.0)
        bias_map = {}
        for nm in ("bq", "bk", "bv", "bo"):
            t_ = kb.sb(nm + "3", [128, NC], F32)
            kb.dma("sp", t_.full, P[nm + "_r"])
            bias_map[nm] = t_
        scb = {}
        pbx = None
        for nm in ("aq", "ak", "av"):
            a1 = kb.sb(nm, [1, 1], F32)
            kb.dma("sp", a1.full, P[nm])
            kb.act("act", a1.full, a1.full, AF.Exp, scale=LNC3)
            sb_ = kb.sb(nm + "b", [128, 1], F32)
            scb[nm] = (a1, sb_)

        L3DBGP = int(os.environ.get("L3DBG", "0"))
        with kb.phase():
            xn_fm = kb.sb("xn3", [128, NC, S], BF16)
            nxb_all = kb.sb("nxb3", [128, NT, 128], BF16)
            nx = kb.sb("nx3", [128, NT], F32)
            with kb.phase():
                pb1 = kb.ps("pb13", [128, 128])
                for nm in ("aq", "ak", "av"):
                    a1, sb_ = scb[nm]
                    kb.mm(R(pb1, pb1.h[:, 0:1]), [(ones1.full, a1.full)])
                    kb.copy("dve", sb_.full, R(pb1, pb1.h[:, 0:1]))
                tmp = kb.sb("lntmp3", [128, D], BF16)
                ptile = kb.ps("ptile3", [128, D], BF16)
                r, b = ln_stats(kb, x, list(range(NT)), 1e-5, junk)
                for t in range(NT):
                    kb.act("act", tmp.full, x[:, t, :], AF.Identity,
                           bias=b[:, t:t + 1], scale=r[:, t:t + 1])
                    kb.act("act", junk[:, 0:D], tmp.full, AF.Square, accum=nx[:, t:t + 1])
                ln_transpose_fm(kb, xn_fm, x, list(range(NT)), r, b, identb, tmp, ptile)
                nxrow = kb.sb("nxrow", [1, 128], F32)
                pnx = kb.ps("pnx3", [1, 128])
                for t in range(NT if L3DBGP < 4 else 0):
                    kb.trans(pnx.full, nx[:, t:t + 1], identf)
                    kb.copy("dve", nxrow.full, pnx.full)
                    kb.mm(pb1.full, [(ones1.full, nxrow.full)])
                    kb.copy("dve", R(nxb_all, nxb_all.h[:, t, :]), pb1.full)
                if L3DBGP >= 4:
                    kb.memset("dve", nxb_all.full, 1.0)
            ptile = kb.ps("ptile3b", [128, D], BF16)

            stage = kb.sb("wstg3", [128, 2, 128], F32)
            wbuf = kb.sb("wbuf3", [128, NC, 128], BF16)
            pd = kb.ps("pd3", [128, 512])

            sq_sb = kb.sb("sq3", [128, 256], F32)
            junkf = kb.sb("junkf3", [128, 256], BF16)
            knw_row = kb.sb("knwrow", [1, 128], F32)
            den_sb = kb.sb("den3", [128, 256], F32)
            nxbf = kb.sb("nxbf3", [128, 256], F32)
            nww = kb.sb("nw3", [128, NC], F32)

            if L3DBGP in (3, 4) or L3DBGP >= 10:
                pass
            if L3DBGP >= 10:
                kb.memset("dve", q_fm.full, 0.01)
                kb.memset("dve", k_fm.full, 0.01)
                kb.memset("dve", V_tm.full, 0.01)
            onesbc = kb.sb("onesbc", [128, 1], BF16)
            kb.memset("dve", onesbc.full, 1.0)
            pnw = kb.ps("pnw3", [1, 128])
            for pj, wkey, bkey, akey in (() if L3DBGP in (3, 4) else (
                (0, "wq", "bq", "aq"), (1, "wk", "bk", "ak"),
                (2, "wv", "bv", "av"))):
                wap = P[wkey].rearrange("(c p) o -> p c o", p=128)
                bv_ = bias_map[bkey]
                scv = scb[akey][1]
                for oc in range(NC):
                    for hh in range(4):
                        kb.dma("pool", stage.full,
                               wap[:, 2 * hh:2 * (hh + 1), 128 * oc:128 * (oc + 1)])
                        kb.cast(wbuf[:, 2 * hh:2 * (hh + 1), :], stage.full,
                                engines=("dve", "act"))
                        if L3DBGP == 10:
                            continue
                        kb.tt_("dve", junkf.full,
                               R(stage, stage.h[:].rearrange("p a b -> p (a b)")),
                               R(stage, stage.h[:].rearrange("p a b -> p (a b)")),
                               ALU.mult)
                        jv = junkf.h[:].rearrange("p (a b) -> p a b", a=2)
                        kb.mm(pnw.full, [(onesbc.full, R(junkf, jv[:, a_, :]))
                                         for a_ in range(2)],
                              start=(hh == 0), stop=(hh == 3))
                    if L3DBGP == 10:
                        kb.memset("dve", nww[:, oc:oc + 1], 1.0)
                    else:
                        kb.copy("dve", R(knw_row, knw_row.h[:, 0:128]), pnw.full)
                        kb.trans(R(pd, pd.h[:, 0:1]), R(knw_row, knw_row.h[:, 0:128]),
                                 R(identf, identf.h[0:1, 0:1]))
                        kb.copy("dve", nww[:, oc:oc + 1], R(pd, pd.h[:, 0:1]))
                    pdv = R(pd, pd.h[:, 0:256])
                    for tq in range(8 if L3DBGP not in (11,) else 0):
                        cols = slice(256 * tq, 256 * (tq + 1))
                        kb.mm(pdv, [(wbuf[:, c, :], xn_fm[:, c, cols])
                                    for c in range(NC)])
                        kb.act("act", sq_sb.full, pdv, AF.Square)
                        if L3DBGP == 20:
                            continue
                        kb.copy("dve", nxbf.full,
                                R(nxb_all, nxb_all.h[:].rearrange(
                                    "p t n -> p (t n)")[:, cols]))
                        if L3DBGP == 23:
                            continue
                        kb.ts_("dve", den_sb.full, pdv, -2.0, ALU.mult)
                        kb.tt_("dve", den_sb.full, den_sb.full, nxbf.full, ALU.add)
                        if L3DBGP == 24:
                            continue
                        kb.ts_("dve", den_sb.full, den_sb.full, nww[:, oc:oc + 1],
                               ALU.add)
                        kb.ts_("dve", den_sb.full, den_sb.full, 1e-5, ALU.add)
                        if L3DBGP == 21:
                            continue
                        kb.act("act", den_sb.full, den_sb.full, AF.Ln)
                        kb.act("act", den_sb.full, den_sb.full, AF.Exp, scale=-1.0)
                        kb.tt_("dve", sq_sb.full, sq_sb.full, den_sb.full, ALU.mult)
                        if L3DBGP == 22:
                            continue
                        if pj == 0:
                            kb.ts_("dve", R(q_fm, q_fm.h[:, oc, cols]), sq_sb.full,
                                   bv_[:, oc:oc + 1], ALU.add, s2=scv.full, op1=ALU.mult)
                        elif pj == 1:
                            kb.ts_("dve", R(k_fm, k_fm.h[:, oc, cols]), sq_sb.full,
                                   bv_[:, oc:oc + 1], ALU.add, s2=scv.full, op1=ALU.mult)
                        else:
                            kb.ts_("dve", R(junkf, junkf.h[:, 0:256]), sq_sb.full,
                                   bv_[:, oc:oc + 1], ALU.add, s2=scv.full, op1=ALU.mult)
                            for blk in range(2):
                                kb.trans(R(ptile, ptile.h[:, 128 * blk:128 * (blk + 1)]),
                                         R(junkf, junkf.h[:, 128 * blk:128 * (blk + 1)]), identb)
                            for blk in range(2):
                                t_ = 2 * tq + blk
                                kb.copy("dve", R(V_tm, V_tm.h[:, t_, 128 * oc:128 * (oc + 1)]),
                                        R(ptile, ptile.h[:, 128 * blk:128 * (blk + 1)]))

        for half in range(2):
            with kb.phase():
                ctxh = kb.sb("ctxh", [128, NC, S // 2], BF16)
                with kb.phase():
                    kn_b = kb.sb("knb3", [128, S], BF16)
                    knrow = kb.sb("knrow", [1, 512], BF16)
                    qn_t = kb.sb("qnt3", [128, NT // 2], F32)
                    ksq = kb.sb("ksq3", [128, 512], F32)
                    den2s = [kb.sb("den23", [128, S], BF16) for _ in range(2)]
                    sq_bs = [kb.sb("sqb3", [128, S], BF16)]
                    pbufs = [kb.sb("pbuf3", [128, S], BF16) for _ in range(2)]
                    sumss = [kb.sb("sums3", [128, 1], F32) for _ in range(2)]
                    rs_s = [kb.sb("rs3", [128, 1], F32) for _ in range(2)]
                    nmxs = [kb.sb("nmx3", [128, 1], F32) for _ in range(2)]
                    ctmbs = [kb.sb("ctmb3", [128, 64], BF16) for _ in range(2)]

                    pS = kb.ps("pS3", [128, S])
                    pkn = kb.ps("pkn3", [1, 512])
                    pknb = kb.ps("pknb3", [128, 512])
                    pcx = kb.ps("pcx3", [128, 64])
                    ptr = kb.ps("ptr3", [128, 512], BF16)

                    for h in range(H):
                        ch, off = h // 2, 64 * (h % 2)
                        sl = slice(off, off + 64)
                        for qc in range(4):
                            cols = slice(512 * qc, 512 * (qc + 1))
                            kb.act("act", R(ksq, ksq.h[sl, :]),
                                   R(k_fm, k_fm.h[sl, ch, cols]), AF.Square)
                            kb.mm(pkn.full, [(R(ones64, ones64.h[sl, :]),
                                              R(ksq, ksq.h[sl, :]))])
                            kb.copy("dve", knrow.full, pkn.full)
                            kb.mm(pknb.full, [(onesb.full, knrow.full)])
                            kb.copy("dve", R(kn_b, kn_b.h[:, cols]), pknb.full)
                        for j in range(NT // 2):
                            t_ = (NT // 2) * half + j
                            kb.act("act", R(ksq, ksq.h[sl, 0:128]),
                                   R(q_fm, q_fm.h[sl, ch, 128 * t_:128 * (t_ + 1)]), AF.Square)
                            kb.mm(R(pkn, pkn.h[:, 0:128]),
                                  [(R(ones64, ones64.h[sl, :]), R(ksq, ksq.h[sl, 0:128]))])
                            kb.copy("dve", R(knrow, knrow.h[:, 0:128]), R(pkn, pkn.h[:, 0:128]))
                            kb.trans(R(ptr, ptr.h[:, 0:1]), R(knrow, knrow.h[:, 0:128]),
                                     R(identb, identb.h[0:1, 0:1]))
                            kb.copy("dve", qn_t[:, j:j + 1], R(ptr, ptr.h[:, 0:1]))
                        for j in range(NT // 2):
                            qt = (NT // 2) * half + j
                            par = j % 2
                            sq_b, den2, pbuf = sq_bs[0], den2s[par], pbufs[par]
                            sums, rs_, nmx, ctmb = (sumss[par], rs_s[par],
                                                    nmxs[par], ctmbs[par])
                            for qc in range(4):
                                kb.mm(R(pS, pS.h[:, 512 * qc:512 * (qc + 1)]),
                                      [(R(q_fm, q_fm.h[sl, ch, 128 * qt:128 * (qt + 1)]),
                                        R(k_fm, k_fm.h[sl, ch, 512 * qc:512 * (qc + 1)]))])
                            kb.act("act", sq_b.full, R(pS, pS.h[:, :]), AF.Square)
                            kb.stt("dve", den2.full, sq_b.full, kb.constf(-2.0), kn_b.full,
                                   ALU.mult, ALU.add)
                            kb.ts_("dve", den2.full, den2.full, qn_t[:, j:j + 1],
                                   ALU.add, s2=1e-5, op1=ALU.add)
                            kb.recip(den2.full, den2.full)
                            kb.tt_("pool", den2.full, sq_b.full, den2.full, ALU.mult)
                            kb.red(nmx.full, den2.full, ALU.max, negate=True)
                            kb.act("act", pbuf.full, den2.full, AF.Exp, bias=nmx.full,
                                   accum=sums.full)
                            kb.recip(rs_.full, sums.full)
                            for ktg in range(4):
                                for b4 in range(4):
                                    kt = 4 * ktg + b4
                                    kb.trans(R(ptr, ptr.h[:, 128 * b4:128 * (b4 + 1)]),
                                             pbuf[:, 128 * kt:128 * (kt + 1)], identb)
                                kb.copy("act", pbuf[:, 512 * ktg:512 * (ktg + 1)], ptr.full)
                            kb.mm(pcx.full,
                                  [(pbuf[:, 128 * kt:128 * (kt + 1)],
                                    R(V_tm, V_tm.h[:, kt, 64 * h:64 * h + 64]))
                                   for kt in range(NT)])
                            kb.ts_("dve", ctmb.full, pcx.full, rs_.full, ALU.mult)
                            kb.trans(R(ptr, ptr.h[0:64, 0:128]), ctmb.full,
                                     R(identb, identb.h[0:128, 0:128]))
                            kb.copy("dve", R(ctxh, ctxh.h[sl, ch, 128 * j:128 * (j + 1)]),
                                    R(ptr, ptr.h[0:64, 0:128]))

                with kb.phase():
                    wob = kb.sb("wob3", [128, NC, D], BF16)
                    stage2 = kb.sb("stg3b", [128, 4, 128], F32)
                    woap = P["wo"].rearrange("(c p) o -> p c o", p=128)
                    for oc in range(NC):
                        for hh in range(2):
                            kb.dma("sp", stage2.full,
                                   woap[:, 4 * hh:4 * (hh + 1), 128 * oc:128 * (oc + 1)])
                            kb.cast(R(wob, wob.h[:, 4 * hh:4 * (hh + 1),
                                                 128 * oc:128 * (oc + 1)]), stage2.full)
                    yfm = kb.sb("yfm3", [128, NC, 128], F32)
                    tmp2 = junk
                    ytm = kb.sb("ytm3", [128, 1, D], F32)
                    pyo = kb.ps("pyo3", [128, 512])
                    pt = kb.ps("pt3", [128, 512], F32)
                    for j in range(NT // 2):
                        for oc in range(NC):
                            kb.mm(R(pyo, pyo.h[:, 128 * (oc % 4):128 * (oc % 4) + 128]),
                                  [(R(wob, wob.h[:, c, 128 * oc:128 * (oc + 1)]),
                                    ctxh[:, c, 128 * j:128 * (j + 1)])
                                   for c in range(NC)])
                            kb.ts_("dve", yfm[:, oc, :],
                                   R(pyo, pyo.h[:, 128 * (oc % 4):128 * (oc % 4) + 128]),
                                   R(bias_map["bo"], bias_map["bo"].h[:, oc:oc + 1]), ALU.add)
                        epilogue_fm_group(kb, st, x, yfm, pt, tmp2, junk,
                                          (NT // 2) * half + j, 1, ytm)



def emit_ffn_quant(kb, st, P):
    x = st["x"]
    identb = st["identb"]
    identf = st["identf"]
    junk = st["junk"]
    QB = 128.0

    with kb.phase():
        gfm = kb.sb("qgam", [128, NC], F32); kb.dma("sp", gfm.full, P["g_r"])
        bfm = kb.sb("qbet", [128, NC], F32); kb.dma("sp", bfm.full, P["b_r"])
        ones1q = kb.sb("ones1q", [1, 128], F32)
        kb.memset("dve", ones1q.full, 1.0)
        onescq = kb.sb("onescq", [128, 1], F32)
        kb.memset("dve", onescq.full, 1.0)
        xn_fm = kb.sb("qxn", [128, NC, S], BF16)
        wq_b = kb.sb("qwq", [128, NC, D], BF16)
        stage = kb.sb("qstg", [128, D], F32)
        tmp = kb.sb("qtmp", [128, D], BF16)
        sc1 = kb.sb("qsc1", [128, 1], F32)
        rowq = kb.sb("qrow", [1, 128], F32)
        gcol = kb.sb("qgcol", [128, NC], F32)
        thr_b = kb.sb("qthr", [128, 1], F32)
        sxb = kb.sb("qsx", [128, 1], F32)
        soutb = kb.sb("qso", [128, 1], F32)
        beta_s = kb.sb("qbeta", [1, 1], F32)
        gam_s = kb.sb("qgam1", [1, 1], F32)
        tmpf = kb.sb("qtf", [128, 1024], F32)
        yfm = kb.sb("qyfm", [128, NC, 512], F32)
        ytm = kb.sb("qytm", [128, 1, D], F32)
        tmp2 = junk

        ptile = kb.ps("qptile", [128, D], BF16)
        prow = kb.ps("qprow", [1, 128])
        pcol = kb.ps("qpcol", [128, 128])
        pmm = kb.ps("qpmm", [128, 512])
        pt = kb.ps("qpt", [128, 512], F32)

        r, b = ln_stats(kb, x, list(range(NT)), 1e-3, junk, name="lnq")
        for t in range(NT):
            kb.act("act", tmp.full, x[:, t, :], AF.Identity,
                   bias=b[:, t:t + 1], scale=r[:, t:t + 1])
            for c in range(NC):
                kb.trans(R(ptile, ptile.h[:, 128 * c:128 * (c + 1)]),
                         tmp[:, 128 * c:128 * (c + 1)], identb)
            dsl = xn_fm.h[:, :, 128 * t:128 * (t + 1)]
            pview = ptile.h[:].rearrange("p (c n) -> p c n", c=NC)
            for c in range(NC):
                kb.emit("dve", lambda e, c=c: e.tensor_scalar(
                    out=dsl[:, c, :], in0=pview[:, c, :], scalar1=gfm.h[:, c:c + 1],
                    scalar2=bfm.h[:, c:c + 1], op0=ALU.mult, op1=ALU.add),
                    reads=[ptile, gfm, bfm], writes=[R(xn_fm, dsl)])

        for c in range(NC):
            kb.red(gcol[:, c:c + 1], xn_fm[:, c, :], ALU.max, absval=True)
        kb.red(sc1.full, gcol.full, ALU.max)
        kb.trans(prow.full, sc1.full, identf)
        kb.copy("dve", rowq.full, prow.full)
        kb.red(R(gam_s, gam_s.h[0:1, :]), rowq.full, ALU.max)

        wap = P["weight"].rearrange("(c p) o -> p c o", p=128)
        bs = kb.sb("qbs", [128, NC], F32)
        for c in range(NC):
            kb.dma("sp", stage.full, wap[:, c, :])
            kb.red(bs[:, c:c + 1], stage.full, ALU.add, absval=True)
        kb.red(sc1.full, bs.full, ALU.add)
        kb.mm(R(prow, prow.h[0:1, 0:1]), [(sc1.full, onescq.full)])
        kb.copy("dve", R(beta_s, beta_s.h[0:1, :]), R(prow, prow.h[0:1, 0:1]))
        kb.ts_("dve", R(beta_s, beta_s.h[0:1, :]), R(beta_s, beta_s.h[0:1, :]),
               1.0 / (D * D), ALU.mult)
        kb.ts_("dve", R(sc1, sc1.h[0:1, :]), R(beta_s, beta_s.h[0:1, :]),
               1e-5, ALU.add, s2=0.5, op1=ALU.mult)
        kb.mm(R(pcol, pcol.h[:, 0:1]), [(ones1q.full, R(sc1, sc1.h[0:1, 0:1]))])
        kb.copy("dve", thr_b.full, R(pcol, pcol.h[:, 0:1]))
        for c in range(NC):
            kb.dma("sp", stage.full, wap[:, c, :])
            kb.act("act", tmpf.full, stage.full, AF.Abs)
            kb.ts_("dve", tmpf.full, tmpf.full, thr_b.full, ALU.subtract)
            kb.act("act", tmpf.full, tmpf.full, AF.Sign)
            kb.ts_("dve", tmpf.full, tmpf.full, 0.0, ALU.max)
            kb.act("act", R(junk, junk.h[:, 0:D]), stage.full, AF.Sign)
            kb.tt_("dve", R(wq_b, wq_b.h[:, c, :]), tmpf.full,
                   R(junk, junk.h[:, 0:D]), ALU.mult)

        kb.act("act", R(sc1, sc1.h[0:1, :]), R(gam_s, gam_s.h[0:1, :]), AF.Ln)
        kb.act("act", R(sc1, sc1.h[0:1, :]), R(sc1, sc1.h[0:1, :]), AF.Exp, scale=-1.0)
        kb.ts_("dve", R(sc1, sc1.h[0:1, :]), R(sc1, sc1.h[0:1, :]), QB, ALU.mult)
        kb.mm(R(pcol, pcol.h[:, 0:1]), [(ones1q.full, R(sc1, sc1.h[0:1, 0:1]))])
        kb.copy("dve", sxb.full, R(pcol, pcol.h[:, 0:1]))
        for c in range(NC):
            kb.ts_("dve", R(xn_fm, xn_fm.h[:, c, :]), R(xn_fm, xn_fm.h[:, c, :]),
                   sxb.full, ALU.mult)
            kb.ts_("dve", R(xn_fm, xn_fm.h[:, c, :]), R(xn_fm, xn_fm.h[:, c, :]),
                   float(-QB + 1e-5), ALU.max)
            kb.ts_("dve", R(xn_fm, xn_fm.h[:, c, :]), R(xn_fm, xn_fm.h[:, c, :]),
                   float(QB - 1e-5), ALU.min)
        kb.tt_("dve", R(sc1, sc1.h[0:1, :]), R(gam_s, gam_s.h[0:1, :]),
               R(beta_s, beta_s.h[0:1, :]), ALU.mult)
        kb.ts_("dve", R(sc1, sc1.h[0:1, :]), R(sc1, sc1.h[0:1, :]), 1.0 / QB, ALU.mult)
        kb.mm(R(pcol, pcol.h[:, 0:1]), [(ones1q.full, R(sc1, sc1.h[0:1, 0:1]))])
        kb.copy("dve", soutb.full, R(pcol, pcol.h[:, 0:1]))

        for tg in range(4):
            cols = slice(512 * tg, 512 * (tg + 1))
            for oc in range(NC):
                kb.mm(pmm.full, [(R(wq_b, wq_b.h[:, c, 128 * oc:128 * (oc + 1)]),
                                  R(xn_fm, xn_fm.h[:, c, cols])) for c in range(NC)])
                kb.ts_("dve", yfm[:, oc, :], pmm.full, soutb.full, ALU.mult)
            epilogue_fm_group(kb, st, x, yfm, pt, tmp2, junk, 4 * tg, 4, ytm)


def declare_io(nc, names_shapes):
    out = {}
    for item in names_shapes:
        name, shape = item[0], item[1]
        dt = item[2] if len(item) > 2 else F32
        out[name] = nc.dram_tensor(name, list(shape), dt, kind="ExternalInput").ap()
    return out


def persistent_state(kb):
    nc = kb.nc
    st = {}
    identb_np = np.eye(128, dtype=BF)
    identf_np = np.eye(128, dtype=np.float32)
    hb = nc.inline_tensor(np.asarray(identb_np), "identb_c")
    hf = nc.inline_tensor(identf_np, "identf_c")
    st["identb"] = kb.sb("identb", [128, 128], BF16, persistent=True)
    st["identf"] = kb.sb("identf", [128, 128], F32, persistent=True)
    kb.dma("sp", st["identb"].full, hb[:])
    kb.dma("sp", st["identf"].full, hf[:])
    st["junk"] = kb.sb("junk", [128, 1024], BF16, persistent=True)
    st["x"] = kb.sb("stream", [128, NT, D], F32, persistent=True)
    return st


def load_stream(kb, st, x_ap):
    kb.dma("sp", st["x"].full, x_ap.rearrange("(i p) d -> p i d", p=128))


def store_stream(kb, st, y_ap):
    kb.fence()
    kb.dma("pool", y_ap.rearrange("(i p) d -> p i d", p=128), st["x"].full)
    kb.drain_all()


def prep_ffn_ternary(inp, li):
    g = lambda k: inp[f"L{li}_ffn_{k}"]
    return {
        "w1T": np.ascontiguousarray(g("w1_w").T),
        "w2T": np.ascontiguousarray(g("w2_w").T),
        "h1T": np.ascontiguousarray(g("h1_w").T),
        "h2T": np.ascontiguousarray(g("h2_w").T),
        "w3T": np.ascontiguousarray(g("w3_w").T),
        "s1r": np.ascontiguousarray(g("w1_s").reshape(32, 128).T),
        "s2r": np.ascontiguousarray(g("w2_s").reshape(32, 128).T),
        "sh1r": np.ascontiguousarray(g("h1_s").reshape(32, 128).T),
        "sh2r": np.ascontiguousarray(g("h2_s").reshape(32, 128).T),
        "s3r": np.ascontiguousarray(g("w3_s").reshape(8, 128).T),
    }


FFN_TERNARY_SHAPES = [("w1T", (D, 4096)), ("w2T", (D, 4096)), ("h1T", (4096, 4096)),
                      ("h2T", (4096, 4096)), ("w3T", (4096, D)),
                      ("s1r", (128, 32)), ("s2r", (128, 32)), ("sh1r", (128, 32)),
                      ("sh2r", (128, 32)), ("s3r", (128, 8))]

MOE_SHAPES = [("up_w", (8, D, 4096)), ("down_w", (8, 2048, D)),
              ("scale_r", (128, 8)), ("ctr", (8, D)), ("ctr_fm", (128, 8, 8))]


def prep_moe(inp):
    g = lambda k: inp[f"L0_ffn_{k}"]
    C = g("centroids")
    return {
        "up_w": np.ascontiguousarray(g("up_w")),
        "down_w": np.ascontiguousarray(g("down_w")),
        "scale_r": np.ascontiguousarray(g("scale").reshape(8, 128).T),
        "ctr": np.ascontiguousarray(C),
        "ctr_fm": np.ascontiguousarray(C.T.reshape(8, 128, 8).transpose(1, 0, 2)),
    }


ATTN_SHAPES_BASE = [("wqkvT", (D, 3 * D)), ("woT", (D, D)), ("mask01", (128, 896), BF16)]
ATTN_SHAPES_L0 = ATTN_SHAPES_BASE + [("bo_r", (128, 8))]
ATTN_SHAPES_L2 = ATTN_SHAPES_BASE + [("cos_fm", (128, S), BF16), ("sin_fm", (128, S), BF16),
                                     ("rt", (128, 64), BF16)]


def host_mask01():
    p = np.arange(128)[:, None]
    m = np.arange(896)[None, :]
    return (m >= 384 + p).astype(BF)


def host_rope_tables():
    inv = 1.0 / (10000.0 ** (np.arange(0, 64, 2, dtype=np.float32) / 64.0))
    t = np.arange(S, dtype=np.float32)
    fr = np.einsum("i,j->ij", t, inv)
    emb = np.concatenate([fr, fr], axis=-1)
    cos = np.cos(emb).T.astype(BF)
    sin = np.sin(emb).T.astype(BF)
    cos = np.concatenate([cos, cos], axis=0)
    sin = np.concatenate([sin, sin], axis=0)
    return np.ascontiguousarray(cos), np.ascontiguousarray(sin)


def host_rt():
    rt = np.zeros((64, 64), dtype=BF)
    for i in range(32):
        rt[i + 32, i] = -1.0
    for i in range(32, 64):
        rt[i - 32, i] = 1.0
    return np.ascontiguousarray(np.concatenate([rt, rt], axis=0))


def prep_attn_l0(inp):
    g = lambda k: inp[f"L0_attn_{k}"]
    return {
        "wqkvT": np.ascontiguousarray(np.concatenate(
            [g("w_q").T, g("w_k").T, g("w_v").T], axis=1)),
        "woT": np.ascontiguousarray(g("w_o").T),
        "bo_r": np.ascontiguousarray(g("b_o").reshape(8, 128).T),
        "mask01": host_mask01(),
    }


ATTN_SHAPES_L1 = [("wqkvT", (D, 3 * D)), ("woT", (D, D)), ("bo_r", (128, 8)),
                  ("maskloc", (128, 2, 128), BF16)]


ATTN_SHAPES_L3 = [("wq", (D, D)), ("wk", (D, D)), ("wv", (D, D)), ("wo", (D, D)),
                  ("bq_r", (128, 8)), ("bk_r", (128, 8)), ("bv_r", (128, 8)),
                  ("bo_r", (128, 8)),
                  ("aq", (1, 1)), ("ak", (1, 1)), ("av", (1, 1))]


def prep_attn_l3(inp):
    g = lambda k: inp[f"L3_attn_{k}"]
    return {
        "wq": np.ascontiguousarray(g("wq")), "wk": np.ascontiguousarray(g("wk")),
        "wv": np.ascontiguousarray(g("wv")), "wo": np.ascontiguousarray(g("wo")),
        "bq_r": np.ascontiguousarray(g("bq").reshape(8, 128).T),
        "bk_r": np.ascontiguousarray(g("bk").reshape(8, 128).T),
        "bv_r": np.ascontiguousarray(g("bv").reshape(8, 128).T),
        "bo_r": np.ascontiguousarray(g("bo").reshape(8, 128).T),
        "aq": g("aq").reshape(1, 1).astype(np.float32),
        "ak": g("ak").reshape(1, 1).astype(np.float32),
        "av": g("av").reshape(1, 1).astype(np.float32),
    }


FFN_QUANT_SHAPES = [("weight", (D, D)), ("g_r", (128, 8)), ("b_r", (128, 8))]


def prep_ffn_quant(inp):
    g = lambda k: inp[f"L3_ffn_{k}"]
    return {
        "weight": np.ascontiguousarray(g("weight")),
        "g_r": np.ascontiguousarray(g("ln_gamma").reshape(8, 128).T),
        "b_r": np.ascontiguousarray(g("ln_beta").reshape(8, 128).T),
    }


def prep_attn_l1(inp):
    g = lambda k: inp[f"L1_attn_{k}"]
    p = np.arange(128)[:, None]
    c = np.arange(128)[None, :]
    maskloc = np.stack([(c <= p), (c >= p)], axis=1).astype(BF)
    return {
        "wqkvT": np.ascontiguousarray(g("w_qkv").T),
        "woT": np.ascontiguousarray(g("w_out").T),
        "bo_r": np.ascontiguousarray(g("b_out").reshape(8, 128).T),
        "maskloc": maskloc,
    }


def prep_attn_l2(inp):
    g = lambda k: inp[f"L2_attn_{k}"]
    cos, sin = host_rope_tables()
    return {
        "wqkvT": np.ascontiguousarray(g("w_qkv").T),
        "woT": np.ascontiguousarray(g("w_out").T),
        "mask01": host_mask01(),
        "cos_fm": cos, "sin_fm": sin, "rt": host_rt(),
    }


def build_block_program(block):
    nc = bass.Bass("TRN2", target_bir_lowering=False)
    x_ap = nc.dram_tensor("x", [S, D], F32, kind="ExternalInput").ap()
    y_ap = nc.dram_tensor("y", [S, D], F32, kind="ExternalOutput").ap()
    kb = KB(nc)
    st = persistent_state(kb)
    load_stream(kb, st, x_ap)
    if block == "ffn_ternary":
        P = declare_io(nc, FFN_TERNARY_SHAPES)
        emit_ffn_ternary(kb, st, P)
    elif block == "attn_l0":
        P = declare_io(nc, ATTN_SHAPES_L0)
        emit_attn_causal(kb, st, P, rope=False, bias=True)
    elif block == "attn_l2":
        P = declare_io(nc, ATTN_SHAPES_L2)
        emit_attn_causal(kb, st, P, rope=True, bias=False)
    elif block == "moe":
        P = declare_io(nc, MOE_SHAPES)
        emit_moe(kb, st, P)
    elif block == "attn_l1":
        P = declare_io(nc, ATTN_SHAPES_L1)
        emit_attn_l1(kb, st, P)
    elif block == "attn_l3":
        P = declare_io(nc, ATTN_SHAPES_L3)
        emit_attn_yat(kb, st, P)
    elif block == "ffn_quant":
        P = declare_io(nc, FFN_QUANT_SHAPES)
        emit_ffn_quant(kb, st, P)
    else:
        raise ValueError(block)
    store_stream(kb, st, y_ap)
    return nc


FUSED_CHAIN = [
    ("b0", ATTN_SHAPES_L0, lambda kb, st, P: emit_attn_causal(kb, st, P, rope=False, bias=True),
     lambda inp: prep_attn_l0(inp)),
    ("b1", MOE_SHAPES, lambda kb, st, P: emit_moe(kb, st, P),
     lambda inp: prep_moe(inp)),
    ("b2", ATTN_SHAPES_L1, lambda kb, st, P: emit_attn_l1(kb, st, P),
     lambda inp: prep_attn_l1(inp)),
    ("b3", FFN_TERNARY_SHAPES, lambda kb, st, P: emit_ffn_ternary(kb, st, P),
     lambda inp: prep_ffn_ternary(inp, 1)),
    ("b4", ATTN_SHAPES_L2, lambda kb, st, P: emit_attn_causal(kb, st, P, rope=True, bias=False),
     lambda inp: prep_attn_l2(inp)),
    ("b5", FFN_TERNARY_SHAPES, lambda kb, st, P: emit_ffn_ternary(kb, st, P),
     lambda inp: prep_ffn_ternary(inp, 2)),
    ("b6", ATTN_SHAPES_L3, lambda kb, st, P: emit_attn_yat(kb, st, P),
     lambda inp: prep_attn_l3(inp)),
    ("b7", FFN_QUANT_SHAPES, lambda kb, st, P: emit_ffn_quant(kb, st, P),
     lambda inp: prep_ffn_quant(inp)),
]


def build_fused_program():
    nc = bass.Bass("TRN2", target_bir_lowering=False)
    x_ap = nc.dram_tensor("x", [S, D], F32, kind="ExternalInput").ap()
    y_ap = nc.dram_tensor("y", [S, D], F32, kind="ExternalOutput").ap()
    kb = KB(nc)
    st = persistent_state(kb)
    load_stream(kb, st, x_ap)
    for prefix, shapes, emitfn, _ in FUSED_CHAIN:
        pre = [(f"{prefix}_{it[0]}",) + tuple(it[1:]) for it in shapes]
        Praw = declare_io(nc, pre)
        P = {k[len(prefix) + 1:]: v for k, v in Praw.items()}
        emitfn(kb, st, P)
    store_stream(kb, st, y_ap)
    return nc


def prep_all(inputs):
    d = {}
    for prefix, _, _, prep in FUSED_CHAIN:
        for k, v in prep(inputs).items():
            d[f"{prefix}_{k}"] = v
    return d


_CHAIN = [
    ("attn_l0", lambda inp: prep_attn_l0(inp)),
    ("moe", lambda inp: prep_moe(inp)),
    ("attn_l1", lambda inp: prep_attn_l1(inp)),
    ("ffn_ternary", lambda inp: prep_ffn_ternary(inp, 1)),
    ("attn_l2", lambda inp: prep_attn_l2(inp)),
    ("ffn_ternary", lambda inp: prep_ffn_ternary(inp, 2)),
    ("attn_l3", lambda inp: prep_attn_l3(inp)),
    ("ffn_quant", lambda inp: prep_ffn_quant(inp)),
]

_prog_cache = {}


def _get_prog(block):
    if block not in _prog_cache:
        _prog_cache[block] = build_block_program(block)
    return _prog_cache[block]


def kernel(**inputs):
    x = inputs["L0_attn_x"]
    shards = [np.ascontiguousarray(x[c]) for c in range(NCORES)]
    if os.environ.get("MK_UNFUSED", "0") == "1":
        for block, prep in _CHAIN:
            nc = _get_prog(block)
            P = prep(inputs)
            in_maps = [{"x": shards[c], **P} for c in range(NCORES)]
            res = run_bass_kernel_spmd(nc, in_maps, core_ids=list(range(NCORES)))
            shards = [np.asarray(r["y"], dtype=np.float32) for r in res.results]
        return np.stack(shards, axis=0)
    if "fused" not in _prog_cache:
        _prog_cache["fused"] = build_fused_program()
    nc = _prog_cache["fused"]
    P = prep_all(inputs)
    in_maps = [{"x": shards[c], **P} for c in range(NCORES)]
    res = run_bass_kernel_spmd(nc, in_maps, core_ids=list(range(NCORES)))
    return np.stack([np.asarray(r["y"], dtype=np.float32) for r in res.results], axis=0)


# version 08 (final)
# speedup vs baseline: 1.1894x; 1.0166x over previous
import os
import numpy as np
import ml_dtypes
from contextlib import ExitStack

import concourse.bass as bass
import concourse.mybir as mybir
from concourse.bass_utils import run_bass_kernel_spmd

F32 = mybir.dt.float32
BF16 = mybir.dt.bfloat16
AF = mybir.ActivationFunctionType
ALU = mybir.AluOpType
AX = mybir.AxisListType

NCORES = 8
S = 2048
D = 1024
NT = S // 128
NC = D // 128
BF = ml_dtypes.bfloat16

ENGS = ("pe", "act", "dve", "pool", "sp")


class R:
    __slots__ = ("tt", "ap")
    def __init__(self, tt, ap):
        self.tt = tt
        self.ap = ap


class TT:
    __slots__ = ("kb", "h", "lw", "readers", "is_psum")
    def __init__(self, kb, h, is_psum=False):
        self.kb = kb
        self.h = h
        self.lw = None
        self.readers = []
        self.is_psum = is_psum
    def __getitem__(self, idx):
        return R(self, self.h[idx])
    @property
    def full(self):
        return R(self, self.h[:])


class KB:

    def __init__(self, nc):
        self.nc = nc
        self.eng = {"pe": nc.tensor, "act": nc.scalar, "dve": nc.vector,
                    "pool": nc.gpsimd, "sp": nc.sync}
        self.stack = ExitStack()
        self.sems = {}
        self.cnt = {}
        self.wm = {}
        self.fences = {}
        self._uid = 0
        self._es = None
        self._cast_rr = 0
        self._consts = {}
        self._cache = [{}]
        self.cur = {}
        self.last_dma = {}
        for e in ENGS:
            self.cur["s_" + e] = "s_" + e
            self._mksem("s_" + e)
        for q in ("sp", "pool", "act", "dve"):
            self.cur["d_" + q] = "d_" + q
            self._mksem("d_" + q)

    def _mksem(self, name):
        self.sems[name] = self.stack.enter_context(self.nc.semaphore(name))
        self.cnt[name] = 0

    def _sem(self, role):
        name = self.cur[role]
        if self.cnt[name] >= 60000:
            name = f"{role}#{len(self.cnt)}"
            self.cur[role] = name
            self._mksem(name)
        return self.cur[role]

    def uid(self, base):
        self._uid += 1
        return f"{base}_{self._uid}"

    def sb(self, name, shape, dtype, persistent=False):
        nm = self.uid(name)
        if persistent or self._es is None:
            h = self.nc.alloc_sbuf_tensor(nm, shape, dtype, side="right")
        else:
            h = self._es.enter_context(self.nc.sbuf_tensor(nm, shape, dtype))
        return TT(self, h)

    def sbc(self, name, shape, dtype):
        key = (name, tuple(shape), dtype)
        c = self._cache[-1]
        if key not in c:
            c[key] = self.sb(name, shape, dtype)
        return c[key]

    def ps(self, name, shape, dtype=F32):
        nm = self.uid(name)
        assert self._es is not None
        h = self._es.enter_context(self.nc.psum_tensor(nm, shape, dtype))
        return TT(self, h, is_psum=True)

    def phase(self):
        kb = self
        class _P:
            def __enter__(self):
                self.es = ExitStack()
                self.prev = kb._es
                kb._es = self.es.__enter__()
                kb._cache.append({})
                return kb._es
            def __exit__(self, *a):
                self.es.__exit__(*a)
                kb._es = self.prev
                kb._cache.pop()
                kb.fence()
                return False
        return _P()

    def fence(self):
        for s, v in self.cnt.items():
            if v > self.fences.get(s, 0):
                self.fences[s] = v

    def _wait(self, ename, sem, val):
        if val <= 0:
            return
        if self.wm.get((ename, sem), 0) >= val:
            return
        self.eng[ename].wait_ge(self.sems[sem], val)
        self.wm[(ename, sem)] = val

    def _gather(self, ename, reads, writes):
        need = dict(self.fences)
        def add(tok):
            if tok is not None:
                s, v = tok
                if need.get(s, 0) < v:
                    need[s] = v
        def tt_of(x):
            return x if isinstance(x, TT) else x.tt
        for r in reads:
            t = tt_of(r)
            add(t.lw)
            if t.is_psum:
                for tok in t.readers:
                    add(tok)
        for w in writes:
            add(tt_of(w).lw)
            for tok in tt_of(w).readers:
                add(tok)
        for s, v in sorted(need.items()):
            if ename == "pe" and s.startswith("s_pe"):
                continue
            self._wait(ename, s, v)

    def emit(self, ename, builder, reads=(), writes=()):
        self._gather(ename, reads, writes)
        inst = builder(self.eng[ename])
        sem = self._sem("s_" + ename)
        self.cnt[sem] += 1
        inst.then_inc(self.sems[sem], 1)
        tok = (sem, self.cnt[sem])
        for w in writes:
            (w if isinstance(w, TT) else w.tt).lw = tok
            (w if isinstance(w, TT) else w.tt).readers = []
        for r in reads:
            (r if isinstance(r, TT) else r.tt).readers.append(tok)
        return inst

    @staticmethod
    def _ap(x):
        if isinstance(x, TT):
            return x.h[:]
        return x.ap if isinstance(x, R) else x

    def dma(self, q, dst, src):
        reads = [src] if isinstance(src, R) else []
        writes = [dst] if isinstance(dst, R) else []
        self._gather(q, reads, writes)
        prev = self.last_dma.get(q)
        if prev is not None:
            self._wait(q, prev[0], prev[1])
        sem = self._sem("d_" + q)
        inst = self.eng[q].dma_start(out=self._ap(dst), in_=self._ap(src))
        self.cnt[sem] += 16
        self.last_dma[q] = (sem, self.cnt[sem])
        inst.then_inc(self.sems[sem], 16)
        tok = (sem, self.cnt[sem])
        for w in writes:
            (w if isinstance(w, TT) else w.tt).lw = tok
            (w if isinstance(w, TT) else w.tt).readers = []
        for r in reads:
            (r if isinstance(r, TT) else r.tt).readers.append(tok)
        return inst

    def mm(self, psum, pairs, start=True, stop=True):
        reads = []
        for l, r in pairs:
            reads += [l, r]
        self._gather("pe", reads, [psum])
        n = len(pairs)
        inst = None
        for i, (l, r) in enumerate(pairs):
            inst = self.eng["pe"].matmul(out=self._ap(psum), lhsT=self._ap(l), rhs=self._ap(r),
                                         start=(start and i == 0),
                                         stop=(stop and i == n - 1))
        sem = self._sem("s_pe")
        self.cnt[sem] += 1
        inst.then_inc(self.sems[sem], 1)
        tok = (sem, self.cnt[sem])
        (psum if isinstance(psum, TT) else psum.tt).lw = tok
        (psum if isinstance(psum, TT) else psum.tt).readers = []
        for r in reads:
            (r if isinstance(r, TT) else r.tt).readers.append(tok)

    def trans(self, psum, src, ident):
        self.emit("pe", lambda e: e.transpose(out=self._ap(psum), in_=self._ap(src), identity=self._ap(ident)),
                  reads=[src, ident], writes=[psum])

    def act(self, ename, dst, src, func, bias=0.0, scale=1.0, accum=None):
        reads = [src] + [x for x in (bias, scale) if isinstance(x, R)]
        writes = [dst] + ([accum] if accum is not None else [])
        kw = {}
        if accum is not None:
            kw["accum_out"] = self._ap(accum)
        self.emit(ename, lambda e: e.activation(out=self._ap(dst), in_=self._ap(src), func=func,
                                                bias=self._ap(bias), scale=self._ap(scale), **kw),
                  reads=reads, writes=writes)

    def copy(self, ename, dst, src):
        if ename == "act":
            self.emit(ename, lambda e: e.copy(out=self._ap(dst), in_=self._ap(src)), reads=[src], writes=[dst])
        else:
            self.emit(ename, lambda e: e.tensor_copy(out=self._ap(dst), in_=self._ap(src)), reads=[src], writes=[dst])

    def cast(self, dst, src, engines=("dve", "act", "pool")):
        e = engines[self._cast_rr % len(engines)]
        self._cast_rr += 1
        self.copy(e, dst, src)

    def tt_(self, ename, dst, a, b, op):
        self.emit(ename, lambda e: e.tensor_tensor(out=self._ap(dst), in0=self._ap(a), in1=self._ap(b), op=op),
                  reads=[a, b], writes=[dst])

    def ts_(self, ename, dst, a, s1, op0, s2=None, op1=None, accum=None):
        reads = [a] + [x for x in (s1, s2) if isinstance(x, R)]
        writes = [dst] + ([accum] if accum is not None else [])
        kw = {}
        if accum is not None:
            kw["accum_out"] = self._ap(accum)
        if op1 is None:
            self.emit(ename, lambda e: e.tensor_scalar(out=self._ap(dst), in0=self._ap(a), scalar1=self._ap(s1),
                                                       scalar2=None, op0=op0, **kw),
                      reads=reads, writes=writes)
        else:
            self.emit(ename, lambda e: e.tensor_scalar(out=self._ap(dst), in0=self._ap(a), scalar1=self._ap(s1),
                                                       scalar2=self._ap(s2), op0=op0, op1=op1, **kw),
                      reads=reads, writes=writes)

    def stt(self, ename, dst, a, s, b, op0, op1):
        reads = [a, b] + ([s] if isinstance(s, R) else [])
        self.emit(ename, lambda e: e.scalar_tensor_tensor(out=self._ap(dst), in0=self._ap(a), scalar=self._ap(s),
                                                          in1=self._ap(b), op0=op0, op1=op1),
                  reads=reads, writes=[dst])

    def red(self, dst, src, op, negate=False, absval=False):
        self.emit("dve", lambda e: e.tensor_reduce(out=self._ap(dst), in_=self._ap(src), axis=AX.X, op=op,
                                                   negate=negate, apply_absolute_value=absval),
                  reads=[src], writes=[dst])

    def memset(self, ename, dst, val):
        self.emit(ename, lambda e: e.memset(self._ap(dst), val), reads=[], writes=[dst])

    def recip(self, dst, src):
        def _b(e):
            with self.nc.allow_low_precision("bf16 reciprocal within validated tolerance"):
                return e.reciprocal(out=self._ap(dst), in_=self._ap(src))
        self.emit("dve", _b, reads=[src], writes=[dst])

    def track_dram(self, handle):
        return TT(self, handle)

    def constf(self, val, dtype=F32):
        key = (float(val), dtype)
        if key not in self._consts:
            t = self.sb("cst", [128, 1], dtype, persistent=True)
            self.memset("dve", t.full, float(val))
            self._consts[key] = t
        return self._consts[key].full

    def drain_all(self):
        for s, v in sorted(self.cnt.items()):
            if v > 0:
                self._wait("pool", s, v)



def ln_stats(kb, x, tiles, eps, junk, name="ln"):
    n = len(tiles)
    sums = kb.sbc(name + "_s", [128, n], F32)
    sq = kb.sbc(name + "_q", [128, n], F32)
    for j, t in enumerate(tiles):
        kb.red(sums[:, j:j + 1], x[:, t, :], ALU.add)
        kb.act("act", junk[:, 0:1024], x[:, t, :], AF.Square, accum=sq[:, j:j + 1])
    m = kb.sbc(name + "_m", [128, n], F32)
    var = kb.sbc(name + "_v", [128, n], F32)
    r = kb.sbc(name + "_r", [128, n], F32)
    b = kb.sbc(name + "_b", [128, n], F32)
    kb.ts_("dve", m.full, sums.full, 1.0 / D, ALU.mult)
    kb.tt_("dve", var.full, m.full, m.full, ALU.mult)
    kb.stt("dve", var.full, sq.full, 1.0 / D, var.full, ALU.mult, ALU.subtract)
    kb.act("act", var.full, var.full, AF.Sqrt, bias=kb.constf(eps))
    kb.recip(r.full, var.full)
    kb.stt("dve", b.full, m.full, -1.0, r.full, ALU.mult, ALU.mult)
    return r, b


def ln_transpose_fm(kb, dst, x, tiles, r, b, ident, tmp, ptile, scale_fm=None, dst_col0=0):
    for j, t in enumerate(tiles):
        kb.act("act", tmp.full, x[:, t, :], AF.Identity,
               bias=b[:, j:j + 1], scale=r[:, j:j + 1])
        for c in range(NC):
            kb.trans(ptile[:, 128 * c:128 * (c + 1)], tmp[:, 128 * c:128 * (c + 1)], ident)
        col = 128 * (dst_col0 + j)
        dsl = dst.h[:, :, col:col + 128]
        dR = R(dst, dsl)
        pview = ptile.h[:].rearrange("p (c n) -> p c n", c=NC)
        if scale_fm is not None:
            kb.emit("dve", lambda e: e.tensor_scalar(out=dsl, in0=pview, scalar1=kb._ap(scale_fm),
                                                     scalar2=None, op0=ALU.mult),
                    reads=[ptile, scale_fm], writes=[dR])
        else:
            kb.emit("dve", lambda e: e.tensor_copy(out=dsl, in_=pview),
                    reads=[ptile], writes=[dR])


def transpose_tm_fm(kb, dst, src, tiles, ident, psname="tp", dtype=BF16,
                    scale_fm=None, dst_f32_stage=None, chunks=range(NC)):
    grp = 4
    p = kb.ps(psname, [128, 128 * grp], dtype)
    for c in chunks:
        for g0 in range(0, len(tiles), grp):
            gt = tiles[g0:g0 + grp]
            for k, t in enumerate(gt):
                kb.trans(p[:, 128 * k:128 * (k + 1)], src[:, t, 128 * c:128 * (c + 1)], ident)
            dsl = dst[:, c, 128 * g0:128 * g0 + 128 * len(gt)]
            if scale_fm is not None:
                kb.ts_("dve", dsl, p[:, 0:128 * len(gt)], scale_fm[:, c:c + 1], ALU.mult)
            else:
                kb.copy("dve", dsl, p[:, 0:128 * len(gt)])


def load_weight_strip(kb, wdram, oc, kchunks, stage_f32, dst_bf16, wname):
    ap = wdram.rearrange("(c p) o -> p c o", p=128)[:, :, 128 * oc:128 * (oc + 1)]
    kb.dma("sp", stage_f32[:, 0:kchunks, :], ap)
    kb.cast(dst_bf16[:, 0:kchunks, :], stage_f32[:, 0:kchunks, :])




def epilogue_fm_group(kb, st, x, yfm, pt, tmp, junk, t0, ntile, ytm):
    for ts4 in range(ntile):
        for cg in range(2):
            for k in range(4):
                c = cg * 4 + k
                kb.trans(pt[:, 128 * k:128 * (k + 1)],
                         yfm[:, c, 128 * ts4:128 * (ts4 + 1)], st["identf"])
            kb.copy("dve", ytm[:, 0, 512 * cg:512 * (cg + 1)], pt[:, 0:512])
        r2, b2 = ln_stats(kb, ytm, [0], 1e-5, junk, name="lno")
        t = t0 + ts4
        kb.act("act", tmp.full, ytm[:, 0, :], AF.Identity,
               bias=b2[:, 0:1], scale=r2[:, 0:1])
        kb.tt_("dve", x[:, t, :], x[:, t, :], tmp.full, ALU.add)


def emit_ffn_ternary(kb, st, P):
    x = st["x"]
    identb = st["identb"]
    junk = st["junk"]
    TtT = 4
    Tcols = 128 * TtT
    HCH = 32
    nc = kb.nc

    caches = {}
    for wname, kch, noc in (("w1T", NC, HCH), ("w2T", NC, HCH), ("h1T", HCH, HCH),
                            ("h2T", HCH, HCH), ("w3T", HCH, NC)):
        h = nc.dram_tensor(kb.uid("wc_" + wname), [noc, 128, kch, 128], BF16,
                           kind="Internal")
        caches[wname] = (kb.track_dram(h), kch, noc)

    with kb.phase():
        s1 = kb.sb("s1", [128, HCH], F32); kb.dma("sp", s1.full, P["s1r"])
        s2 = kb.sb("s2", [128, HCH], F32); kb.dma("sp", s2.full, P["s2r"])
        sh1 = kb.sb("sh1", [128, HCH], F32); kb.dma("sp", sh1.full, P["sh1r"])
        sh2 = kb.sb("sh2", [128, HCH], F32); kb.dma("sp", sh2.full, P["sh2r"])
        s3 = kb.sb("s3", [128, NC], F32); kb.dma("sp", s3.full, P["s3r"])

        xn_fm = kb.sb("xn_fm", [128, NC, S], BF16)
        tmp = kb.sb("lntmp", [128, D], BF16)
        ptile = kb.ps("ptile", [128, D], BF16)
        r, b = ln_stats(kb, x, list(range(NT)), 1e-5, junk)
        ln_transpose_fm(kb, xn_fm, x, list(range(NT)), r, b, identb, tmp, ptile)

        pp = [kb.ps("pmm0", [128, Tcols]), kb.ps("pmm1", [128, Tcols])]
        pt = kb.ps("ptr", [128, 512], F32)
        wbufs = [kb.sb("wbuf", [128, HCH, 128], BF16) for _ in range(2)]
        stage0 = kb.sb("wstg", [128, NC, 128], F32)
        g_a = kb.sb("g_a", [128, HCH, Tcols], BF16)
        g_b = kb.sb("g_b", [128, HCH, Tcols], BF16)
        a_t = kb.sb("a_t", [128, Tcols], BF16)
        yfm = kb.sb("yfm", [128, NC, 512], F32)
        ytm = kb.sb("ytm", [128, 1, D], F32)
        sctr = [0]
        queues = ("sp", "act")

        def wstrip(wname, oc, first):
            wc, kch, noc = caches[wname]
            i = sctr[0] % 2
            q = queues[i]
            sctr[0] += 1
            wb = wbufs[i]
            stg = stage0
            if first:
                ap = P[wname].rearrange("(c p) o -> p c o", p=128)
                for q0 in range(0, kch, NC):
                    kb.dma(q, R(stg, stg.h[:, 0:NC, :]),
                           ap[:, q0:q0 + NC, 128 * oc:128 * (oc + 1)])
                    kb.cast(R(wb, wb.h[:, q0:q0 + NC, :]), R(stg, stg.h[:, 0:NC, :]),
                            engines=("dve", "act"))
                kb.dma("pool", R(wc, wc.h[oc, :, :, :]), R(wb, wb.h[:, 0:kch, :]))
            else:
                kb.dma(q, R(wb, wb.h[:, 0:kch, :]), R(wc, wc.h[oc, :, :, :]))
            return wb

        for tt in range(S // Tcols):
            first = tt == 0
            tok = slice(Tcols * tt, Tcols * (tt + 1))
            for oc in range(HCH):
                wb = wstrip("w1T", oc, first)
                p = pp[0]
                kb.mm(p.full, [(R(wb, wb.h[:, c, :]), xn_fm[:, c, tok]) for c in range(NC)])
                kb.act("act", a_t.full, p.full, AF.Silu, scale=s1[:, oc:oc + 1])
                wb = wstrip("w2T", oc, first)
                p2 = pp[1]
                kb.mm(p2.full, [(R(wb, wb.h[:, c, :]), xn_fm[:, c, tok]) for c in range(NC)])
                kb.stt("dve", g_a[:, oc, :], p2.full, s2[:, oc:oc + 1], a_t.full,
                       ALU.mult, ALU.mult)
            for oc in range(HCH):
                wb = wstrip("h1T", oc, first)
                p = pp[oc % 2]
                kb.mm(p.full, [(R(wb, wb.h[:, c, :]), g_a[:, c, :]) for c in range(HCH)])
                kb.act("act", g_b[:, oc, :], p.full, AF.Gelu, scale=sh1[:, oc:oc + 1])
            for oc in range(HCH):
                wb = wstrip("h2T", oc, first)
                p = pp[oc % 2]
                kb.mm(p.full, [(R(wb, wb.h[:, c, :]), g_b[:, c, :]) for c in range(HCH)])
                kb.act("act", g_a[:, oc, :], p.full, AF.Gelu, scale=sh2[:, oc:oc + 1])
            for oc in range(NC):
                wb = wstrip("w3T", oc, first)
                p = pp[oc % 2]
                kb.mm(p.full, [(R(wb, wb.h[:, c, :]), g_a[:, c, :]) for c in range(HCH)])
                kb.ts_("dve", yfm[:, oc, :], p.full, s3[:, oc:oc + 1], ALU.mult)
            epilogue_fm_group(kb, st, x, yfm, pt, tmp, junk, tt * TtT, TtT, ytm)



def emit_attn_causal(kb, st, P, rope=False, bias=False):
    x = st["x"]
    identb = st["identb"]
    junk = st["junk"]
    H, DH = 16, 64
    SCL = 0.125

    with kb.phase():
        if bias:
            bo = kb.sb("bo", [128, NC], F32)
            kb.dma("sp", bo.full, P["bo_r"])
        Q = kb.sb("Qfm", [128, NC, S], BF16)
        Kt = kb.sb("Kfm", [128, NC, S], BF16)
        V = kb.sb("Vaug", [128, NT, H, DH], BF16)

        with kb.phase():
            xn_fm = kb.sb("xn_fm", [128, NC, S], BF16)
            tmp = kb.sb("lntmp", [128, D], BF16)
            ptile = kb.ps("ptile", [128, D], BF16)
            r, b = ln_stats(kb, x, list(range(NT)), 1e-5, junk)
            ln_transpose_fm(kb, xn_fm, x, list(range(NT)), r, b, identb, tmp, ptile)

            stage = kb.sb("wstg", [128, 4, 128], F32)
            wbuf = kb.sb("wbuf", [128, NC, 128], BF16)
            pq = kb.ps("pq", [128, 512])
            wap = P["wqkvT"].rearrange("(c p) o -> p c o", p=128)

            wbufs2 = [wbuf, kb.sb("wbufB", [128, NC, 128], BF16)]
            pqs = [pq, kb.ps("pqB", [128, 512])]
            pctr = [0]

            def pstrip(ocol):
                i = pctr[0] % 2
                pctr[0] += 1
                q = ("sp", "act")[i]
                wb = wbufs2[i]
                for hh in range(2):
                    kb.dma(q, R(stage, stage.h[:, :, :]),
                           wap[:, 4 * hh:4 * (hh + 1), 128 * ocol:128 * (ocol + 1)])
                    kb.cast(R(wb, wb.h[:, 4 * hh:4 * (hh + 1), :]),
                            R(stage, stage.h[:, :, :]), engines=("dve", "pool"))
                return wb, pqs[i]

            def proj_fm(dst, ocol0, scl):
                for oc in range(NC):
                    wb, p = pstrip(ocol0 + oc)
                    for qc in range(4):
                        kb.mm(p.full, [(R(wb, wb.h[:, c, :]),
                                        xn_fm[:, c, 512 * qc:512 * (qc + 1)])
                                       for c in range(NC)])
                        kb.act("act", dst[:, oc, 512 * qc:512 * (qc + 1)], p.full,
                               AF.Identity, scale=scl)

            proj_fm(Q, 0, 0.125)
            proj_fm(Kt, 8, 1.0)
            for oc in range(NC):
                wb, p = pstrip(16 + oc)
                for kt in range(NT):
                    pv = R(p, p.h[:, 0:128])
                    kb.mm(pv, [(xn_fm[:, c, 128 * kt:128 * (kt + 1)],
                                R(wb, wb.h[:, c, :])) for c in range(NC)])
                    kb.copy("dve", V[:, kt, 2 * oc:2 * oc + 2, 0:DH],
                            R(p, p.h[:, 0:128].rearrange("p (h d) -> p h d", h=2)))

        if rope:
            with kb.phase():
                cosfm = kb.sb("cosfm", [128, S], BF16)
                sinfm = kb.sb("sinfm", [128, S], BF16)
                rt = kb.sb("rt", [128, 64], BF16)
                kb.dma("sp", cosfm.full, P["cos_fm"])
                kb.dma("sp", sinfm.full, P["sin_fm"])
                kb.dma("sp", rt.full, P["rt"])
                prs = [kb.ps("pr", [128, 512]) for _ in range(2)]
                rss = [kb.sb("ropes", [128, 512], BF16) for _ in range(2)]
                rcs = [kb.sb("ropec", [128, 512], BF16) for _ in range(2)]
                rctr = 0
                for dst in (Q, Kt):
                    for h in range(H):
                        ch, off = h // 2, 64 * (h % 2)
                        sl = slice(off, off + 64)
                        for qc in range(4):
                            pr, rs, rc = prs[rctr % 2], rss[rctr % 2], rcs[rctr % 2]
                            rctr += 1
                            cols = slice(512 * qc, 512 * (qc + 1))
                            src = R(dst, dst.h[sl, ch, cols])
                            kb.mm(R(pr, pr.h[sl, :]), [(R(rt, rt.h[sl, :]), src)])
                            kb.tt_("dve", R(rs, rs.h[sl, :]), R(pr, pr.h[sl, :]),
                                   R(sinfm, sinfm.h[sl, cols]), ALU.mult)
                            kb.tt_("pool", R(rc, rc.h[sl, :]), src,
                                   R(cosfm, cosfm.h[sl, cols]), ALU.mult)
                            kb.tt_("dve", src, R(rc, rc.h[sl, :]), R(rs, rs.h[sl, :]), ALU.add)

        mask01 = kb.sb("mask01", [128, 896], BF16)
        kb.dma("sp", mask01.full, P["mask01"])
        wo = kb.sb("wo", [128, NC, D], BF16)
        with kb.phase():
            wstg2 = kb.sb("wstg2", [128, 4, 128], F32)
            woap = P["woT"].rearrange("(c p) o -> p c o", p=128)
            for oc in range(NC):
                for hh in range(2):
                    kb.dma("sp", wstg2.full,
                           woap[:, 4 * hh:4 * (hh + 1), 128 * oc:128 * (oc + 1)])
                    kb.cast(wo[:, 4 * hh:4 * (hh + 1), 128 * oc:128 * (oc + 1)], wstg2.full)
        NB = 3
        ctxq = kb.sb("ctxq", [128, NC, 512], BF16)
        pbufs = [kb.sb("pbuf", [128, 512], BF16) for _ in range(2)]
        vscs = [kb.sb("vsc", [128, DH + 1], BF16) for _ in range(2)]
        mus = [kb.sb("mu", [128, 1], F32) for _ in range(2)]
        rvs = [kb.sb("rv", [128, 1], F32) for _ in range(2)]
        dens = kb.sb("dens", [1, 512], BF16)
        recb = kb.sb("recb", [64, 512], BF16)
        onesf = kb.sb("onesf", [1, 64], BF16)
        kb.memset("dve", onesf.full, 1.0)
        yfm = kb.sb("yfm_a", [128, NC, 256], F32)
        tmp2 = junk
        ytm = kb.sb("ytm_a", [128, 1, D], F32)

        psts = [kb.ps("pst", [128, 512]) for _ in range(NB)]
        pcxs = [kb.ps("pcx", [65, 512]) for _ in range(2)]
        pbc = kb.ps("pbc", [64, 512])
        pyo = kb.ps("pyo", [128, 512])
        pt = kb.ps("pte", [128, 512], F32)

        for qc in range(4):
            qcols = slice(512 * qc, 512 * (qc + 1))
            for h in range(H):
                ch, off = h // 2, 64 * (h % 2)
                pcx = pcxs[h % 2]
                nkt = 4 * qc + 4
                for kt in range(nkt):
                    pst = psts[kt % NB]
                    i = kt % 2
                    pbuf, vsc, mu, rv = pbufs[i], vscs[i], mus[i], rvs[i]
                    kb.mm(pst.full, [(R(Kt, Kt.h[off:off + 64, ch, 128 * kt:128 * (kt + 1)]),
                                      R(Q, Q.h[off:off + 64, ch, qcols]))])
                    kb.red(mu.full, pst.full, ALU.max, negate=True)
                    kb.act("act", pbuf.full, pst.full, AF.Exp, bias=mu.full)
                    kb.act("act", rv.full, mu.full, AF.Exp, scale=-1.0)
                    j = kt - 4 * qc
                    if j >= 0:
                        kb.tt_("dve", pbuf.full, pbuf.full,
                               R(mask01, mask01.h[:, 384 - 128 * j:896 - 128 * j]), ALU.mult)
                    kb.act("act", R(vsc, vsc.h[:, 0:DH]), V[:, kt, h, :],
                           AF.Identity, scale=rv.full)
                    kb.act("act", R(vsc, vsc.h[:, DH:DH + 1]), rv.full, AF.Identity)
                    kb.mm(pcx.full, [(vsc.full, pbuf.full)],
                          start=(kt == 0), stop=(kt == nkt - 1))
                kb.copy("dve", dens.full, R(pcx, pcx.h[64:65, :]))
                kb.act("act", dens.full, dens.full, AF.Ln)
                kb.act("act", dens.full, dens.full, AF.Exp, scale=-1.0)
                kb.mm(pbc.full, [(onesf.full, dens.full)])
                kb.copy("dve", recb.full, pbc.full)
                kb.tt_("dve", R(ctxq, ctxq.h[off:off + 64, ch, :]),
                       R(pcx, pcx.h[0:64, :]), recb.full, ALU.mult)
            for tg in range(2):
                for oc in range(NC):
                    kb.mm(R(pyo, pyo.h[:, 0:256]),
                          [(R(wo, wo.h[:, c, 128 * oc:128 * (oc + 1)]),
                            ctxq[:, c, 256 * tg:256 * (tg + 1)]) for c in range(NC)])
                    if bias:
                        kb.ts_("dve", yfm[:, oc, :], R(pyo, pyo.h[:, 0:256]),
                               bo[:, oc:oc + 1], ALU.add)
                    else:
                        kb.copy("dve", yfm[:, oc, :], R(pyo, pyo.h[:, 0:256]))
                epilogue_fm_group(kb, st, x, yfm, pt, tmp2, junk, 4 * qc + 2 * tg, 2, ytm)





def emit_moe(kb, st, P):
    nc = kb.nc
    x = st["x"]
    junk = st["junk"]
    identf = st["identf"]
    E = 8
    HCH = 16

    with kb.phase():
        scale_fm = kb.sb("mscale", [128, NC], F32)
        kb.dma("sp", scale_fm.full, P["scale_r"])
        ctr_tm = kb.sb("ctr_tm", [8, D], F32)
        kb.dma("sp", ctr_tm.full, P["ctr"])
        ctr_fm = kb.sb("ctr_fm", [128, NC, E], F32)
        kb.dma("sp", ctr_fm.full, P["ctr_fm"])
        ny = kb.sb("ny", [8, 1], F32)
        kb.act("act", junk[0:8, 0:D], ctr_tm.full, AF.Square, accum=ny.full)
        kb.ts_("dve", ny.full, ny.full, -0.5, ALU.mult)
        nyr = kb.sb("nyr", [1, 8], F32)
        onesc = kb.sb("onesc", [1, 128], F32)
        kb.memset("dve", onesc.full, 1.0)

        rL, bL = ln_stats(kb, x, list(range(NT)), 1e-5, junk, name="lnm")

        xn_fm = kb.sb("mxn_fm", [128, NC, S], BF16)
        masks = kb.sb("masks", [128, E, NT], F32)
        tmpa = kb.sb("tmpa", [128, D], F32)
        tmpb = kb.sb("tmpb", [128, D], F32)
        xnf = kb.sb("xnf", [128, NC, 128], F32)
        rmsq = kb.sb("rmsq", [128, 1], F32)
        rstd = kb.sb("rstd", [128, 1], F32)
        mx = kb.sb("mx", [128, 1], F32)
        ge = kb.sb("ge", [128, 1], F32)
        avail = kb.sb("avail", [128, 1], F32)

        with kb.phase():
            ptr = kb.ps("ptr_m", [128, D], F32)
            psc = kb.ps("psc", [128, 8], F32)
            pny = kb.ps("pny", [1, 8], F32)
            kb.trans(pny.full, ny.full, R(identf, identf.h[0:8, 0:8]))
            kb.copy("dve", nyr.full, pny.full)
            for t in range(NT):
                kb.act("act", tmpa.full, x[:, t, :], AF.Identity,
                       bias=bL[:, t:t + 1], scale=rL[:, t:t + 1])
                kb.act("act", junk[:, 0:D], tmpa.full, AF.Square, accum=rmsq.full)
                kb.ts_("dve", rmsq.full, rmsq.full, 1.0 / D, ALU.mult)
                kb.act("act", rmsq.full, rmsq.full, AF.Sqrt, bias=kb.constf(1e-6))
                kb.recip(rstd.full, rmsq.full)
                kb.act("act", tmpb.full, tmpa.full, AF.Identity, scale=rstd.full)
                for c in range(NC):
                    kb.trans(R(ptr, ptr.h[:, 128 * c:128 * (c + 1)]),
                             tmpb[:, 128 * c:128 * (c + 1)], identf)
                for c in range(NC):
                    kb.ts_("dve", xnf[:, c, :], R(ptr, ptr.h[:, 128 * c:128 * (c + 1)]),
                           scale_fm[:, c:c + 1], ALU.mult)
                kb.cast(R(xn_fm, xn_fm.h[:, :, 128 * t:128 * (t + 1)]), xnf.full,
                        engines=("dve",))
                kb.mm(psc.full, [(onesc.full, nyr.full)], stop=False)
                kb.mm(psc.full, [(xnf[:, c, :], ctr_fm[:, c, :]) for c in range(NC)],
                      start=False)
                kb.red(mx.full, psc.full, ALU.max)
                kb.memset("dve", avail.full, 1.0)
                for e in range(E):
                    kb.ts_("dve", ge.full, R(psc, psc.h[:, e:e + 1]), mx.full, ALU.is_ge)
                    kb.tt_("dve", R(masks, masks.h[:, e, t:t + 1]), ge.full, avail.full,
                           ALU.mult)
                    kb.tt_("dve", avail.full, avail.full,
                           R(masks, masks.h[:, e, t:t + 1]), ALU.subtract)

        aa = kb.sb("aa", [128, HCH, 1024], BF16)
        y_acc = kb.sb("y_acc", [128, NT // 2, D], F32)
        stage = kb.sb("mstg", [128, HCH, 128], F32)
        wbufs = [kb.sb("mwbuf", [128, HCH, 128], BF16) for _ in range(2)]
        sg = kb.sb("sg", [128, 512], BF16)
        upap = P["up_w"].rearrange("e (c p) o -> p e c o", p=128)
        dnap = P["down_w"].rearrange("e (c p) o -> p e c o", p=128)
        pmm = kb.ps("pmm_m", [128, 512])
        pmt = kb.ps("pmt_m", [128, 512], F32)
        upc = kb.track_dram(nc.dram_tensor(kb.uid("wc_up"), [E, 2 * HCH, 128, NC, 128],
                                           BF16, kind="Internal"))
        dnc = kb.track_dram(nc.dram_tensor(kb.uid("wc_dn"), [E, NC, 128, HCH, 128],
                                           BF16, kind="Internal"))
        mctr = [0]
        mqueues = ("sp", "act")

        def mstrip(which, e, oc, kch, first):
            i = mctr[0] % 2
            q = mqueues[i]
            mctr[0] += 1
            wb = wbufs[i]
            wc = upc if which == 0 else dnc
            src = upap if which == 0 else dnap
            if first:
                kb.dma(q, R(stage, stage.h[:, 0:kch, :]),
                       src[:, e, :, 128 * oc:128 * (oc + 1)])
                kb.cast(R(wb, wb.h[:, 0:kch, :]), R(stage, stage.h[:, 0:kch, :]),
                        engines=("dve", "act"))
                kb.dma("pool", R(wc, wc.h[e, oc, :, :, :]), R(wb, wb.h[:, 0:kch, :]))
            else:
                kb.dma(q, R(wb, wb.h[:, 0:kch, :]), R(wc, wc.h[e, oc, :, :, :]))
            return wb

        for half in range(2):
            first = half == 0
            tok0 = 1024 * half
            t0 = NT // 2 * half
            for tloc in range(NT // 2):
                kb.memset("dve", y_acc[:, tloc, :], 0.0)
            for e in range(E):
                for oc in range(2 * HCH):
                    wb = mstrip(0, e, oc, NC, first)
                    for tq in range(2):
                        cols = slice(tok0 + 512 * tq, tok0 + 512 * (tq + 1))
                        lcols = slice(512 * tq, 512 * (tq + 1))
                        kb.mm(pmm.full, [(R(wb, wb.h[:, c, :]), xn_fm[:, c, cols])
                                         for c in range(NC)])
                        if oc < HCH:
                            kb.act("act", R(aa, aa.h[:, oc, lcols]), pmm.full, AF.Identity)
                        else:
                            kb.act("act", sg.full, pmm.full, AF.Silu)
                            kb.tt_("dve", R(aa, aa.h[:, oc - HCH, lcols]),
                                   R(aa, aa.h[:, oc - HCH, lcols]), sg.full, ALU.mult)
                for od in range(NC):
                    wb = mstrip(1, e, od, HCH, first)
                    for tq in range(2):
                        lcols = slice(512 * tq, 512 * (tq + 1))
                        kb.mm(pmm.full, [(R(wb, wb.h[:, c, :]), R(aa, aa.h[:, c, lcols]))
                                         for c in range(HCH)])
                        kb.copy("dve", tmpa[:, 0:512], pmm.full)
                        for blk in range(4):
                            kb.trans(R(pmt, pmt.h[:, 128 * blk:128 * (blk + 1)]),
                                     tmpa[:, 128 * blk:128 * (blk + 1)], identf)
                        for blk in range(4):
                            tloc = 4 * tq + blk
                            tglob = t0 + tloc
                            kb.stt("dve", R(y_acc, y_acc.h[:, tloc, 128 * od:128 * (od + 1)]),
                                   R(pmt, pmt.h[:, 128 * blk:128 * (blk + 1)]),
                                   R(masks, masks.h[:, e, tglob:tglob + 1]),
                                   R(y_acc, y_acc.h[:, tloc, 128 * od:128 * (od + 1)]),
                                   ALU.mult, ALU.add)
            for tloc in range(NT // 2):
                t = t0 + tloc
                kb.act("act", tmpa.full, x[:, t, :], AF.Identity,
                       bias=bL[:, t:t + 1], scale=rL[:, t:t + 1])
                kb.tt_("dve", tmpb.full, R(y_acc, y_acc.h[:, tloc, :]), tmpa.full, ALU.add)
                tb = kb.sbc("moey", [128, 1, D], F32)
                kb.copy("dve", tb[:, 0, :], tmpb.full)
                r2, b2 = ln_stats(kb, tb, [0], 1e-5, junk, name="lnmo")
                kb.act("act", tmpa.full, tb[:, 0, :], AF.Identity,
                       bias=b2[:, 0:1], scale=r2[:, 0:1])
                kb.tt_("dve", x[:, t, :], x[:, t, :], tmpa.full, ALU.add)




def emit_attn_l1(kb, st, P):
    x = st["x"]
    identb = st["identb"]
    junk = st["junk"]
    H, DH = 16, 64
    SCL = 0.125

    with kb.phase():
        bo = kb.sb("bo1", [128, NC], F32)
        kb.dma("sp", bo.full, P["bo_r"])
        Q = kb.sb("Qfm1", [128, NC, S], BF16)
        Kt = kb.sb("Kfm1", [128, NC, S], BF16)
        V = kb.sb("Vaug1", [128, NT, H, DH], BF16)
        kg = kb.sb("kg", [128, NC, NT, 1], BF16)
        Vg = kb.sb("Vg", [16, H, DH + 1], BF16)
        kb.memset("dve", Vg[:, :, DH:DH + 1], 1.0)

        with kb.phase():
            xn_fm = kb.sb("xn_fm1", [128, NC, S], BF16)
            tmp = kb.sb("lntmp1", [128, D], BF16)
            ptile = kb.ps("ptile1", [128, D], BF16)
            r, b = ln_stats(kb, x, list(range(NT)), 1e-5, junk)
            ln_transpose_fm(kb, xn_fm, x, list(range(NT)), r, b, identb, tmp, ptile)
            xn_g = kb.sb("xn_g", [128, NC, NT, 1], BF16)
            kb.copy("dve", xn_g.full,
                    R(xn_fm, xn_fm.h[:].rearrange("p c (g s) -> p c g s", s=128)[:, :, :, 0:1]))

            stage = kb.sb("wstg1", [128, 4, 128], F32)
            wbuf = kb.sb("wbuf1", [128, NC, 128], BF16)
            pq = kb.ps("pq1", [128, 512])
            wap = P["wqkvT"].rearrange("(c p) o -> p c o", p=128)

            def ldstrip(ocol):
                for hh in range(2):
                    kb.dma("sp", stage.full,
                           wap[:, 4 * hh:4 * (hh + 1), 128 * ocol:128 * (ocol + 1)])
                    kb.cast(R(wbuf, wbuf.h[:, 4 * hh:4 * (hh + 1), :]), stage.full)

            def proj_fm(dst, ocol0, scl=1.0):
                for oc in range(NC):
                    ldstrip(ocol0 + oc)
                    for qc in range(4):
                        kb.mm(pq.full, [(wbuf[:, c, :], xn_fm[:, c, 512 * qc:512 * (qc + 1)])
                                        for c in range(NC)])
                        kb.act("act", dst[:, oc, 512 * qc:512 * (qc + 1)], pq.full,
                               AF.Identity, scale=scl)

            proj_fm(Q, 0, 0.125)
            proj_fm(Kt, 8)
            kb.copy("dve", kg.full,
                    R(Kt, Kt.h[:].rearrange("p c (g s) -> p c g s", s=128)[:, :, :, 0:1]))
            for oc in range(NC):
                ldstrip(16 + oc)
                for kt in range(NT):
                    pv = R(pq, pq.h[:, 0:128])
                    kb.mm(pv, [(xn_fm[:, c, 128 * kt:128 * (kt + 1)], wbuf[:, c, :])
                               for c in range(NC)])
                    kb.copy("dve", V[:, kt, 2 * oc:2 * oc + 2, 0:DH],
                            R(pq, pq.h[:, 0:128].rearrange("p (h d) -> p h d", h=2)))
                pvg = R(pq, pq.h[0:16, 0:128])
                kb.mm(pvg, [(R(xn_g, xn_g.h[:, c, :, 0]), wbuf[:, c, :]) for c in range(NC)])
                kb.copy("dve", R(Vg, Vg.h[0:16, 2 * oc:2 * oc + 2, 0:DH]),
                        R(pq, pq.h[0:16, 0:128].rearrange("p (h d) -> p h d", h=2)))

        maskloc = kb.sb("maskloc", [128, 2, 128], BF16)
        kb.dma("sp", maskloc.full, P["maskloc"])
        wo = kb.sb("wo1", [128, NC, D], BF16)
        with kb.phase():
            wstg2 = kb.sb("wstg21", [128, NC, 128], F32)
            woap = P["woT"].rearrange("(c p) o -> p c o", p=128)
            for oc in range(NC):
                kb.dma("sp", wstg2.full, woap[:, :, 128 * oc:128 * (oc + 1)])
                kb.cast(wo[:, :, 128 * oc:128 * (oc + 1)], wstg2.full)
        NB = 3
        ctxq = kb.sb("ctxq1", [128, NC, 512], BF16)
        pbufs = [kb.sb("pbuf1", [128, 512], BF16) for _ in range(NB)]
        vscs = [kb.sb("vsc1", [128, DH + 1], BF16) for _ in range(NB)]
        vscg = kb.sb("vscg", [16, DH + 1], BF16)
        mus = [kb.sb("mu1", [128, 1], F32) for _ in range(NB)]
        rvs = [kb.sb("rv1", [128, 1], F32) for _ in range(NB)]
        dens = kb.sb("dens1", [1, 512], BF16)
        recb = kb.sb("recb1", [64, 512], BF16)
        onesf = kb.sb("onesf1", [1, 64], BF16)
        kb.memset("dve", onesf.full, 1.0)
        tmpg = pbufs[1]
        yfm = kb.sb("yfm_1", [128, NC, 256], F32)
        tmp2 = junk
        ytm = kb.sb("ytm_1", [128, 1, D], F32)

        psts = [kb.ps("pst1", [128, 512]) for _ in range(NB)]
        pcx = kb.ps("pcx1", [65, 512])
        pcg = kb.ps("pcg1", [65, 512])
        pbc = kb.ps("pbc1", [64, 512])
        pyo = kb.ps("pyo1", [128, 512])
        pt = kb.ps("pte1", [128, 512], F32)

        def normalize(pnum, dst, add, off=0):
            kb.copy("dve", dens.full, R(pnum, pnum.h[64:65, :]))
            kb.act("act", dens.full, dens.full, AF.Ln)
            kb.act("act", dens.full, dens.full, AF.Exp, scale=-1.0)
            kb.mm(pbc.full, [(onesf.full, dens.full)])
            kb.copy("dve", recb.full, pbc.full)
            if add:
                kb.tt_("dve", R(tmpg, tmpg.h[off:off + 64, :]), R(pnum, pnum.h[0:64, :]),
                       recb.full, ALU.mult)
            else:
                kb.tt_("dve", dst, R(pnum, pnum.h[0:64, :]), recb.full, ALU.mult)

        bctr = [0]
        for qc in range(4):
            qcols = slice(512 * qc, 512 * (qc + 1))
            for h in range(H):
                ch, off = h // 2, 64 * (h % 2)
                for j in range(4):
                    qt = 4 * qc + j
                    kts = [kt for kt in (qt - 1, qt) if kt >= 0]
                    for i, kt in enumerate(kts):
                        bi = bctr[0] % NB
                        bctr[0] += 1
                        pst, pbuf, vsc, mu, rv = psts[bi], pbufs[bi], vscs[bi], mus[bi], rvs[bi]
                        pslice = R(pst, pst.h[:, 0:128])
                        kb.mm(pslice, [(R(Kt, Kt.h[off:off + 64, ch, 128 * kt:128 * (kt + 1)]),
                                        R(Q, Q.h[off:off + 64, ch, 128 * qt:128 * (qt + 1)]))])
                        kb.red(mu.full, pslice, ALU.max, negate=True)
                        kb.act("act", R(pbuf, pbuf.h[:, 0:128]), pslice, AF.Exp,
                               bias=mu.full)
                        mj = 0 if kt < qt else 1
                        kb.tt_("dve", R(pbuf, pbuf.h[:, 0:128]), R(pbuf, pbuf.h[:, 0:128]),
                               maskloc[:, mj, :], ALU.mult)
                        kb.act("act", rv.full, mu.full, AF.Exp, scale=-1.0)
                        kb.act("act", R(vsc, vsc.h[:, 0:DH]), V[:, kt, h, :],
                               AF.Identity, scale=rv.full)
                        kb.act("act", R(vsc, vsc.h[:, DH:DH + 1]), rv.full, AF.Identity)
                        kb.mm(R(pcx, pcx.h[:, 128 * j:128 * (j + 1)]),
                              [(vsc.full, R(pbuf, pbuf.h[:, 0:128]))],
                              start=(i == 0), stop=(i == len(kts) - 1))
                normalize(pcx, R(ctxq, ctxq.h[off:off + 64, ch, :]), add=False)
                pst, pbuf, mu, rv = psts[0], pbufs[0], mus[0], rvs[0]
                pg = R(pst, pst.h[0:16, 0:512])
                kb.mm(pg, [(R(kg, kg.h[off:off + 64, ch, :, 0]),
                            R(Q, Q.h[off:off + 64, ch, qcols]))])
                kb.red(R(mu, mu.h[0:16, :]), pg, ALU.max, negate=True)
                kb.act("act", R(pbuf, pbuf.h[0:16, :]), pg, AF.Exp,
                       bias=R(mu, mu.h[0:16, :]))
                kb.act("act", R(rv, rv.h[0:16, :]), R(mu, mu.h[0:16, :]), AF.Exp, scale=-1.0)
                kb.act("act", vscg.full, R(Vg, Vg.h[0:16, h, :]), AF.Identity,
                       scale=R(rv, rv.h[0:16, :]))
                kb.mm(pcg.full, [(vscg.full, R(pbuf, pbuf.h[0:16, :]))])
                normalize(pcg, None, add=True, off=off)
                kb.tt_("dve", R(ctxq, ctxq.h[off:off + 64, ch, :]),
                       R(ctxq, ctxq.h[off:off + 64, ch, :]),
                       R(tmpg, tmpg.h[off:off + 64, :]), ALU.add)
            for tg in range(2):
                for oc in range(NC):
                    kb.mm(R(pyo, pyo.h[:, 0:256]),
                          [(R(wo, wo.h[:, c, 128 * oc:128 * (oc + 1)]),
                            ctxq[:, c, 256 * tg:256 * (tg + 1)]) for c in range(NC)])
                    kb.ts_("dve", yfm[:, oc, :], R(pyo, pyo.h[:, 0:256]),
                           bo[:, oc:oc + 1], ALU.add)
                epilogue_fm_group(kb, st, x, yfm, pt, tmp2, junk, 4 * qc + 2 * tg, 2, ytm)




def emit_attn_yat(kb, st, P):
    x = st["x"]
    identb = st["identb"]
    identf = st["identf"]
    junk = st["junk"]
    H, DH = 16, 64
    LNC3 = float(np.log(np.sqrt(1024.0) / np.log(1025.0)))

    with kb.phase():
        q_fm = kb.sb("q3", [128, NC, S], BF16)
        k_fm = kb.sb("k3", [128, NC, S], BF16)
        V_tm = kb.sb("v3", [128, NT, D], BF16)
        ones1 = kb.sb("ones1", [1, 128], F32)
        kb.memset("dve", ones1.full, 1.0)
        ones64 = kb.sb("ones64", [128, 1], F32)
        kb.memset("dve", ones64.full, 1.0)
        onesb = kb.sb("onesb1", [1, 128], BF16)
        kb.memset("dve", onesb.full, 1.0)
        bias_map = {}
        for nm in ("bq", "bk", "bv", "bo"):
            t_ = kb.sb(nm + "3", [128, NC], F32)
            kb.dma("sp", t_.full, P[nm + "_r"])
            bias_map[nm] = t_
        scb = {}
        pbx = None
        for nm in ("aq", "ak", "av"):
            a1 = kb.sb(nm, [1, 1], F32)
            kb.dma("sp", a1.full, P[nm])
            kb.act("act", a1.full, a1.full, AF.Exp, scale=LNC3)
            sb_ = kb.sb(nm + "b", [128, 1], F32)
            scb[nm] = (a1, sb_)

        L3DBGP = int(os.environ.get("L3DBG", "0"))
        with kb.phase():
            xn_fm = kb.sb("xn3", [128, NC, S], BF16)
            nxb_all = kb.sb("nxb3", [128, NT, 128], BF16)
            nx = kb.sb("nx3", [128, NT], F32)
            with kb.phase():
                pb1 = kb.ps("pb13", [128, 128])
                for nm in ("aq", "ak", "av"):
                    a1, sb_ = scb[nm]
                    kb.mm(R(pb1, pb1.h[:, 0:1]), [(ones1.full, a1.full)])
                    kb.copy("dve", sb_.full, R(pb1, pb1.h[:, 0:1]))
                tmp = kb.sb("lntmp3", [128, D], BF16)
                ptile = kb.ps("ptile3", [128, D], BF16)
                r, b = ln_stats(kb, x, list(range(NT)), 1e-5, junk)
                for t in range(NT):
                    kb.act("act", tmp.full, x[:, t, :], AF.Identity,
                           bias=b[:, t:t + 1], scale=r[:, t:t + 1])
                    kb.act("act", junk[:, 0:D], tmp.full, AF.Square, accum=nx[:, t:t + 1])
                ln_transpose_fm(kb, xn_fm, x, list(range(NT)), r, b, identb, tmp, ptile)
                nxrow = kb.sb("nxrow", [1, 128], F32)
                pnx = kb.ps("pnx3", [1, 128])
                for t in range(NT if L3DBGP < 4 else 0):
                    kb.trans(pnx.full, nx[:, t:t + 1], identf)
                    kb.copy("dve", nxrow.full, pnx.full)
                    kb.mm(pb1.full, [(ones1.full, nxrow.full)])
                    kb.copy("dve", R(nxb_all, nxb_all.h[:, t, :]), pb1.full)
                if L3DBGP >= 4:
                    kb.memset("dve", nxb_all.full, 1.0)
            ptile = kb.ps("ptile3b", [128, D], BF16)

            stage = kb.sb("wstg3", [128, 2, 128], F32)
            wbuf = kb.sb("wbuf3", [128, NC, 128], BF16)
            pd = kb.ps("pd3", [128, 512])

            sq_sb = kb.sb("sq3", [128, 256], F32)
            junkf = kb.sb("junkf3", [128, 256], BF16)
            knw_row = kb.sb("knwrow", [1, 128], F32)
            den_sb = kb.sb("den3", [128, 256], F32)
            nxbf = kb.sb("nxbf3", [128, 256], F32)
            nww = kb.sb("nw3", [128, NC], F32)

            if L3DBGP in (3, 4) or L3DBGP >= 10:
                pass
            if L3DBGP >= 10:
                kb.memset("dve", q_fm.full, 0.01)
                kb.memset("dve", k_fm.full, 0.01)
                kb.memset("dve", V_tm.full, 0.01)
            onesbc = kb.sb("onesbc", [128, 1], BF16)
            kb.memset("dve", onesbc.full, 1.0)
            pnw = kb.ps("pnw3", [1, 128])
            for pj, wkey, bkey, akey in (() if L3DBGP in (3, 4) else (
                (0, "wq", "bq", "aq"), (1, "wk", "bk", "ak"),
                (2, "wv", "bv", "av"))):
                wap = P[wkey].rearrange("(c p) o -> p c o", p=128)
                bv_ = bias_map[bkey]
                scv = scb[akey][1]
                for oc in range(NC):
                    for hh in range(4):
                        kb.dma("pool", stage.full,
                               wap[:, 2 * hh:2 * (hh + 1), 128 * oc:128 * (oc + 1)])
                        kb.cast(wbuf[:, 2 * hh:2 * (hh + 1), :], stage.full,
                                engines=("dve", "act"))
                        if L3DBGP == 10:
                            continue
                        kb.tt_("dve", junkf.full,
                               R(stage, stage.h[:].rearrange("p a b -> p (a b)")),
                               R(stage, stage.h[:].rearrange("p a b -> p (a b)")),
                               ALU.mult)
                        jv = junkf.h[:].rearrange("p (a b) -> p a b", a=2)
                        kb.mm(pnw.full, [(onesbc.full, R(junkf, jv[:, a_, :]))
                                         for a_ in range(2)],
                              start=(hh == 0), stop=(hh == 3))
                    if L3DBGP == 10:
                        kb.memset("dve", nww[:, oc:oc + 1], 1.0)
                    else:
                        kb.copy("dve", R(knw_row, knw_row.h[:, 0:128]), pnw.full)
                        kb.trans(R(pd, pd.h[:, 0:1]), R(knw_row, knw_row.h[:, 0:128]),
                                 R(identf, identf.h[0:1, 0:1]))
                        kb.copy("dve", nww[:, oc:oc + 1], R(pd, pd.h[:, 0:1]))
                    pdv = R(pd, pd.h[:, 0:256])
                    for tq in range(8 if L3DBGP not in (11,) else 0):
                        cols = slice(256 * tq, 256 * (tq + 1))
                        kb.mm(pdv, [(wbuf[:, c, :], xn_fm[:, c, cols])
                                    for c in range(NC)])
                        kb.act("act", sq_sb.full, pdv, AF.Square)
                        if L3DBGP == 20:
                            continue
                        kb.copy("dve", nxbf.full,
                                R(nxb_all, nxb_all.h[:].rearrange(
                                    "p t n -> p (t n)")[:, cols]))
                        if L3DBGP == 23:
                            continue
                        kb.ts_("dve", den_sb.full, pdv, -2.0, ALU.mult)
                        kb.tt_("dve", den_sb.full, den_sb.full, nxbf.full, ALU.add)
                        if L3DBGP == 24:
                            continue
                        kb.ts_("dve", den_sb.full, den_sb.full, nww[:, oc:oc + 1],
                               ALU.add)
                        kb.ts_("dve", den_sb.full, den_sb.full, 1e-5, ALU.add)
                        if L3DBGP == 21:
                            continue
                        kb.act("act", den_sb.full, den_sb.full, AF.Ln)
                        kb.act("act", den_sb.full, den_sb.full, AF.Exp, scale=-1.0)
                        kb.tt_("dve", sq_sb.full, sq_sb.full, den_sb.full, ALU.mult)
                        if L3DBGP == 22:
                            continue
                        if pj == 0:
                            kb.ts_("dve", R(q_fm, q_fm.h[:, oc, cols]), sq_sb.full,
                                   bv_[:, oc:oc + 1], ALU.add, s2=scv.full, op1=ALU.mult)
                        elif pj == 1:
                            kb.ts_("dve", R(k_fm, k_fm.h[:, oc, cols]), sq_sb.full,
                                   bv_[:, oc:oc + 1], ALU.add, s2=scv.full, op1=ALU.mult)
                        else:
                            kb.ts_("dve", R(junkf, junkf.h[:, 0:256]), sq_sb.full,
                                   bv_[:, oc:oc + 1], ALU.add, s2=scv.full, op1=ALU.mult)
                            for blk in range(2):
                                kb.trans(R(ptile, ptile.h[:, 128 * blk:128 * (blk + 1)]),
                                         R(junkf, junkf.h[:, 128 * blk:128 * (blk + 1)]), identb)
                            for blk in range(2):
                                t_ = 2 * tq + blk
                                kb.copy("dve", R(V_tm, V_tm.h[:, t_, 128 * oc:128 * (oc + 1)]),
                                        R(ptile, ptile.h[:, 128 * blk:128 * (blk + 1)]))

        for half in range(2):
            with kb.phase():
                ctxh = kb.sb("ctxh", [128, NC, S // 2], BF16)
                with kb.phase():
                    kn_b = kb.sb("knb3", [128, S], BF16)
                    knrow = kb.sb("knrow", [1, 512], BF16)
                    qn_t = kb.sb("qnt3", [128, NT // 2], F32)
                    ksq = kb.sb("ksq3", [128, 512], F32)
                    den2s = [kb.sb("den23", [128, S], BF16) for _ in range(2)]
                    pbufs = [kb.sb("pbuf3", [128, S], BF16) for _ in range(2)]
                    sumss = [kb.sb("sums3", [128, 1], F32) for _ in range(2)]
                    rs_s = [kb.sb("rs3", [128, 1], F32) for _ in range(2)]
                    nmxs = [kb.sb("nmx3", [128, 1], F32) for _ in range(2)]
                    ctmbs = [kb.sb("ctmb3", [128, 64], BF16) for _ in range(2)]

                    pS = kb.ps("pS3", [128, S])
                    pkn = kb.ps("pkn3", [1, 512])
                    pknb = kb.ps("pknb3", [128, 512])
                    pcx = kb.ps("pcx3", [128, 64])
                    ptr = kb.ps("ptr3", [128, 512], BF16)

                    for h in range(H):
                        ch, off = h // 2, 64 * (h % 2)
                        sl = slice(off, off + 64)
                        for qc in range(4):
                            cols = slice(512 * qc, 512 * (qc + 1))
                            kb.act("act", R(ksq, ksq.h[sl, :]),
                                   R(k_fm, k_fm.h[sl, ch, cols]), AF.Square)
                            kb.mm(pkn.full, [(R(ones64, ones64.h[sl, :]),
                                              R(ksq, ksq.h[sl, :]))])
                            kb.copy("dve", knrow.full, pkn.full)
                            kb.mm(pknb.full, [(onesb.full, knrow.full)])
                            kb.copy("dve", R(kn_b, kn_b.h[:, cols]), pknb.full)
                        for j in range(NT // 2):
                            t_ = (NT // 2) * half + j
                            kb.act("act", R(ksq, ksq.h[sl, 0:128]),
                                   R(q_fm, q_fm.h[sl, ch, 128 * t_:128 * (t_ + 1)]), AF.Square)
                            kb.mm(R(pkn, pkn.h[:, 0:128]),
                                  [(R(ones64, ones64.h[sl, :]), R(ksq, ksq.h[sl, 0:128]))])
                            kb.copy("dve", R(knrow, knrow.h[:, 0:128]), R(pkn, pkn.h[:, 0:128]))
                            kb.trans(R(ptr, ptr.h[:, 0:1]), R(knrow, knrow.h[:, 0:128]),
                                     R(identb, identb.h[0:1, 0:1]))
                            kb.copy("dve", qn_t[:, j:j + 1], R(ptr, ptr.h[:, 0:1]))
                        def scores(j):
                            qt = (NT // 2) * half + j
                            for qc in range(4):
                                kb.mm(R(pS, pS.h[:, 512 * qc:512 * (qc + 1)]),
                                      [(R(q_fm, q_fm.h[sl, ch, 128 * qt:128 * (qt + 1)]),
                                        R(k_fm, k_fm.h[sl, ch, 512 * qc:512 * (qc + 1)]))])

                        def consume(j):
                            par = j % 2
                            pbuf, rs_, ctmb = pbufs[par], rs_s[par], ctmbs[par]
                            for ktg in range(4):
                                for b4 in range(4):
                                    kt = 4 * ktg + b4
                                    kb.trans(R(ptr, ptr.h[:, 128 * b4:128 * (b4 + 1)]),
                                             pbuf[:, 128 * kt:128 * (kt + 1)], identb)
                                kb.copy("act", pbuf[:, 512 * ktg:512 * (ktg + 1)], ptr.full)
                            kb.mm(pcx.full,
                                  [(pbuf[:, 128 * kt:128 * (kt + 1)],
                                    R(V_tm, V_tm.h[:, kt, 64 * h:64 * h + 64]))
                                   for kt in range(NT)])
                            kb.ts_("dve", ctmb.full, pcx.full, rs_.full, ALU.mult)
                            kb.trans(R(ptr, ptr.h[0:64, 0:128]), ctmb.full,
                                     R(identb, identb.h[0:128, 0:128]))
                            kb.copy("dve", R(ctxh, ctxh.h[sl, ch, 128 * j:128 * (j + 1)]),
                                    R(ptr, ptr.h[0:64, 0:128]))

                        for j in range(NT // 2):
                            par = j % 2
                            den2, pbuf = den2s[par], pbufs[par]
                            sums, rs_, nmx = sumss[par], rs_s[par], nmxs[par]
                            if j == 0:
                                scores(0)
                            kb.act("act", pbuf.full, R(pS, pS.h[:, :]), AF.Square)
                            kb.stt("dve", den2.full, pbuf.full, kb.constf(-2.0), kn_b.full,
                                   ALU.mult, ALU.add)
                            if j < NT // 2 - 1:
                                scores(j + 1)
                            kb.ts_("dve", den2.full, den2.full, qn_t[:, j:j + 1],
                                   ALU.add, s2=1e-5, op1=ALU.add)
                            kb.recip(den2.full, den2.full)
                            kb.tt_("pool", den2.full, pbuf.full, den2.full, ALU.mult)
                            kb.red(nmx.full, den2.full, ALU.max, negate=True)
                            kb.act("act", pbuf.full, den2.full, AF.Exp, bias=nmx.full,
                                   accum=sums.full)
                            kb.recip(rs_.full, sums.full)
                            if j > 0:
                                consume(j - 1)
                        consume(NT // 2 - 1)

                with kb.phase():
                    wob = kb.sb("wob3", [128, NC, D], BF16)
                    stage2 = kb.sb("stg3b", [128, 4, 128], F32)
                    woap = P["wo"].rearrange("(c p) o -> p c o", p=128)
                    for oc in range(NC):
                        for hh in range(2):
                            kb.dma("sp", stage2.full,
                                   woap[:, 4 * hh:4 * (hh + 1), 128 * oc:128 * (oc + 1)])
                            kb.cast(R(wob, wob.h[:, 4 * hh:4 * (hh + 1),
                                                 128 * oc:128 * (oc + 1)]), stage2.full)
                    yfm = kb.sb("yfm3", [128, NC, 128], F32)
                    tmp2 = junk
                    ytm = kb.sb("ytm3", [128, 1, D], F32)
                    pyo = kb.ps("pyo3", [128, 512])
                    pt = kb.ps("pt3", [128, 512], F32)
                    for j in range(NT // 2):
                        for oc in range(NC):
                            kb.mm(R(pyo, pyo.h[:, 128 * (oc % 4):128 * (oc % 4) + 128]),
                                  [(R(wob, wob.h[:, c, 128 * oc:128 * (oc + 1)]),
                                    ctxh[:, c, 128 * j:128 * (j + 1)])
                                   for c in range(NC)])
                            kb.ts_("dve", yfm[:, oc, :],
                                   R(pyo, pyo.h[:, 128 * (oc % 4):128 * (oc % 4) + 128]),
                                   R(bias_map["bo"], bias_map["bo"].h[:, oc:oc + 1]), ALU.add)
                        epilogue_fm_group(kb, st, x, yfm, pt, tmp2, junk,
                                          (NT // 2) * half + j, 1, ytm)



def emit_ffn_quant(kb, st, P):
    x = st["x"]
    identb = st["identb"]
    identf = st["identf"]
    junk = st["junk"]
    QB = 128.0

    with kb.phase():
        gfm = kb.sb("qgam", [128, NC], F32); kb.dma("sp", gfm.full, P["g_r"])
        bfm = kb.sb("qbet", [128, NC], F32); kb.dma("sp", bfm.full, P["b_r"])
        ones1q = kb.sb("ones1q", [1, 128], F32)
        kb.memset("dve", ones1q.full, 1.0)
        onescq = kb.sb("onescq", [128, 1], F32)
        kb.memset("dve", onescq.full, 1.0)
        xn_fm = kb.sb("qxn", [128, NC, S], BF16)
        wq_b = kb.sb("qwq", [128, NC, D], BF16)
        stage = kb.sb("qstg", [128, D], F32)
        tmp = kb.sb("qtmp", [128, D], BF16)
        sc1 = kb.sb("qsc1", [128, 1], F32)
        rowq = kb.sb("qrow", [1, 128], F32)
        gcol = kb.sb("qgcol", [128, NC], F32)
        thr_b = kb.sb("qthr", [128, 1], F32)
        sxb = kb.sb("qsx", [128, 1], F32)
        soutb = kb.sb("qso", [128, 1], F32)
        beta_s = kb.sb("qbeta", [1, 1], F32)
        gam_s = kb.sb("qgam1", [1, 1], F32)
        tmpf = kb.sb("qtf", [128, 1024], F32)
        yfm = kb.sb("qyfm", [128, NC, 512], F32)
        ytm = kb.sb("qytm", [128, 1, D], F32)
        tmp2 = junk

        ptile = kb.ps("qptile", [128, D], BF16)
        prow = kb.ps("qprow", [1, 128])
        pcol = kb.ps("qpcol", [128, 128])
        pmm = kb.ps("qpmm", [128, 512])
        pt = kb.ps("qpt", [128, 512], F32)

        r, b = ln_stats(kb, x, list(range(NT)), 1e-3, junk, name="lnq")
        for t in range(NT):
            kb.act("act", tmp.full, x[:, t, :], AF.Identity,
                   bias=b[:, t:t + 1], scale=r[:, t:t + 1])
            for c in range(NC):
                kb.trans(R(ptile, ptile.h[:, 128 * c:128 * (c + 1)]),
                         tmp[:, 128 * c:128 * (c + 1)], identb)
            dsl = xn_fm.h[:, :, 128 * t:128 * (t + 1)]
            pview = ptile.h[:].rearrange("p (c n) -> p c n", c=NC)
            for c in range(NC):
                kb.emit("dve", lambda e, c=c: e.tensor_scalar(
                    out=dsl[:, c, :], in0=pview[:, c, :], scalar1=gfm.h[:, c:c + 1],
                    scalar2=bfm.h[:, c:c + 1], op0=ALU.mult, op1=ALU.add),
                    reads=[ptile, gfm, bfm], writes=[R(xn_fm, dsl)])

        for c in range(NC):
            kb.red(gcol[:, c:c + 1], xn_fm[:, c, :], ALU.max, absval=True)
        kb.red(sc1.full, gcol.full, ALU.max)
        kb.trans(prow.full, sc1.full, identf)
        kb.copy("dve", rowq.full, prow.full)
        kb.red(R(gam_s, gam_s.h[0:1, :]), rowq.full, ALU.max)

        wap = P["weight"].rearrange("(c p) o -> p c o", p=128)
        bs = kb.sb("qbs", [128, NC], F32)
        for c in range(NC):
            kb.dma("sp", stage.full, wap[:, c, :])
            kb.red(bs[:, c:c + 1], stage.full, ALU.add, absval=True)
        kb.red(sc1.full, bs.full, ALU.add)
        kb.mm(R(prow, prow.h[0:1, 0:1]), [(sc1.full, onescq.full)])
        kb.copy("dve", R(beta_s, beta_s.h[0:1, :]), R(prow, prow.h[0:1, 0:1]))
        kb.ts_("dve", R(beta_s, beta_s.h[0:1, :]), R(beta_s, beta_s.h[0:1, :]),
               1.0 / (D * D), ALU.mult)
        kb.ts_("dve", R(sc1, sc1.h[0:1, :]), R(beta_s, beta_s.h[0:1, :]),
               1e-5, ALU.add, s2=0.5, op1=ALU.mult)
        kb.mm(R(pcol, pcol.h[:, 0:1]), [(ones1q.full, R(sc1, sc1.h[0:1, 0:1]))])
        kb.copy("dve", thr_b.full, R(pcol, pcol.h[:, 0:1]))
        for c in range(NC):
            kb.dma("sp", stage.full, wap[:, c, :])
            kb.act("act", tmpf.full, stage.full, AF.Abs)
            kb.ts_("dve", tmpf.full, tmpf.full, thr_b.full, ALU.subtract)
            kb.act("act", tmpf.full, tmpf.full, AF.Sign)
            kb.ts_("dve", tmpf.full, tmpf.full, 0.0, ALU.max)
            kb.act("act", R(junk, junk.h[:, 0:D]), stage.full, AF.Sign)
            kb.tt_("dve", R(wq_b, wq_b.h[:, c, :]), tmpf.full,
                   R(junk, junk.h[:, 0:D]), ALU.mult)

        kb.act("act", R(sc1, sc1.h[0:1, :]), R(gam_s, gam_s.h[0:1, :]), AF.Ln)
        kb.act("act", R(sc1, sc1.h[0:1, :]), R(sc1, sc1.h[0:1, :]), AF.Exp, scale=-1.0)
        kb.ts_("dve", R(sc1, sc1.h[0:1, :]), R(sc1, sc1.h[0:1, :]), QB, ALU.mult)
        kb.mm(R(pcol, pcol.h[:, 0:1]), [(ones1q.full, R(sc1, sc1.h[0:1, 0:1]))])
        kb.copy("dve", sxb.full, R(pcol, pcol.h[:, 0:1]))
        for c in range(NC):
            kb.ts_("dve", R(xn_fm, xn_fm.h[:, c, :]), R(xn_fm, xn_fm.h[:, c, :]),
                   sxb.full, ALU.mult)
            kb.ts_("dve", R(xn_fm, xn_fm.h[:, c, :]), R(xn_fm, xn_fm.h[:, c, :]),
                   float(-QB + 1e-5), ALU.max)
            kb.ts_("dve", R(xn_fm, xn_fm.h[:, c, :]), R(xn_fm, xn_fm.h[:, c, :]),
                   float(QB - 1e-5), ALU.min)
        kb.tt_("dve", R(sc1, sc1.h[0:1, :]), R(gam_s, gam_s.h[0:1, :]),
               R(beta_s, beta_s.h[0:1, :]), ALU.mult)
        kb.ts_("dve", R(sc1, sc1.h[0:1, :]), R(sc1, sc1.h[0:1, :]), 1.0 / QB, ALU.mult)
        kb.mm(R(pcol, pcol.h[:, 0:1]), [(ones1q.full, R(sc1, sc1.h[0:1, 0:1]))])
        kb.copy("dve", soutb.full, R(pcol, pcol.h[:, 0:1]))

        for tg in range(4):
            cols = slice(512 * tg, 512 * (tg + 1))
            for oc in range(NC):
                kb.mm(pmm.full, [(R(wq_b, wq_b.h[:, c, 128 * oc:128 * (oc + 1)]),
                                  R(xn_fm, xn_fm.h[:, c, cols])) for c in range(NC)])
                kb.ts_("dve", yfm[:, oc, :], pmm.full, soutb.full, ALU.mult)
            epilogue_fm_group(kb, st, x, yfm, pt, tmp2, junk, 4 * tg, 4, ytm)


def declare_io(nc, names_shapes):
    out = {}
    for item in names_shapes:
        name, shape = item[0], item[1]
        dt = item[2] if len(item) > 2 else F32
        out[name] = nc.dram_tensor(name, list(shape), dt, kind="ExternalInput").ap()
    return out


def persistent_state(kb):
    nc = kb.nc
    st = {}
    identb_np = np.eye(128, dtype=BF)
    identf_np = np.eye(128, dtype=np.float32)
    hb = nc.inline_tensor(np.asarray(identb_np), "identb_c")
    hf = nc.inline_tensor(identf_np, "identf_c")
    st["identb"] = kb.sb("identb", [128, 128], BF16, persistent=True)
    st["identf"] = kb.sb("identf", [128, 128], F32, persistent=True)
    kb.dma("sp", st["identb"].full, hb[:])
    kb.dma("sp", st["identf"].full, hf[:])
    st["junk"] = kb.sb("junk", [128, 1024], BF16, persistent=True)
    st["x"] = kb.sb("stream", [128, NT, D], F32, persistent=True)
    return st


def load_stream(kb, st, x_ap):
    kb.dma("sp", st["x"].full, x_ap.rearrange("(i p) d -> p i d", p=128))


def store_stream(kb, st, y_ap):
    kb.fence()
    kb.dma("pool", y_ap.rearrange("(i p) d -> p i d", p=128), st["x"].full)
    kb.drain_all()


def prep_ffn_ternary(inp, li):
    g = lambda k: inp[f"L{li}_ffn_{k}"]
    return {
        "w1T": np.ascontiguousarray(g("w1_w").T),
        "w2T": np.ascontiguousarray(g("w2_w").T),
        "h1T": np.ascontiguousarray(g("h1_w").T),
        "h2T": np.ascontiguousarray(g("h2_w").T),
        "w3T": np.ascontiguousarray(g("w3_w").T),
        "s1r": np.ascontiguousarray(g("w1_s").reshape(32, 128).T),
        "s2r": np.ascontiguousarray(g("w2_s").reshape(32, 128).T),
        "sh1r": np.ascontiguousarray(g("h1_s").reshape(32, 128).T),
        "sh2r": np.ascontiguousarray(g("h2_s").reshape(32, 128).T),
        "s3r": np.ascontiguousarray(g("w3_s").reshape(8, 128).T),
    }


FFN_TERNARY_SHAPES = [("w1T", (D, 4096)), ("w2T", (D, 4096)), ("h1T", (4096, 4096)),
                      ("h2T", (4096, 4096)), ("w3T", (4096, D)),
                      ("s1r", (128, 32)), ("s2r", (128, 32)), ("sh1r", (128, 32)),
                      ("sh2r", (128, 32)), ("s3r", (128, 8))]

MOE_SHAPES = [("up_w", (8, D, 4096)), ("down_w", (8, 2048, D)),
              ("scale_r", (128, 8)), ("ctr", (8, D)), ("ctr_fm", (128, 8, 8))]


def prep_moe(inp):
    g = lambda k: inp[f"L0_ffn_{k}"]
    C = g("centroids")
    return {
        "up_w": np.ascontiguousarray(g("up_w")),
        "down_w": np.ascontiguousarray(g("down_w")),
        "scale_r": np.ascontiguousarray(g("scale").reshape(8, 128).T),
        "ctr": np.ascontiguousarray(C),
        "ctr_fm": np.ascontiguousarray(C.T.reshape(8, 128, 8).transpose(1, 0, 2)),
    }


ATTN_SHAPES_BASE = [("wqkvT", (D, 3 * D)), ("woT", (D, D)), ("mask01", (128, 896), BF16)]
ATTN_SHAPES_L0 = ATTN_SHAPES_BASE + [("bo_r", (128, 8))]
ATTN_SHAPES_L2 = ATTN_SHAPES_BASE + [("cos_fm", (128, S), BF16), ("sin_fm", (128, S), BF16),
                                     ("rt", (128, 64), BF16)]


def host_mask01():
    p = np.arange(128)[:, None]
    m = np.arange(896)[None, :]
    return (m >= 384 + p).astype(BF)


def host_rope_tables():
    inv = 1.0 / (10000.0 ** (np.arange(0, 64, 2, dtype=np.float32) / 64.0))
    t = np.arange(S, dtype=np.float32)
    fr = np.einsum("i,j->ij", t, inv)
    emb = np.concatenate([fr, fr], axis=-1)
    cos = np.cos(emb).T.astype(BF)
    sin = np.sin(emb).T.astype(BF)
    cos = np.concatenate([cos, cos], axis=0)
    sin = np.concatenate([sin, sin], axis=0)
    return np.ascontiguousarray(cos), np.ascontiguousarray(sin)


def host_rt():
    rt = np.zeros((64, 64), dtype=BF)
    for i in range(32):
        rt[i + 32, i] = -1.0
    for i in range(32, 64):
        rt[i - 32, i] = 1.0
    return np.ascontiguousarray(np.concatenate([rt, rt], axis=0))


def prep_attn_l0(inp):
    g = lambda k: inp[f"L0_attn_{k}"]
    return {
        "wqkvT": np.ascontiguousarray(np.concatenate(
            [g("w_q").T, g("w_k").T, g("w_v").T], axis=1)),
        "woT": np.ascontiguousarray(g("w_o").T),
        "bo_r": np.ascontiguousarray(g("b_o").reshape(8, 128).T),
        "mask01": host_mask01(),
    }


ATTN_SHAPES_L1 = [("wqkvT", (D, 3 * D)), ("woT", (D, D)), ("bo_r", (128, 8)),
                  ("maskloc", (128, 2, 128), BF16)]


ATTN_SHAPES_L3 = [("wq", (D, D)), ("wk", (D, D)), ("wv", (D, D)), ("wo", (D, D)),
                  ("bq_r", (128, 8)), ("bk_r", (128, 8)), ("bv_r", (128, 8)),
                  ("bo_r", (128, 8)),
                  ("aq", (1, 1)), ("ak", (1, 1)), ("av", (1, 1))]


def prep_attn_l3(inp):
    g = lambda k: inp[f"L3_attn_{k}"]
    return {
        "wq": np.ascontiguousarray(g("wq")), "wk": np.ascontiguousarray(g("wk")),
        "wv": np.ascontiguousarray(g("wv")), "wo": np.ascontiguousarray(g("wo")),
        "bq_r": np.ascontiguousarray(g("bq").reshape(8, 128).T),
        "bk_r": np.ascontiguousarray(g("bk").reshape(8, 128).T),
        "bv_r": np.ascontiguousarray(g("bv").reshape(8, 128).T),
        "bo_r": np.ascontiguousarray(g("bo").reshape(8, 128).T),
        "aq": g("aq").reshape(1, 1).astype(np.float32),
        "ak": g("ak").reshape(1, 1).astype(np.float32),
        "av": g("av").reshape(1, 1).astype(np.float32),
    }


FFN_QUANT_SHAPES = [("weight", (D, D)), ("g_r", (128, 8)), ("b_r", (128, 8))]


def prep_ffn_quant(inp):
    g = lambda k: inp[f"L3_ffn_{k}"]
    return {
        "weight": np.ascontiguousarray(g("weight")),
        "g_r": np.ascontiguousarray(g("ln_gamma").reshape(8, 128).T),
        "b_r": np.ascontiguousarray(g("ln_beta").reshape(8, 128).T),
    }


def prep_attn_l1(inp):
    g = lambda k: inp[f"L1_attn_{k}"]
    p = np.arange(128)[:, None]
    c = np.arange(128)[None, :]
    maskloc = np.stack([(c <= p), (c >= p)], axis=1).astype(BF)
    return {
        "wqkvT": np.ascontiguousarray(g("w_qkv").T),
        "woT": np.ascontiguousarray(g("w_out").T),
        "bo_r": np.ascontiguousarray(g("b_out").reshape(8, 128).T),
        "maskloc": maskloc,
    }


def prep_attn_l2(inp):
    g = lambda k: inp[f"L2_attn_{k}"]
    cos, sin = host_rope_tables()
    return {
        "wqkvT": np.ascontiguousarray(g("w_qkv").T),
        "woT": np.ascontiguousarray(g("w_out").T),
        "mask01": host_mask01(),
        "cos_fm": cos, "sin_fm": sin, "rt": host_rt(),
    }


def build_block_program(block):
    nc = bass.Bass("TRN2", target_bir_lowering=False)
    x_ap = nc.dram_tensor("x", [S, D], F32, kind="ExternalInput").ap()
    y_ap = nc.dram_tensor("y", [S, D], F32, kind="ExternalOutput").ap()
    kb = KB(nc)
    st = persistent_state(kb)
    load_stream(kb, st, x_ap)
    if block == "ffn_ternary":
        P = declare_io(nc, FFN_TERNARY_SHAPES)
        emit_ffn_ternary(kb, st, P)
    elif block == "attn_l0":
        P = declare_io(nc, ATTN_SHAPES_L0)
        emit_attn_causal(kb, st, P, rope=False, bias=True)
    elif block == "attn_l2":
        P = declare_io(nc, ATTN_SHAPES_L2)
        emit_attn_causal(kb, st, P, rope=True, bias=False)
    elif block == "moe":
        P = declare_io(nc, MOE_SHAPES)
        emit_moe(kb, st, P)
    elif block == "attn_l1":
        P = declare_io(nc, ATTN_SHAPES_L1)
        emit_attn_l1(kb, st, P)
    elif block == "attn_l3":
        P = declare_io(nc, ATTN_SHAPES_L3)
        emit_attn_yat(kb, st, P)
    elif block == "ffn_quant":
        P = declare_io(nc, FFN_QUANT_SHAPES)
        emit_ffn_quant(kb, st, P)
    else:
        raise ValueError(block)
    store_stream(kb, st, y_ap)
    return nc


FUSED_CHAIN = [
    ("b0", ATTN_SHAPES_L0, lambda kb, st, P: emit_attn_causal(kb, st, P, rope=False, bias=True),
     lambda inp: prep_attn_l0(inp)),
    ("b1", MOE_SHAPES, lambda kb, st, P: emit_moe(kb, st, P),
     lambda inp: prep_moe(inp)),
    ("b2", ATTN_SHAPES_L1, lambda kb, st, P: emit_attn_l1(kb, st, P),
     lambda inp: prep_attn_l1(inp)),
    ("b3", FFN_TERNARY_SHAPES, lambda kb, st, P: emit_ffn_ternary(kb, st, P),
     lambda inp: prep_ffn_ternary(inp, 1)),
    ("b4", ATTN_SHAPES_L2, lambda kb, st, P: emit_attn_causal(kb, st, P, rope=True, bias=False),
     lambda inp: prep_attn_l2(inp)),
    ("b5", FFN_TERNARY_SHAPES, lambda kb, st, P: emit_ffn_ternary(kb, st, P),
     lambda inp: prep_ffn_ternary(inp, 2)),
    ("b6", ATTN_SHAPES_L3, lambda kb, st, P: emit_attn_yat(kb, st, P),
     lambda inp: prep_attn_l3(inp)),
    ("b7", FFN_QUANT_SHAPES, lambda kb, st, P: emit_ffn_quant(kb, st, P),
     lambda inp: prep_ffn_quant(inp)),
]


def build_fused_program():
    nc = bass.Bass("TRN2", target_bir_lowering=False)
    x_ap = nc.dram_tensor("x", [S, D], F32, kind="ExternalInput").ap()
    y_ap = nc.dram_tensor("y", [S, D], F32, kind="ExternalOutput").ap()
    kb = KB(nc)
    st = persistent_state(kb)
    load_stream(kb, st, x_ap)
    for prefix, shapes, emitfn, _ in FUSED_CHAIN:
        pre = [(f"{prefix}_{it[0]}",) + tuple(it[1:]) for it in shapes]
        Praw = declare_io(nc, pre)
        P = {k[len(prefix) + 1:]: v for k, v in Praw.items()}
        emitfn(kb, st, P)
    store_stream(kb, st, y_ap)
    return nc


def prep_all(inputs):
    d = {}
    for prefix, _, _, prep in FUSED_CHAIN:
        for k, v in prep(inputs).items():
            d[f"{prefix}_{k}"] = v
    return d


_CHAIN = [
    ("attn_l0", lambda inp: prep_attn_l0(inp)),
    ("moe", lambda inp: prep_moe(inp)),
    ("attn_l1", lambda inp: prep_attn_l1(inp)),
    ("ffn_ternary", lambda inp: prep_ffn_ternary(inp, 1)),
    ("attn_l2", lambda inp: prep_attn_l2(inp)),
    ("ffn_ternary", lambda inp: prep_ffn_ternary(inp, 2)),
    ("attn_l3", lambda inp: prep_attn_l3(inp)),
    ("ffn_quant", lambda inp: prep_ffn_quant(inp)),
]

_prog_cache = {}


def _get_prog(block):
    if block not in _prog_cache:
        _prog_cache[block] = build_block_program(block)
    return _prog_cache[block]


def kernel(**inputs):
    x = inputs["L0_attn_x"]
    shards = [np.ascontiguousarray(x[c]) for c in range(NCORES)]
    if os.environ.get("MK_UNFUSED", "0") == "1":
        for block, prep in _CHAIN:
            nc = _get_prog(block)
            P = prep(inputs)
            in_maps = [{"x": shards[c], **P} for c in range(NCORES)]
            res = run_bass_kernel_spmd(nc, in_maps, core_ids=list(range(NCORES)))
            shards = [np.asarray(r["y"], dtype=np.float32) for r in res.results]
        return np.stack(shards, axis=0)
    if "fused" not in _prog_cache:
        _prog_cache["fused"] = build_fused_program()
    nc = _prog_cache["fused"]
    P = prep_all(inputs)
    in_maps = [{"x": shards[c], **P} for c in range(NCORES)]
    res = run_bass_kernel_spmd(nc, in_maps, core_ids=list(range(NCORES)))
    return np.stack([np.asarray(r["y"], dtype=np.float32) for r in res.results], axis=0)
